# version 23
# baseline (speedup 1.0000x reference)
import contextlib
import numpy as np
import concourse.bass as bass
import concourse.mybir as mybir
from concourse.bass_utils import run_bass_kernel_spmd

F32 = mybir.dt.float32
BF16 = mybir.dt.bfloat16
AF = mybir.ActivationFunctionType
ALU = mybir.AluOpType
AX = mybir.AxisListType

D = 1024
L = 4096
CH = 512
NCH = L // CH
DT = D // 128
IFF = 2816
ITT = IFF // 128
H, DKQ, DVV = 4, 256, 256
CK = 128
NCK = L // CK
EXP = 8
HM = 4096
HMT = HM // 128

DBG_RMS = False
QTHR = (0.75, 1.25, 1.75, 2.5, 3.5)
CCVALS = (1e-5, 1e-6, -4.0, -6.0)


class Sem:
    def __init__(self, es, nc, name):
        self.h = es.enter_context(nc.semaphore(name))
        self.n = 0

    def inc(self, instr, k=1):
        instr.then_inc(self.h, k)
        self.n += k
        return instr


class Prog:
    def __init__(self, ext_in, ext_out):
        self.nc = bass.Bass("TRN2", target_bir_lowering=False, debug=False)
        self.es = contextlib.ExitStack()
        self.ext_in = set(ext_in)
        self.ext_out = set(ext_out)
        self.dram = {}

    def d(self, name, shape, dtype=F32):
        if name not in self.dram:
            kind = "Internal"
            if name in self.ext_in:
                kind = "ExternalInput"
            elif name in self.ext_out:
                kind = "ExternalOutput"
            self.dram[name] = self.nc.dram_tensor(name, list(shape), dtype, kind=kind).ap()
        return self.dram[name]

    def sb(self, es, name, shape, dtype=F32):
        return es.enter_context(self.nc.sbuf_tensor(name, list(shape), dtype)).ap()

    def ps(self, es, name, shape, dtype=F32):
        return es.enter_context(self.nc.psum_tensor(name, list(shape), dtype)).ap()

    def open(self):
        self.es.__enter__()
        self.sld = Sem(self.es, self.nc, "sld")
        self.sgp = Sem(self.es, self.nc, "sgp")
        self.smm = Sem(self.es, self.nc, "smm")
        self.sev = Sem(self.es, self.nc, "sev")
        self.ssc = Sem(self.es, self.nc, "ssc")
        self.pb = self.sld
        g = self.es
        self.pbk = [self.ps(g, f"pbk{i}", [128, 512], F32) for i in range(8)]
        self.pbt = self.pbk[7].bitcast(BF16)
        self.ones_b = self.sb(g, "ones_b", [128, 128], BF16)
        self.ones_f1 = self.sb(g, "ones_f1", [1, 128], F32)
        self.ones_b1 = self.sb(g, "ones_b1", [1, 128], BF16)
        self.ident_b = self.sb(g, "ident_b", [128, 128], BF16)
        self.ident_f = self.sb(g, "ident_f", [128, 128], F32)
        self.mask_lo = self.sb(g, "mask_lo", [128, 128], F32)
        self.mask_ups = self.sb(g, "mask_ups", [128, 128], F32)
        self.mask_upi = self.sb(g, "mask_upi", [128, 128], F32)
        self.ccol_t = self.sb(g, "ccol", [128, len(CCVALS)], F32)
        self.scope = None
        self.block = self.es.enter_context(self.nc.Block())
        sems = [self.sld, self.sgp, self.smm, self.sev, self.ssc]

        def gp0(g2):
            for s in sems:
                g2.sem_clear(s.h)

        def wait0(eng):
            for s in sems:
                eng.wait_op(s.h, 0, "sem-eq")

        self.seq("sem_init", [("gpsimd", gp0), ("sync", wait0), ("vector", wait0),
                              ("scalar", wait0), ("tensor", wait0)])

    def close(self):
        self.es.__exit__(None, None, None)

    def _emit(self, name, steps):
        block = self.block
        reg = {"sync": block.sync, "gpsimd": block.gpsimd, "vector": block.vector,
               "scalar": block.scalar, "tensor": block.tensor}
        tag = name.rstrip("0123456789").rstrip("_")
        tag = tag.split("_")[0] if tag else name
        for ename, f in steps:
            def body(eng, f=f):
                with self.nc.named_scope(self.scope or tag):
                    f(eng)
            reg[ename](body)

    def seq(self, name, steps):
        self._emit(name, steps)
        drip = getattr(self, "_drip", None)
        if drip is not None and not getattr(self, "_dripping", False):
            self._dripping = True
            save = self.scope
            self.scope = getattr(self, "_drip_scope", None)
            try:
                for _ in range(getattr(self, "_drip_k", 1)):
                    try:
                        nm, st = next(drip)
                    except StopIteration:
                        self._drip = None
                        break
                    self._emit(nm, st)
            finally:
                self.scope = save
                self._dripping = False

    def drip_flush(self):
        drip = getattr(self, "_drip", None)
        if drip is None:
            return
        self._dripping = True
        save = self.scope
        self.scope = getattr(self, "_drip_scope", None)
        try:
            for nm, st in drip:
                self._emit(nm, st)
        finally:
            self.scope = save
            self._dripping = False
            self._drip = None

    def newsem(self, name):
        return Sem(self.es, self.nc, name)

    def cc(self, val):
        i = CCVALS.index(val)
        return self.ccol_t[:, i:i + 1]

    def w(self, eng, *sems):
        for s in (self.sld, self.sgp, self.smm, self.sev, self.ssc):
            if s.n:
                eng.wait_ge(s.h, s.n)


def wpack_hbm(whbm):
    return whbm.rearrange("(k p) (m q) -> p k m q", p=128, q=128)


def wv(wsb, k, m, nm):
    c = (k * nm + m) * 128
    return wsb[:, c:c + 128]


def cview(hbm2d, c, w=CH):
    return hbm2d[:, c * w:(c + 1) * w].rearrange("(k p) n -> p k n", p=128)


def sview(sb2d, w=CH):
    return sb2d.rearrange("p (k n) -> p k n", n=w)


def ct(buf, k, w=CH):
    return buf[:, k * w:(k + 1) * w]



def fin(P, tag):
    def syf(sync):
        P.w(sync, P.sld)
        return sync.nop(), 1
    P.seq(f"{tag}_fin", [("sync", syf)])


def ph_const(P):
    def gp(g):
        g.memset(P.ones_b, 1.0)
        g.memset(P.ones_f1, 1.0)
        g.memset(P.ones_b1, 1.0)
        for t in (P.ident_b, P.ident_f):
            g.memset(t, 0.0)
            g.affine_select(out=t, in_=t, compare_op=ALU.not_equal, fill=1.0,
                            base=0, pattern=[[-1, 128]], channel_multiplier=1)
        g.memset(P.mask_lo, 1.0)
        g.affine_select(out=P.mask_lo, in_=P.mask_lo, compare_op=ALU.is_gt,
                        fill=0.0, base=0, pattern=[[-1, 128]], channel_multiplier=1)
        g.memset(P.mask_upi, 1.0)
        g.affine_select(out=P.mask_upi, in_=P.mask_upi, compare_op=ALU.is_ge,
                        fill=0.0, base=0, pattern=[[1, 128]], channel_multiplier=-1)
        g.memset(P.mask_ups, 1.0)
        g.affine_select(out=P.mask_ups, in_=P.mask_ups, compare_op=ALU.is_gt,
                        fill=0.0, base=0, pattern=[[1, 128]], channel_multiplier=-1)
        for i, v in enumerate(CCVALS):
            last = g.memset(P.ccol_t[:, i:i + 1], v)
        return P.sgp.inc(last, 1), 1
    P.seq("const", [("gpsimd", gp)])


def colvec(P, es, name, n, dtype=F32):
    nt = (n + 127) // 128
    pp = 128 if n >= 128 else n
    hv = P.d(name, [n], F32)
    out = P.sb(es, f"cv_{name}", [pp, nt], dtype)

    def sy(sync):
        P.w(sync)
        src_ = hv.rearrange("(p t) -> p t", t=nt)
        return Sem.inc(P.sld, sync.dma_start(out=out, in_=src_), 16), 16

    P.seq(f"cv_{name}", [("sync", sy)])
    return out


def host_colvec(vec):
    v = np.asarray(vec, np.float32)
    n = v.shape[0]
    if n < 128:
        return np.ascontiguousarray(v)
    return np.ascontiguousarray(v.reshape(n // 128, 128).T.reshape(-1))


def load_wpack(P, es, name, hname, nk, nm):
    wsb = P.sb(es, f"wp_{name}", [128, nk * nm * 128], BF16)
    hv = P.d(hname, [nk * 128, nm * 128], F32)

    def gp(g):
        P.w(g)
        dst = wsb.rearrange("p (k m q) -> p k m q", m=nm, q=128)
        return P.sgp.inc(g.dma_start(out=dst, in_=wpack_hbm(hv)), 16), 16
    P.seq(f"wp_{name}", [("gpsimd", gp)])
    return wsb


def ph_ln(P, src, dst_f, dst_b, tag):
    hsrc = P.d(src, [D, L])
    hf = P.d(dst_f, [D, L])
    hb = P.d(dst_b, [D, L], BF16)
    with contextlib.ExitStack() as es:
        xf = P.sb(es, f"{tag}_xf", [128, DT * CH], F32)
        xb = P.sb(es, f"{tag}_xb", [128, DT * CH], BF16)
        sq = P.sb(es, f"{tag}_sq", [128, DT * CH], BF16)
        st = P.sb(es, f"{tag}_st", [128, 3 * CH], F32)
        outf = P.sb(es, f"{tag}_of", [128, DT * CH], F32)
        outb = P.sb(es, f"{tag}_ob", [128, DT * CH], BF16)
        m_t, var_t, inv_t = st[:, 0:CH], st[:, CH:2 * CH], st[:, 2 * CH:3 * CH]

        for c in range(NCH):
            def sy(sync, c=c):
                P.w(sync, P.sev, P.ssc)
                return P.sld.inc(sync.dma_start(out=sview(xf), in_=cview(hsrc, c)), 16), 16

            def gp(g, c=c):
                P.w(g, P.ssc, P.smm)
                return P.sgp.inc(g.dma_start(out=sview(xb), in_=cview(hsrc, c)), 16), 16

            def sc1(v, c=c):
                P.w(v)
                return P.sev.inc(v.tensor_mul(sview(sq), sview(xf), sview(xf)), 1), 1

            def tn(t, c=c):
                P.w(t, P.ssc, P.sev, P.sgp)
                for k in range(DT):
                    t.matmul(P.pbk[0], P.ones_b, ct(xb, k), start=(k == 0), stop=(k == DT - 1))
                for k in range(DT):
                    i2 = t.matmul(P.pbk[1], P.ones_b, ct(sq, k), start=(k == 0), stop=(k == DT - 1))
                return P.smm.inc(i2, 1), 1

            def ve1(v, c=c):
                P.w(v, P.smm)
                v.tensor_scalar_mul(out=m_t, in0=P.pbk[0], scalar1=1.0 / D)
                v.tensor_scalar_mul(out=var_t, in0=P.pbk[1], scalar1=1.0 / D)
                v.scalar_tensor_tensor(out=inv_t, in0=m_t, scalar=-1.0, in1=m_t,
                                       op0=ALU.mult, op1=ALU.mult)
                return P.sev.inc(v.tensor_add(var_t, var_t, inv_t), 1), 1

            def sc2(s, c=c):
                P.w(s, P.sev)
                return P.ssc.inc(s.activation(out=inv_t, in_=var_t, func=AF.Sqrt, bias=P.cc(1e-5)), 1), 1

            def ve2(v, c=c):
                P.w(v, P.ssc)
                v.reciprocal(out=inv_t, in_=inv_t)
                for k in range(DT):
                    v.tensor_sub(ct(outf, k), ct(xf, k), m_t)
                    i2 = v.tensor_mul(ct(outf, k), ct(outf, k), inv_t)
                return P.sev.inc(i2, 1), 1

            def sc3(v, c=c):
                P.w(v)
                i2 = v.tensor_copy(out=outb, in_=outf)
                return P.sev.inc(i2, 1), 1

            def sy2(sync, c=c):
                P.w(sync, P.sev, P.ssc)
                P.sld.inc(sync.dma_start(out=cview(hf, c), in_=sview(outf)), 16)
                return P.sld.inc(sync.dma_start(out=cview(hb, c), in_=sview(outb)), 16), 16

            P.seq(f"{tag}_c{c}", [("sync", sy), ("gpsimd", gp), ("vector", sc1),
                                  ("tensor", tn), ("vector", ve1), ("scalar", sc2),
                                  ("vector", ve2), ("vector", sc3), ("sync", sy2)])

        fin(P, tag)


def ph_rms(P, src_f, wcol, dst_b, tag, eps):
    hsf = P.d(src_f, [D, L])
    hd = P.d(dst_b, [D, L], BF16)
    with contextlib.ExitStack() as es:
        xf = P.sb(es, f"{tag}_xf", [128, DT * CH], F32)
        sq = P.sb(es, f"{tag}_sq", [128, DT * CH], BF16)
        st = P.sb(es, f"{tag}_st", [128, 2 * CH], F32)
        outf = P.sb(es, f"{tag}_of", [128, DT * CH], F32)
        outb = P.sb(es, f"{tag}_ob", [128, DT * CH], BF16)
        inv_t, tmp_t = st[:, 0:CH], st[:, CH:2 * CH]

        for c in range(NCH):
            def sy(sync, c=c):
                P.w(sync, P.sev, P.ssc)
                return P.sld.inc(sync.dma_start(out=sview(xf), in_=cview(hsf, c)), 16), 16

            def sc1(v, c=c):
                P.w(v)
                return P.sev.inc(v.tensor_mul(sview(sq), sview(xf), sview(xf)), 1), 1

            def tn(t, c=c):
                P.w(t, P.ssc, P.sev)
                for k in range(DT):
                    i2 = t.matmul(P.pbk[0], P.ones_b, ct(sq, k), start=(k == 0), stop=(k == DT - 1))
                return P.smm.inc(i2, 1), 1

            def ve1(v, c=c):
                P.w(v, P.smm)
                return P.sev.inc(v.tensor_scalar_mul(out=tmp_t, in0=P.pbk[0], scalar1=1.0 / D), 1), 1

            def sc2(s, c=c):
                P.w(s, P.sev)
                return P.ssc.inc(s.activation(out=inv_t, in_=tmp_t, func=AF.Sqrt, bias=P.cc(eps)), 1), 1

            def ve2(v, c=c):
                P.w(v, P.ssc)
                v.reciprocal(out=inv_t, in_=inv_t)
                for k in range(DT):
                    i2 = v.tensor_mul(ct(outf, k), ct(xf, k), inv_t)
                return P.sev.inc(i2, 1), 1

            def sc3(v, c=c):
                P.w(v)
                for k in range(DT):
                    i2 = v.tensor_scalar(out=ct(outb, k), in0=ct(outf, k),
                                         scalar1=wcol[:, k:k + 1], scalar2=None,
                                         op0=ALU.mult)
                return P.sev.inc(i2, 1), 1

            def sy2(sync, c=c):
                P.w(sync, P.ssc)
                i2 = P.sld.inc(sync.dma_start(out=cview(hd, c), in_=sview(outb)), 16)
                if DBG_RMS and c == 0:
                    hdbg = P.d(f"dbg_{tag}", [128, 2 * CH])
                    i2 = P.sld.inc(sync.dma_start(out=hdbg, in_=st), 16)
                return i2, 16

            P.seq(f"{tag}_c{c}", [("sync", sy), ("vector", sc1), ("tensor", tn),
                                  ("vector", ve1), ("scalar", sc2), ("vector", ve2),
                                  ("vector", sc3), ("sync", sy2)])

        fin(P, tag)


def ph_l0_projscan(P):
    hnin = P.d("nin_b", [D, L], BF16)
    hhx = P.d("hx_f", [D, L])
    with contextlib.ExitStack() as es:
        wg = load_wpack(P, es, "wg", "wgT", DT, DT)
        wc = load_wpack(P, es, "wc", "wcT", DT, DT)
        mov = P.sb(es, "psn_mov", [128, DT * CH], BF16)
        beta = P.sb(es, "psn_beta", [128, DT * CH], F32)
        af = P.sb(es, "psn_a", [128, DT * CH], F32)
        xs = P.sb(es, "psn_xs", [128, DT * CH], F32)
        hx = P.sb(es, "psn_hx", [128, DT * CH], F32)
        carry = P.sb(es, "psn_carry", [128, DT], F32)

        def gp0(g):
            P.w(g)
            return P.sgp.inc(g.memset(carry, 0.0), 1), 1
        P.seq("scan_init", [("gpsimd", gp0)])

        for c in range(NCH):
            steps = []

            def sy(sync, c=c):
                P.w(sync, P.smm)
                return P.sld.inc(sync.dma_start(out=sview(mov), in_=cview(hnin, c)), 16), 16
            steps.append(("sync", sy))

            for half in range(2):
                def tn_b(t, c=c, half=half):
                    P.w(t, P.sld, P.sgp, P.ssc, P.sev)
                    for j in range(4):
                        m = half * 4 + j
                        for k in range(DT):
                            i2 = t.matmul(P.pbk[j], wv(wg, k, m, DT), ct(mov, k),
                                          start=(k == 0), stop=(k == DT - 1))
                    return P.smm.inc(i2, 1), 1

                def sc_b(s, c=c, half=half):
                    P.w(s, P.smm)
                    for j in range(4):
                        m = half * 4 + j
                        i2 = s.activation(out=ct(beta, m), in_=P.pbk[j], func=AF.Sigmoid)
                    return P.ssc.inc(i2, 1), 1
                steps += [("tensor", tn_b), ("scalar", sc_b)]

            for half in range(2):
                def tn_h(t, c=c, half=half):
                    P.w(t, P.ssc, P.sev)
                    for j in range(4):
                        m = half * 4 + j
                        for k in range(DT):
                            i2 = t.matmul(P.pbk[j], wv(wc, k, m, DT), ct(mov, k),
                                          start=(k == 0), stop=(k == DT - 1))
                    return P.smm.inc(i2, 1), 1

                def ve_h(v, c=c, half=half):
                    P.w(v, P.smm)
                    for j in range(4):
                        m = half * 4 + j
                        i2 = v.tensor_mul(ct(xs, m), ct(beta, m), P.pbk[j])
                    return P.sev.inc(i2, 1), 1
                steps += [("tensor", tn_h), ("vector", ve_h)]

            def ve_s(v, c=c):
                P.w(v)
                for m in range(DT):
                    v.tensor_scalar(out=ct(af, m), in0=ct(beta, m), scalar1=-1.0,
                                    scalar2=1.0, op0=ALU.mult, op1=ALU.add)
                for m in range(DT):
                    i2 = v.tensor_tensor_scan(out=ct(hx, m), data0=ct(af, m),
                                              data1=ct(xs, m), initial=carry[:, m:m + 1],
                                              op0=ALU.mult, op1=ALU.add)
                return P.sev.inc(i2, 1), 1
            steps.append(("vector", ve_s))

            def sc_carry(s, c=c):
                P.w(s)
                lastcols = hx.rearrange("p (m n) -> p m n", n=CH)[:, :, CH - 1:CH]
                i2 = s.activation(out=carry.rearrange("p (m o) -> p m o", o=1),
                                  in_=lastcols, func=AF.Copy)
                return P.ssc.inc(i2, 1), 1
            steps.append(("scalar", sc_carry))

            def sy2(sync, c=c):
                P.w(sync, P.sev)
                i2 = P.sld.inc(sync.dma_start(out=cview(hhx, c), in_=sview(hx)), 16)
                if DBG_RMS:
                    hdc = P.d("dbg_carry", [NCH, 128, DT])
                    i2 = P.sld.inc(sync.dma_start(out=hdc[c], in_=carry), 16)
                return i2, 16
            steps.append(("sync", sy2))
            P.seq(f"pscan_c{c}", steps)

        fin(P, "pscan")


def ph_l0_mlp1(P):
    hhn = P.d("hn_b", [D, L], BF16)
    hout = P.d("ffh_b", [D, L], BF16)
    with contextlib.ExitStack() as es:
        w1 = load_wpack(P, es, "w1", "w1T", DT, DT)
        w3 = load_wpack(P, es, "w3", "w3T", DT, DT)
        mov = P.sb(es, "m1_mov", [128, DT * CH], BF16)
        u1 = P.sb(es, "m1_u1", [128, DT * CH], F32)
        outb = P.sb(es, "m1_ob", [128, DT * CH], BF16)

        for c in range(NCH):
            steps = []

            def sy(sync, c=c):
                P.w(sync, P.smm)
                return P.sld.inc(sync.dma_start(out=sview(mov), in_=cview(hhn, c)), 16), 16
            steps.append(("sync", sy))

            for half in range(2):
                def tn1(t, c=c, half=half):
                    P.w(t, P.sld, P.sgp, P.ssc, P.sev)
                    for j in range(4):
                        m = half * 4 + j
                        for k in range(DT):
                            i2 = t.matmul(P.pbk[j], wv(w1, k, m, DT), ct(mov, k),
                                          start=(k == 0), stop=(k == DT - 1))
                    return P.smm.inc(i2, 1), 1

                def sc1(s, c=c, half=half):
                    P.w(s, P.smm)
                    for j in range(4):
                        m = half * 4 + j
                        i2 = s.activation(out=ct(u1, m), in_=P.pbk[j], func=AF.Silu)
                    return P.ssc.inc(i2, 1), 1
                steps += [("tensor", tn1), ("scalar", sc1)]

            for half in range(2):
                def tn3(t, c=c, half=half):
                    P.w(t, P.ssc, P.sev)
                    for j in range(4):
                        m = half * 4 + j
                        for k in range(DT):
                            i2 = t.matmul(P.pbk[j], wv(w3, k, m, DT), ct(mov, k),
                                          start=(k == 0), stop=(k == DT - 1))
                    return P.smm.inc(i2, 1), 1

                def ve3(v, c=c, half=half):
                    P.w(v, P.smm)
                    for j in range(4):
                        m = half * 4 + j
                        i2 = v.tensor_mul(ct(outb, m), ct(u1, m), P.pbk[j])
                    return P.sev.inc(i2, 1), 1
                steps += [("tensor", tn3), ("vector", ve3)]

            def sy2(sync, c=c):
                P.w(sync, P.sev)
                return P.sld.inc(sync.dma_start(out=cview(hout, c), in_=sview(outb)), 16), 16
            steps.append(("sync", sy2))
            P.seq(f"mlp1_c{c}", steps)

        fin(P, "mlp1")


def ph_l0_mlp2(P):
    hff = P.d("ffh_b", [D, L], BF16)
    hhx = P.d("hx_f", [D, L])
    hs0 = P.d("s0_f", [D, L])
    with contextlib.ExitStack() as es:
        w2 = load_wpack(P, es, "w2", "w2T", DT, DT)
        mov = P.sb(es, "m2_mov", [128, DT * CH], BF16)
        hxs = P.sb(es, "m2_hx", [128, DT * CH], F32)
        outf = P.sb(es, "m2_of", [128, DT * CH], F32)

        for c in range(NCH):
            steps = []

            def sy(sync, c=c):
                P.w(sync, P.smm, P.sev)
                P.sld.inc(sync.dma_start(out=sview(mov), in_=cview(hff, c)), 16)
                return P.sld.inc(sync.dma_start(out=sview(hxs), in_=cview(hhx, c)), 16), 16
            steps.append(("sync", sy))

            for half in range(2):
                def tn(t, c=c, half=half):
                    P.w(t, P.sld, P.sgp, P.sev)
                    for j in range(4):
                        m = half * 4 + j
                        for k in range(DT):
                            i2 = t.matmul(P.pbk[j], wv(w2, k, m, DT), ct(mov, k),
                                          start=(k == 0), stop=(k == DT - 1))
                    return P.smm.inc(i2, 1), 1

                def ve(v, c=c, half=half):
                    P.w(v, P.smm)
                    for j in range(4):
                        m = half * 4 + j
                        i2 = v.tensor_add(ct(outf, m), ct(hxs, m), P.pbk[j])
                    return P.sev.inc(i2, 1), 1
                steps += [("tensor", tn), ("vector", ve)]

            def sy2(sync, c=c):
                P.w(sync, P.sev)
                return P.sld.inc(sync.dma_start(out=cview(hs0, c), in_=sview(outf)), 16), 16
            steps.append(("sync", sy2))
            P.seq(f"mlp2_c{c}", steps)

        fin(P, "mlp2")



def emit_quant_steps2(P, steps, xin, xout, tmps, nblk, pp=128):
    W = nblk * 16
    xa, sg, vm, cnt, r1, scl, rs = (tmps[k] for k in
                                    ("xa", "sg", "vm", "cnt", "r1", "scl", "rs"))

    def sA(v):
        P.w(v)
        v.tensor_scalar_mul(out=xa[:, 0:W], in0=xin, scalar1=-1.0)
        v.tensor_max(xa[:, 0:W], xa[:, 0:W], xin)
        return P.sev.inc(v.tensor_scalar(out=sg[:, 0:W], in0=xin, scalar1=0.0,
                                         scalar2=None, op0=ALU.is_gt), 1), 1

    def vA(v):
        P.w(v, P.ssc)
        xav = xa[:, 0:W].rearrange("p (b s) -> p b s", s=16)
        v.tensor_reduce(out=scl[:, 0:nblk], in_=xav, axis=AX.X, op=ALU.max,
                        apply_absolute_value=True)
        v.tensor_scalar(out=scl[:, 0:nblk], in0=scl[:, 0:nblk], scalar1=1.0 / 6.0,
                        scalar2=1e-12, op0=ALU.mult, op1=ALU.max)
        v.reciprocal(out=rs[:, 0:nblk], in_=scl[:, 0:nblk])
        rsb = rs[:, 0:nblk].broadcast_to([pp, nblk, 16])
        vmv = vm[:, 0:W].rearrange("p (b s) -> p b s", s=16)
        v.tensor_tensor(out=vmv, in0=xav, in1=rsb, op=ALU.mult)
        v.tensor_scalar(out=cnt[:, 0:W], in0=vm[:, 0:W], scalar1=0.25, scalar2=None,
                        op0=ALU.is_gt)
        for t in QTHR:
            v.scalar_tensor_tensor(out=cnt[:, 0:W], in0=vm[:, 0:W], scalar=t,
                                   in1=cnt[:, 0:W], op0=ALU.is_gt, op1=ALU.add)
        i2 = v.scalar_tensor_tensor(out=cnt[:, 0:W], in0=vm[:, 0:W], scalar=5.0,
                                    in1=cnt[:, 0:W], op0=ALU.is_gt, op1=ALU.add)
        return P.sev.inc(i2, 1), 1

    def sB(s):
        P.w(s, P.sev)
        s.activation(out=r1[:, 0:W], in_=cnt[:, 0:W], func=AF.Relu, bias=P.cc(-4.0))
        return P.ssc.inc(s.activation(out=vm[:, 0:W], in_=cnt[:, 0:W], func=AF.Relu, bias=P.cc(-6.0)), 1), 1

    def vB(v):
        P.w(v, P.ssc)
        v.tensor_add(cnt[:, 0:W], cnt[:, 0:W], r1[:, 0:W])
        v.scalar_tensor_tensor(out=cnt[:, 0:W], in0=cnt[:, 0:W], scalar=0.5,
                               in1=vm[:, 0:W], op0=ALU.mult, op1=ALU.add)
        hsb = scl[:, 0:nblk].broadcast_to([pp, nblk, 16])
        cntv = cnt[:, 0:W].rearrange("p (b s) -> p b s", s=16)
        v.tensor_tensor(out=cntv, in0=cntv, in1=hsb, op=ALU.mult)
        v.tensor_mul(r1[:, 0:W], cnt[:, 0:W], sg[:, 0:W])
        return P.sev.inc(v.scalar_tensor_tensor(out=xout, in0=r1[:, 0:W], scalar=2.0,
                                                in1=cnt[:, 0:W], op0=ALU.mult,
                                                op1=ALU.subtract), 1), 1

    steps += [("vector", sA), ("vector", vA), ("scalar", sB), ("vector", vB)]


def mk_qtmp(P, es, tag, W, pp=128):
    t = {}
    for nm in ("xa", "sg", "vm", "cnt", "r1"):
        t[nm] = P.sb(es, f"{tag}_{nm}", [pp, W], F32)
    for nm in ("scl", "rs"):
        t[nm] = P.sb(es, f"{tag}_{nm}", [pp, W // 16], F32)
    return t


def gen_fp4_weights(P):
    specs = [("wgate", IFF, D, "qgT_b"), ("wup", IFF, D, "quT_b"), ("wdown", D, IFF, "qdT_b")]
    with contextlib.ExitStack() as es0:
        wq = P.sb(es0, "q_w", [128, 22 * 1024], BF16)
        wfull = P.sb(es0, "f_w", [128, IFF], F32)
        tmps = mk_qtmp(P, es0, "qt_w", IFF // 2)
        rowbuf = P.sb(es0, "rb_w", [128, IFF], BF16)
        for hname, R, Cc, oname in specs:
            nr, ncc = R // 128, Cc // 128
            hsrc = P.d(hname, [R, Cc], F32)
            hdst = P.d(oname, [Cc, R], BF16)
            wf = wfull[:, 0:Cc]

            for i in range(nr):
                steps = []

                def sy(sync, i=i, hsrc=hsrc, wf=wf):
                    P.w(sync, P.sev)
                    src = hsrc[i * 128:(i + 1) * 128, :].rearrange("p n -> p n")
                    return P.sld.inc(sync.dma_start(out=wf, in_=src), 16), 16
                steps.append(("sync", sy))
                nh = 2 if Cc > IFF // 2 else 1
                hw = Cc // nh
                for h in range(nh):
                    emit_quant_steps2(P, steps, wf[:, h * hw:(h + 1) * hw],
                                      wq[:, i * Cc + h * hw: i * Cc + (h + 1) * hw],
                                      tmps, hw // 16)
                yield f"q_{hname}_{i}", steps

            for m in range(ncc):
                for w0 in range(0, nr, 8):
                    wn = min(8, nr - w0)

                    def tn(t, m=m, w0=w0, wn=wn):
                        P.w(t, P.sev)
                        for j in range(wn):
                            i = w0 + j
                            i2 = t.transpose(P.pbt[:, j * 128:(j + 1) * 128],
                                             wq[:, i * Cc + m * 128: i * Cc + (m + 1) * 128],
                                             P.ident_b)
                        return P.smm.inc(i2, 1), 1

                    def ve(v, m=m, w0=w0, wn=wn):
                        P.w(v, P.smm)
                        i2 = v.tensor_copy(out=rowbuf[:, w0 * 128:(w0 + wn) * 128],
                                           in_=P.pbt[:, 0:wn * 128])
                        return P.sev.inc(i2, 1), 1
                    yield f"t_{hname}_{m}_{w0}", [("tensor", tn), ("vector", ve)]

                def sy2(sync, m=m, hdst=hdst, R=R):
                    P.w(sync, P.sev)
                    dst = hdst[m * 128:(m + 1) * 128, :]
                    return P.sld.inc(sync.dma_start(out=dst, in_=rowbuf[:, 0:R]), 16), 16
                yield f"st_{hname}_{m}", [("sync", sy2)]

        def syf(sync):
            P.w(sync, P.sld)
            return sync.nop(), 1
        yield f"w_{hname}_fin", [("sync", syf)]


def ph_fp4_x(P):
    hx = P.d("xn_f", [D, L])
    hq = P.d("xq_b", [D, L], BF16)
    with contextlib.ExitStack() as es:
        xf = P.sb(es, "qx_xf", [128, DT * CH], F32)
        xtm = P.sb(es, "qx_tm", [128, D], F32)
        qtm = P.sb(es, "qx_qtm", [128, D], BF16)
        outb = P.sb(es, "qx_ob", [128, DT * CH], BF16)
        tmps = mk_qtmp(P, es, "qx_t", D)

        for c in range(NCH):
            def sy(sync, c=c):
                P.w(sync, P.smm)
                return P.sld.inc(sync.dma_start(out=sview(xf), in_=cview(hx, c)), 16), 16
            P.seq(f"qx_ld{c}", [("sync", sy)])

            for lt in range(4):
                def tn1(t, c=c, lt=lt):
                    P.w(t, P.sld, P.sev)
                    for k in range(DT):
                        bank = P.pbk[4 + k // 4]
                        i2 = t.transpose(bank[:, (k % 4) * 128:(k % 4 + 1) * 128],
                                         xf[:, k * CH + lt * 128: k * CH + lt * 128 + 128],
                                         P.ident_f)
                    return P.smm.inc(i2, 1), 1

                def ve1(v, c=c, lt=lt):
                    P.w(v, P.smm)
                    v.tensor_copy(out=xtm[:, 0:512], in_=P.pbk[4])
                    i2 = v.tensor_copy(out=xtm[:, 512:1024], in_=P.pbk[5])
                    return P.sev.inc(i2, 1), 1
                steps = [("tensor", tn1), ("vector", ve1)]
                emit_quant_steps2(P, steps, xtm, qtm, tmps, D // 16)

                def tn2(t, c=c, lt=lt):
                    P.w(t, P.sev)
                    for k in range(DT):
                        i2 = t.transpose(P.pbt[:, k * 128:(k + 1) * 128],
                                         qtm[:, k * 128:(k + 1) * 128], P.ident_b)
                    return P.smm.inc(i2, 1), 1

                def ve2(v, c=c, lt=lt):
                    P.w(v, P.smm)
                    for k in range(DT):
                        i2 = v.tensor_copy(out=outb[:, k * CH + lt * 128: k * CH + lt * 128 + 128],
                                           in_=P.pbt[:, k * 128:(k + 1) * 128])
                    return P.sev.inc(i2, 1), 1
                steps += [("tensor", tn2), ("vector", ve2)]
                P.seq(f"qx_c{c}_l{lt}", steps)

            def sy2(sync, c=c):
                P.w(sync, P.sev)
                return P.sld.inc(sync.dma_start(out=cview(hq, c), in_=sview(outb)), 16), 16
            P.seq(f"qx_st{c}", [("sync", sy2)])

        fin(P, "qx")


def ph_fp4_mlp(P):
    hxq = P.d("xq_b", [D, L], BF16)
    hqg = P.d("qgT_b", [D, IFF], BF16)
    hqu = P.d("quT_b", [D, IFF], BF16)
    hout = P.d("hq_b", [IFF, L], BF16)
    with contextlib.ExitStack() as es:
        mov = P.sb(es, "fm_mov", [128, DT * CH], BF16)
        wgw = P.sb(es, "fm_wg", [128, DT * 4 * 128], BF16)
        wuw = P.sb(es, "fm_wu", [128, DT * 4 * 128], BF16)
        gf = P.sb(es, "fm_gf", [128, 4 * CH], F32)
        hf = P.sb(es, "fm_hf", [128, ITT * CH], F32)
        htm = P.sb(es, "fm_htm", [128, IFF], F32)
        qtm = P.sb(es, "fm_qtm", [128, IFF], BF16)
        outb = P.sb(es, "fm_ob", [128, ITT * CH], BF16)
        tmps = mk_qtmp(P, es, "fm_t", IFF)
        NW = (ITT + 3) // 4

        def wslice(h, m0, mn):
            return h[:, m0 * 128:(m0 + mn) * 128].rearrange(
                "(k p) (m q) -> p k m q", p=128, q=128)

        for c in range(NCH):
            def sy(sync, c=c):
                P.w(sync, P.smm)
                return P.sld.inc(sync.dma_start(out=sview(mov), in_=cview(hxq, c)), 16), 16
            P.seq(f"fm_ld{c}", [("sync", sy)])

            for w in range(NW):
                m0 = w * 4
                mn = min(4, ITT - m0)

                def syw(sync, c=c, m0=m0, mn=mn):
                    P.w(sync, P.smm)
                    dg = wgw[:, 0:DT * mn * 128].rearrange("p (k m q) -> p k m q", m=mn, q=128)
                    du = wuw[:, 0:DT * mn * 128].rearrange("p (k m q) -> p k m q", m=mn, q=128)
                    P.sld.inc(sync.dma_start(out=dg, in_=wslice(hqg, m0, mn)), 16)
                    return P.sld.inc(sync.dma_start(out=du, in_=wslice(hqu, m0, mn)), 16), 16

                def tn_g(t, c=c, m0=m0, mn=mn):
                    P.w(t, P.sld, P.ssc, P.sev)
                    for j in range(mn):
                        for k in range(DT):
                            i2 = t.matmul(P.pbk[j], wv(wgw, k, j, mn), ct(mov, k),
                                          start=(k == 0), stop=(k == DT - 1))
                    return P.smm.inc(i2, 1), 1

                def sc_g(s, c=c, m0=m0, mn=mn):
                    P.w(s, P.smm)
                    for j in range(mn):
                        i2 = s.activation(out=ct(gf, j), in_=P.pbk[j], func=AF.Silu)
                    return P.ssc.inc(i2, 1), 1

                def tn_u(t, c=c, m0=m0, mn=mn):
                    P.w(t, P.ssc, P.sev)
                    for j in range(mn):
                        for k in range(DT):
                            i2 = t.matmul(P.pbk[j], wv(wuw, k, j, mn), ct(mov, k),
                                          start=(k == 0), stop=(k == DT - 1))
                    return P.smm.inc(i2, 1), 1

                def ve_u(v, c=c, m0=m0, mn=mn):
                    P.w(v, P.smm)
                    for j in range(mn):
                        i2 = v.tensor_mul(ct(hf, m0 + j), ct(gf, j), P.pbk[j])
                    return P.sev.inc(i2, 1), 1

                P.seq(f"fm_c{c}_w{w}", [("sync", syw), ("tensor", tn_g), ("scalar", sc_g),
                                        ("tensor", tn_u), ("vector", ve_u)])

            for lt in range(4):
                steps = []
                for w0 in range(0, ITT, 4):
                    wn = min(4, ITT - w0)

                    def tn1(t, c=c, lt=lt, w0=w0, wn=wn):
                        P.w(t, P.sev)
                        for j in range(wn):
                            k = w0 + j
                            i2 = t.transpose(P.pbk[4][:, j * 128:(j + 1) * 128],
                                             hf[:, k * CH + lt * 128: k * CH + lt * 128 + 128],
                                             P.ident_f)
                        return P.smm.inc(i2, 1), 1

                    def ve1(v, c=c, lt=lt, w0=w0, wn=wn):
                        P.w(v, P.smm)
                        i2 = v.tensor_copy(out=htm[:, w0 * 128:(w0 + wn) * 128],
                                           in_=P.pbk[4][:, 0:wn * 128])
                        return P.sev.inc(i2, 1), 1
                    steps += [("tensor", tn1), ("vector", ve1)]
                emit_quant_steps2(P, steps, htm, qtm, tmps, IFF // 16)
                for w0 in range(0, ITT, 8):
                    wn = min(8, ITT - w0)

                    def tn2(t, c=c, lt=lt, w0=w0, wn=wn):
                        P.w(t, P.sev)
                        for j in range(wn):
                            k = w0 + j
                            i2 = t.transpose(P.pbt[:, j * 128:(j + 1) * 128],
                                             qtm[:, k * 128:(k + 1) * 128], P.ident_b)
                        return P.smm.inc(i2, 1), 1

                    def ve2(v, c=c, lt=lt, w0=w0, wn=wn):
                        P.w(v, P.smm)
                        for j in range(wn):
                            k = w0 + j
                            i2 = v.tensor_copy(
                                out=outb[:, k * CH + lt * 128: k * CH + lt * 128 + 128],
                                in_=P.pbt[:, j * 128:(j + 1) * 128])
                        return P.sev.inc(i2, 1), 1
                    steps += [("tensor", tn2), ("vector", ve2)]
                P.seq(f"fm_q_c{c}_l{lt}", steps)

            def sy2(sync, c=c):
                P.w(sync, P.sev)
                return P.sld.inc(sync.dma_start(
                    out=hout[:, c * CH:(c + 1) * CH].rearrange("(k p) n -> p k n", p=128),
                    in_=outb.rearrange("p (k n) -> p k n", n=CH)), 16), 16
            P.seq(f"fm_st{c}", [("sync", sy2)])

        fin(P, "fm")


def ph_l0_down(P):
    hhq = P.d("hq_b", [IFF, L], BF16)
    hqd = P.d("qdT_b", [IFF, D], BF16)
    hx0 = P.d("x0", [D, L])
    hs0 = P.d("s0_f", [D, L])
    hx1 = P.d("x1_f", [D, L])
    with contextlib.ExitStack() as es:
        qd = P.sb(es, "dn_w", [128, ITT * DT * 128], BF16)
        mov = P.sb(es, "dn_mov", [128, ITT * CH], BF16)
        x0s = P.sb(es, "dn_x0", [128, DT * CH], F32)
        s0s = P.sb(es, "dn_s0", [128, DT * CH], F32)
        gs = P.sb(es, "dn_gs", [128, 4 * CH], F32)
        outf = P.sb(es, "dn_of", [128, DT * CH], F32)

        def gpw(g):
            P.w(g)
            dst = qd.rearrange("p (k m q) -> p k m q", m=DT, q=128)
            return P.sgp.inc(g.dma_start(out=dst, in_=wpack_hbm(hqd)), 16), 16
        P.seq("dn_w", [("gpsimd", gpw)])

        for c in range(NCH):
            steps = []

            def sy(sync, c=c):
                P.w(sync, P.smm, P.sev)
                P.sld.inc(sync.dma_start(out=sview(mov), in_=cview(hhq, c)), 16)
                P.sld.inc(sync.dma_start(out=sview(x0s), in_=cview(hx0, c)), 16)
                return P.sld.inc(sync.dma_start(out=sview(s0s), in_=cview(hs0, c)), 16), 16
            steps.append(("sync", sy))

            for half in range(2):
                def tn(t, c=c, half=half):
                    P.w(t, P.sld, P.sgp, P.sev, P.ssc)
                    for j in range(4):
                        m = half * 4 + j
                        for k in range(ITT):
                            i2 = t.matmul(P.pbk[j], wv(qd, k, m, DT), ct(mov, k),
                                          start=(k == 0), stop=(k == ITT - 1))
                    return P.smm.inc(i2, 1), 1

                def sc(s, c=c, half=half):
                    P.w(s, P.smm)
                    for j in range(4):
                        i2 = s.activation(out=ct(gs, j), in_=P.pbk[j], func=AF.Silu)
                    return P.ssc.inc(i2, 1), 1

                def ve(v, c=c, half=half):
                    P.w(v, P.ssc)
                    for j in range(4):
                        m = half * 4 + j
                        v.tensor_mul(ct(outf, m), ct(s0s, m), ct(gs, j))
                        i2 = v.tensor_add(ct(outf, m), ct(outf, m), ct(x0s, m))
                    return P.sev.inc(i2, 1), 1
                steps += [("tensor", tn), ("scalar", sc), ("vector", ve)]

            def sy2(sync, c=c):
                P.w(sync, P.sev)
                return P.sld.inc(sync.dma_start(out=cview(hx1, c), in_=sview(outf)), 16), 16
            steps.append(("sync", sy2))
            P.seq(f"dn_c{c}", steps)

        fin(P, "dn")




def ph_l1_qkv(P):
    hx1 = P.d("xn1_b", [D, L], BF16)
    outs = {"q": P.d("qpre_f", [D, L]), "k": P.d("kpre_f", [D, L]),
            "v": P.d("vpre_f", [D, L])}
    hbeta = P.d("beta_f", [H, L])
    hpadj = P.d("padj_f", [H, L])
    hcmass = P.d("cmass_f", [H, L])
    with contextlib.ExitStack() as es:
        ws = {nm: load_wpack(P, es, f"l1{nm}", w, DT, DT)
              for nm, w in (("q", "wq"), ("k", "wk"), ("v", "wv"))}
        wbs = P.sb(es, "l1_wb", [128, DT * H], BF16)
        igs = P.sb(es, "l1_ig", [128, DT * H], BF16)
        igb_c = colvec(P, es, "igb", H)
        mov = P.sb(es, "l1q_mov", [128, DT * CH], BF16)
        outf = P.sb(es, "l1q_of", [128, DT * CH], F32)
        small = P.sb(es, "l1q_sm", [H, 3 * CH], F32)

        def gpw(g):
            P.w(g)
            for t, hn in ((wbs, "wb"), (igs, "igw")):
                hv = P.d(hn, [D, H], F32)
                i2 = g.dma_start(out=t.rearrange("p (k m) -> p k m", m=H),
                                 in_=hv.rearrange("(k p) m -> p k m", p=128))
                P.sgp.inc(i2, 16)
            return i2, 16
        P.seq("l1_smallw", [("gpsimd", gpw)])

        for c in range(NCH):
            def sy(sync, c=c):
                P.w(sync)
                return P.sld.inc(sync.dma_start(out=sview(mov), in_=cview(hx1, c)), 16), 16
            P.seq(f"l1q_ld{c}", [("sync", sy)])

            for nm in ("q", "k", "v"):
                for half in range(2):
                    def tn(t, c=c, nm=nm, half=half):
                        P.w(t)
                        for j in range(4):
                            m = half * 4 + j
                            for k in range(DT):
                                i2 = t.matmul(P.pbk[j], wv(ws[nm], k, m, DT), ct(mov, k),
                                              start=(k == 0), stop=(k == DT - 1))
                        return P.smm.inc(i2, 1), 1

                    def ve(v, c=c, nm=nm, half=half):
                        P.w(v)
                        for j in range(4):
                            m = half * 4 + j
                            i2 = v.tensor_copy(out=ct(outf, m), in_=P.pbk[j])
                        return P.sev.inc(i2, 1), 1
                    P.seq(f"l1q_{nm}{c}h{half}", [("tensor", tn), ("vector", ve)])

                def sy2(sync, c=c, nm=nm):
                    P.w(sync)
                    return P.sld.inc(sync.dma_start(out=cview(outs[nm], c), in_=sview(outf)), 16), 16
                P.seq(f"l1q_st_{nm}{c}", [("sync", sy2)])

            def tn_b(t, c=c):
                P.w(t)
                for k in range(DT):
                    t.matmul(P.pbk[0][0:H, :], wbs[:, k * H:(k + 1) * H], ct(mov, k),
                             start=(k == 0), stop=(k == DT - 1))
                for k in range(DT):
                    i2 = t.matmul(P.pbk[1][0:H, :], igs[:, k * H:(k + 1) * H], ct(mov, k),
                                  start=(k == 0), stop=(k == DT - 1))
                return P.smm.inc(i2, 1), 1

            def sc_b(s, c=c):
                P.w(s)
                s.activation(out=small[:, 0:CH], in_=P.pbk[0][0:H, :], func=AF.Sigmoid)
                i2 = s.activation(out=small[:, CH:2 * CH], in_=P.pbk[1][0:H, :],
                                  func=AF.Sigmoid, bias=igb_c)
                return P.ssc.inc(i2, 1), 1

            def ve_b(v, c=c):
                P.w(v)
                pv = small[:, CH:2 * CH]
                pa = small[:, 2 * CH:3 * CH]
                v.tensor_scalar(out=pa, in0=pv, scalar1=0.09, scalar2=0.9,
                                op0=ALU.mult, op1=ALU.add)
                v.tensor_mul(pa, pa, pv)
                i2 = v.tensor_scalar(out=pv, in0=pa, scalar1=-1.0, scalar2=1.0,
                                     op0=ALU.mult, op1=ALU.add)
                return P.sev.inc(i2, 1), 1

            def sy_b(sync, c=c):
                P.w(sync)
                P.sld.inc(sync.dma_start(out=hbeta[:, c * CH:(c + 1) * CH], in_=small[:, 0:CH]), 16)
                P.sld.inc(sync.dma_start(out=hpadj[:, c * CH:(c + 1) * CH], in_=small[:, 2 * CH:3 * CH]), 16)
                return P.sld.inc(sync.dma_start(out=hcmass[:, c * CH:(c + 1) * CH], in_=small[:, CH:2 * CH]), 16), 16
            P.seq(f"l1q_beta{c}", [("tensor", tn_b), ("scalar", sc_b), ("vector", ve_b), ("sync", sy_b)])

        fin(P, "l1q")


def ph_l1_conv(P):
    PAD = 32
    specs = [("qpre_f", "qcw", "qs_f"), ("kpre_f", "kcw", "ks_f"), ("vpre_f", "vcw", "vs_f")]
    with contextlib.ExitStack() as es:
        xt = P.sb(es, "cv4_x", [128, PAD + L], F32)
        acc = P.sb(es, "cv4_a", [128, L], F32)
        wc = {w: colvec2d(P, es, w, 4) for _, w, _ in specs}

        def gp0(g):
            P.w(g)
            return P.sgp.inc(g.memset(xt[:, 0:PAD], 0.0), 1), 1
        P.seq("cv4_pad", [("gpsimd", gp0)])

        for src, w, dst in specs:
            hs = P.d(src, [D, L])
            hd = P.d(dst, [D, L])
            for dt in range(DT):
                def sy(sync, src=src, dt=dt):
                    P.w(sync)
                    return P.sld.inc(sync.dma_start(
                        out=xt[:, PAD:PAD + L],
                        in_=P.d(src, [D, L])[dt * 128:(dt + 1) * 128, :]), 16), 16

                def ve(v, w=w, dt=dt):
                    P.w(v)
                    K = 4
                    for k in range(K):
                        sh = xt[:, PAD - (K - 1) + k: PAD - (K - 1) + k + L]
                        col = wc[w][:, dt * K + k: dt * K + k + 1]
                        if k == 0:
                            i2 = v.tensor_scalar(out=acc, in0=sh, scalar1=col,
                                                 scalar2=None, op0=ALU.mult)
                        else:
                            i2 = v.scalar_tensor_tensor(out=acc, in0=sh, scalar=col,
                                                        in1=acc, op0=ALU.mult, op1=ALU.add)
                    return P.sev.inc(i2, 1), 1

                def sc(s, dt=dt):
                    P.w(s)
                    return P.ssc.inc(s.activation(out=xt[:, PAD:PAD + L], in_=acc, func=AF.Silu), 1), 1

                def sy2(sync, dst=dst, dt=dt):
                    P.w(sync)
                    return P.sld.inc(sync.dma_start(
                        out=P.d(dst, [D, L])[dt * 128:(dt + 1) * 128, :],
                        in_=xt[:, PAD:PAD + L]), 16), 16
                P.seq(f"cv4_{src}_{dt}", [("sync", sy), ("vector", ve), ("scalar", sc), ("sync", sy2)])
        fin(P, "cv4")


def colvec2d(P, es, name, K):
    hv = P.d(name, [128, DT * K], F32)
    out = P.sb(es, f"cv2_{name}", [128, DT * K], F32)

    def sy(sync):
        P.w(sync)
        return Sem.inc(P.sld, sync.dma_start(out=out, in_=hv), 16), 16
    P.seq(f"cv2_{name}", [("sync", sy)])
    return out


def host_colvec2d(w):
    w = np.asarray(w, np.float32)
    if w.ndim == 3:
        w = w.reshape(-1, w.shape[-1])
    K = w.shape[-1]
    return np.ascontiguousarray(w.reshape(DT, 128, K).transpose(1, 0, 2).reshape(128, DT * K))


def ph_l1_fir2(P):
    PAD = 32
    hs = P.d("vs_f", [D, L])
    with contextlib.ExitStack() as es:
        xt = P.sb(es, "fir_x", [128, PAD + L], F32)
        acc = P.sb(es, "fir_a", [128, L], F32)
        wcs = {K: colvec2d(P, es, f"fir{K}w", K) for K in (3, 7, 15, 31)}

        def gp0(g):
            P.w(g)
            return P.sgp.inc(g.memset(xt[:, 0:PAD], 0.0), 1), 1
        P.seq("fir_pad", [("gpsimd", gp0)])

        for dt in range(DT):
            def sy(sync, dt=dt):
                P.w(sync)
                return P.sld.inc(sync.dma_start(
                    out=xt[:, PAD:PAD + L], in_=hs[dt * 128:(dt + 1) * 128, :]), 16), 16
            P.seq(f"fir_ld{dt}", [("sync", sy)])

            for K in (3, 7, 15, 31):
                hd = P.d(f"fir{K}_f", [D, L])

                def ve(v, K=K, dt=dt):
                    P.w(v)
                    for k in range(K):
                        sh = xt[:, PAD - (K - 1) + k: PAD - (K - 1) + k + L]
                        col = wcs[K][:, dt * K + k: dt * K + k + 1]
                        if k == 0:
                            i2 = v.tensor_scalar(out=acc, in0=sh, scalar1=col,
                                                 scalar2=None, op0=ALU.mult)
                        else:
                            i2 = v.scalar_tensor_tensor(out=acc, in0=sh, scalar=col,
                                                        in1=acc, op0=ALU.mult, op1=ALU.add)
                    return P.sev.inc(i2, 1), 1

                def sy2(sync, K=K, dt=dt):
                    P.w(sync)
                    return P.sld.inc(sync.dma_start(
                        out=P.d(f"fir{K}_f", [D, L])[dt * 128:(dt + 1) * 128, :],
                        in_=acc), 16), 16
                P.seq(f"fir_{K}_{dt}", [("vector", ve), ("sync", sy2)])
        fin(P, "fir")


def ph_l1_delta(P):
    hq = P.d("qs_f", [D, L])
    hk = P.d("ks_f", [D, L])
    hv = P.d("vs_f", [D, L])
    hbeta = P.d("beta_f", [H, L])
    hdelta = P.d("delta_f", [D, L])
    NT = 2
    with contextlib.ExitStack() as es:
        qn = P.sb(es, "dl_qn", [128, NT * L], BF16)
        kn = P.sb(es, "dl_kn", [128, NT * L], BF16)
        vb = P.sb(es, "dl_vb", [128, NT * L], BF16)
        kb = P.sb(es, "dl_kb", [128, NT * L], BF16)
        xf = P.sb(es, "dl_xf", [128, NT * CH], F32)
        xf2 = P.sb(es, "dl_xf2", [128, NT * CH], F32)
        sq = P.sb(es, "dl_sq", [128, NT * CH], BF16)
        brow = P.sb(es, "dl_brow", [1, L], F32)
        brep = P.sb(es, "dl_brep", [128, CH], F32)
        rrep = P.sb(es, "dl_rrep", [128, CH], F32)
        S_f = P.sb(es, "dl_Sf", [128, NT * 256], F32)
        S_b = P.sb(es, "dl_Sb", [128, NT * 256], BF16)
        npow = P.sb(es, "dl_np", [128, 128], BF16)
        npow2 = P.sb(es, "dl_np2", [128, 128], BF16)
        spow = P.sb(es, "dl_sp", [128, 7 * 128], BF16)
        attT = P.sb(es, "dl_attT", [128, 128], BF16)
        xtm = P.sb(es, "dl_xtm", [128, 512], BF16)
        xtm2 = P.sb(es, "dl_xtm2", [128, 512], BF16)
        ktm = P.sb(es, "dl_ktm", [128, 256], BF16)
        wfm = P.sb(es, "dl_wfm", [128, 256], BF16)
        uadj = P.sb(es, "dl_uadj", [128, 256], BF16)
        otm = P.sb(es, "dl_otm", [128, 256], F32)
        ofm = P.sb(es, "dl_ofm", [128, 256], F32)

        for h in range(H):
            def syb(sync, h=h):
                P.w(sync)
                return P.sld.inc(sync.dma_start(out=brow, in_=hbeta[h:h + 1, :]), 16), 16
            P.seq(f"dl_b{h}", [("sync", syb)])

            for c in range(NCH):
                sl = slice(c * CH, (c + 1) * CH)

                def sy(sync, h=h, c=c):
                    P.w(sync)
                    src = hq[h * 256:(h + 1) * 256, c * CH:(c + 1) * CH]
                    P.sld.inc(sync.dma_start(out=xf.rearrange("p (t n) -> p t n", n=CH),
                                             in_=src.rearrange("(t p) n -> p t n", p=128)), 16)
                    src2 = hk[h * 256:(h + 1) * 256, c * CH:(c + 1) * CH]
                    return P.sld.inc(sync.dma_start(out=xf2.rearrange("p (t n) -> p t n", n=CH),
                                                    in_=src2.rearrange("(t p) n -> p t n", p=128)), 16), 16

                def tn_r(t, h=h, c=c):
                    P.w(t)
                    i2 = t.matmul(P.pbk[2], P.ones_f1, brow[:, c * CH:(c + 1) * CH],
                                  start=True, stop=True)
                    return P.smm.inc(i2, 1), 1

                def sc_q(s, h=h, c=c):
                    P.w(s)
                    i2 = s.activation(out=sview(sq, CH), in_=sview(xf, CH), func=AF.Square)
                    return P.ssc.inc(i2, 1), 1

                def tn_q(t, h=h, c=c):
                    P.w(t)
                    for tt in range(NT):
                        i2 = t.matmul(P.pbk[0], P.ones_b, ct(sq, tt, CH),
                                      start=(tt == 0), stop=(tt == NT - 1))
                    return P.smm.inc(i2, 1), 1

                def ve_q(v, h=h, c=c):
                    P.w(v)
                    v.tensor_copy(out=brep, in_=P.pbk[2])
                    v.tensor_scalar(out=rrep, in0=P.pbk[0], scalar1=1e-6, scalar2=None,
                                    op0=ALU.add)
                    i2 = v.reciprocal(out=rrep, in_=rrep)
                    return P.sev.inc(i2, 1), 1

                def sc_q2(s, h=h, c=c):
                    P.w(s)
                    i2 = s.activation(out=rrep, in_=rrep, func=AF.Sqrt)
                    return P.ssc.inc(i2, 1), 1

                def ve_q3(v, h=h, c=c):
                    P.w(v)
                    for tt in range(NT):
                        i2 = v.tensor_mul(qn[:, tt * L + c * CH: tt * L + (c + 1) * CH],
                                          ct(xf, tt, CH), rrep)
                    return P.sev.inc(i2, 1), 1
                P.seq(f"dl_q{h}_{c}", [("sync", sy), ("tensor", tn_r), ("scalar", sc_q),
                                       ("tensor", tn_q), ("vector", ve_q), ("scalar", sc_q2),
                                       ("vector", ve_q3)])

                def sc_k(s, h=h, c=c):
                    P.w(s)
                    i2 = s.activation(out=sview(sq, CH), in_=sview(xf2, CH), func=AF.Square)
                    return P.ssc.inc(i2, 1), 1

                def tn_k(t, h=h, c=c):
                    P.w(t)
                    for tt in range(NT):
                        i2 = t.matmul(P.pbk[0], P.ones_b, ct(sq, tt, CH),
                                      start=(tt == 0), stop=(tt == NT - 1))
                    return P.smm.inc(i2, 1), 1

                def ve_k(v, h=h, c=c):
                    P.w(v)
                    v.tensor_scalar(out=rrep, in0=P.pbk[0], scalar1=1e-6, scalar2=None,
                                    op0=ALU.add)
                    i2 = v.reciprocal(out=rrep, in_=rrep)
                    return P.sev.inc(i2, 1), 1

                def sc_k2(s, h=h, c=c):
                    P.w(s)
                    i2 = s.activation(out=rrep, in_=rrep, func=AF.Sqrt)
                    return P.ssc.inc(i2, 1), 1

                def ve_k3(v, h=h, c=c):
                    P.w(v)
                    for tt in range(NT):
                        v.tensor_mul(ct(xf2, tt, CH), ct(xf2, tt, CH), rrep)
                        v.tensor_copy(out=kn[:, tt * L + c * CH: tt * L + (c + 1) * CH],
                                      in_=ct(xf2, tt, CH))
                        i2 = v.tensor_mul(kb[:, tt * L + c * CH: tt * L + (c + 1) * CH],
                                          ct(xf2, tt, CH), brep)
                    return P.sev.inc(i2, 1), 1
                P.seq(f"dl_k{h}_{c}", [("scalar", sc_k), ("tensor", tn_k), ("vector", ve_k),
                                       ("scalar", sc_k2), ("vector", ve_k3)])

                def sy_v(sync, h=h, c=c):
                    P.w(sync)
                    src = hv[h * 256:(h + 1) * 256, c * CH:(c + 1) * CH]
                    return P.sld.inc(sync.dma_start(out=xf.rearrange("p (t n) -> p t n", n=CH),
                                                    in_=src.rearrange("(t p) n -> p t n", p=128)), 16), 16

                def ve_v(v, h=h, c=c):
                    P.w(v)
                    for tt in range(NT):
                        i2 = v.tensor_mul(vb[:, tt * L + c * CH: tt * L + (c + 1) * CH],
                                          ct(xf, tt, CH), brep)
                    return P.sev.inc(i2, 1), 1
                P.seq(f"dl_v{h}_{c}", [("sync", sy_v), ("vector", ve_v)])

            def gp_s(g, h=h):
                P.w(g)
                g.memset(S_f, 0.0)
                return P.sgp.inc(g.memset(S_b, 0.0), 1), 1
            P.seq(f"dl_s0_{h}", [("gpsimd", gp_s)])

            for ck in range(NCK):
                col = ck * CK

                def qc(buf, tt, w=CK):
                    return buf[:, tt * L + col: tt * L + col + w]

                def tn_A(t, h=h, ck=ck):
                    P.w(t)
                    for tt in range(NT):
                        t.matmul(P.pbk[0][:, 0:128], qc(kb, tt), qc(kn, tt),
                                 start=(tt == 0), stop=(tt == NT - 1))
                    for tt in range(NT):
                        i2 = t.matmul(P.pbk[0][:, 128:256], qc(kn, tt), qc(qn, tt),
                                      start=(tt == 0), stop=(tt == NT - 1))
                    return P.smm.inc(i2, 1), 1

                def ve_A(v, h=h, ck=ck):
                    P.w(v)
                    v.tensor_mul(npow, P.pbk[0][:, 0:128], P.mask_lo)
                    i2 = v.tensor_mul(attT, P.pbk[0][:, 128:256], P.mask_upi)
                    return P.sev.inc(i2, 1), 1
                P.seq(f"dl_A{h}_{ck}", [("tensor", tn_A), ("vector", ve_A)])

                def tn_t1(t, h=h, ck=ck):
                    P.w(t)
                    i2 = t.transpose(P.pbt[:, 0:128], npow, P.ident_b)
                    return P.smm.inc(i2, 1), 1

                def ve_t1(v, h=h, ck=ck):
                    P.w(v)
                    i2 = v.tensor_copy(out=spow[:, 0:128], in_=P.pbt[:, 0:128])
                    return P.sev.inc(i2, 1), 1
                P.seq(f"dl_t1{h}_{ck}", [("tensor", tn_t1), ("vector", ve_t1)])

                cur_n = npow
                for j in range(1, 7):
                    def tn_d(t, h=h, ck=ck, j=j, cur_n=cur_n):
                        P.w(t)
                        t.matmul(P.pbk[1][:, 0:128], spow[:, (j - 1) * 128: j * 128], cur_n,
                                 start=True, stop=True)
                        i2 = t.matmul(P.pbk[1][:, 128:256], cur_n, spow[:, (j - 1) * 128: j * 128],
                                      start=True, stop=True)
                        return P.smm.inc(i2, 1), 1

                    nxt = npow2 if cur_n is npow else npow

                    def ve_d(v, h=h, ck=ck, j=j, nxt=nxt):
                        P.w(v)
                        v.tensor_copy(out=nxt, in_=P.pbk[1][:, 0:128])
                        i2 = v.tensor_copy(out=spow[:, j * 128:(j + 1) * 128], in_=P.pbk[1][:, 128:256])
                        return P.sev.inc(i2, 1), 1
                    P.seq(f"dl_d{h}_{ck}_{j}", [("tensor", tn_d), ("vector", ve_d)])
                    cur_n = nxt

                def tn_t2(t, h=h, ck=ck):
                    P.w(t)
                    for tt in range(NT):
                        t.transpose(P.pbt[:, tt * 128:(tt + 1) * 128], qc(vb, tt), P.ident_b)
                    for tt in range(NT):
                        t.transpose(P.pbt[:, 256 + tt * 128: 256 + (tt + 1) * 128], qc(kb, tt), P.ident_b)
                    for tt in range(NT):
                        i2 = t.transpose(P.pbt[:, 512 + tt * 128: 512 + (tt + 1) * 128], qc(kn, tt), P.ident_b)
                    return P.smm.inc(i2, 1), 1

                def ve_t2(v, h=h, ck=ck):
                    P.w(v)
                    v.tensor_copy(out=xtm, in_=P.pbt[:, 0:512])
                    i2 = v.tensor_copy(out=ktm, in_=P.pbt[:, 512:768])
                    return P.sev.inc(i2, 1), 1
                P.seq(f"dl_t2{h}_{ck}", [("tensor", tn_t2), ("vector", ve_t2)])

                src_x, dst_x = xtm, xtm2
                for j in range(6, 0, -1):
                    def tn_a(t, h=h, ck=ck, j=j, src_x=src_x):
                        P.w(t)
                        i2 = t.matmul(P.pbk[1], spow[:, j * 128:(j + 1) * 128], src_x,
                                      start=True, stop=True)
                        return P.smm.inc(i2, 1), 1

                    def ve_a(v, h=h, ck=ck, src_x=src_x, dst_x=dst_x):
                        P.w(v)
                        i2 = v.tensor_add(dst_x, src_x, P.pbk[1])
                        return P.sev.inc(i2, 1), 1
                    P.seq(f"dl_a{h}_{ck}_{j}", [("tensor", tn_a), ("vector", ve_a)])
                    src_x, dst_x = dst_x, src_x

                def tn_a0(t, h=h, ck=ck, src_x=src_x):
                    P.w(t)
                    i2 = t.matmul(P.pbk[1], spow[:, 0:128], src_x, start=True, stop=True)
                    return P.smm.inc(i2, 1), 1

                def ve_a0(v, h=h, ck=ck, src_x=src_x, dst_x=dst_x):
                    P.w(v)
                    i2 = v.tensor_sub(dst_x, src_x, P.pbk[1])
                    return P.sev.inc(i2, 1), 1
                P.seq(f"dl_a0{h}_{ck}", [("tensor", tn_a0), ("vector", ve_a0)])
                uw = dst_x

                def tn_t3(t, h=h, ck=ck, uw=uw):
                    P.w(t)
                    for tt in range(NT):
                        i2 = t.transpose(P.pbt[:, tt * 128:(tt + 1) * 128],
                                         uw[:, 256 + tt * 128: 256 + (tt + 1) * 128], P.ident_b)
                    return P.smm.inc(i2, 1), 1

                def ve_t3(v, h=h, ck=ck):
                    P.w(v)
                    i2 = v.tensor_copy(out=wfm, in_=P.pbt[:, 0:256])
                    return P.sev.inc(i2, 1), 1
                P.seq(f"dl_t3{h}_{ck}", [("tensor", tn_t3), ("vector", ve_t3)])

                def tn_u(t, h=h, ck=ck):
                    P.w(t)
                    for tt in range(NT):
                        i2 = t.matmul(P.pbk[1][:, 0:256], wfm[:, tt * 128:(tt + 1) * 128],
                                      S_b[:, tt * 256:(tt + 1) * 256],
                                      start=(tt == 0), stop=(tt == NT - 1))
                    return P.smm.inc(i2, 1), 1

                def ve_u(v, h=h, ck=ck, uw=uw):
                    P.w(v)
                    i2 = v.tensor_sub(uadj, uw[:, 0:256], P.pbk[1][:, 0:256])
                    return P.sev.inc(i2, 1), 1
                P.seq(f"dl_u{h}_{ck}", [("tensor", tn_u), ("vector", ve_u)])

                def tn_o(t, h=h, ck=ck):
                    P.w(t)
                    for tt in range(NT):
                        t.matmul(P.pbk[2][:, 0:256], qc(qn, tt), S_b[:, tt * 256:(tt + 1) * 256],
                                 start=(tt == 0), stop=False)
                    t.matmul(P.pbk[2][:, 0:256], attT, uadj, start=False, stop=True)
                    for tt in range(NT):
                        i2 = t.matmul(P.pbk[3], ktm[:, tt * 128:(tt + 1) * 128], uadj,
                                      start=True, stop=True)
                    return P.smm.inc(i2, 1), 1

                def tn_o2(t, h=h, ck=ck):
                    P.w(t)
                    for tt in range(NT):
                        t.matmul(P.pbk[2][:, 0:256], qc(qn, tt), S_b[:, tt * 256:(tt + 1) * 256],
                                 start=(tt == 0), stop=False)
                    t.matmul(P.pbk[2][:, 0:256], attT, uadj, start=False, stop=True)
                    t.matmul(P.pbk[3][:, 0:256], ktm[:, 0:128], uadj, start=True, stop=True)
                    i2 = t.matmul(P.pbk[4][:, 0:256], ktm[:, 128:256], uadj, start=True, stop=True)
                    return P.smm.inc(i2, 1), 1

                def ve_o(v, h=h, ck=ck):
                    P.w(v)
                    v.tensor_copy(out=otm, in_=P.pbk[2][:, 0:256])
                    v.tensor_add(S_f[:, 0:256], S_f[:, 0:256], P.pbk[3][:, 0:256])
                    i2 = v.tensor_add(S_f[:, 256:512], S_f[:, 256:512], P.pbk[4][:, 0:256])
                    return P.sev.inc(i2, 1), 1

                def sc_o(s, h=h, ck=ck):
                    P.w(s)
                    i2 = s.activation(out=S_b, in_=S_f, func=AF.Copy)
                    return P.ssc.inc(i2, 1), 1
                P.seq(f"dl_o{h}_{ck}", [("tensor", tn_o2), ("vector", ve_o), ("scalar", sc_o)])

                def tn_t4(t, h=h, ck=ck):
                    P.w(t)
                    for tt in range(NT):
                        i2 = t.transpose(P.pbk[5][:, tt * 128:(tt + 1) * 128],
                                         otm[:, tt * 128:(tt + 1) * 128], P.ident_f)
                    return P.smm.inc(i2, 1), 1

                def ve_t4(v, h=h, ck=ck):
                    P.w(v)
                    i2 = v.tensor_copy(out=ofm, in_=P.pbk[5][:, 0:256])
                    return P.sev.inc(i2, 1), 1

                def sy_o(sync, h=h, ck=ck):
                    P.w(sync)
                    dst = hdelta[h * 256:(h + 1) * 256, col:col + CK]
                    i2 = sync.dma_start(out=dst.rearrange("(t p) n -> p t n", p=128),
                                        in_=ofm.rearrange("p (t n) -> p t n", n=128))
                    return P.sld.inc(i2, 16), 16
                P.seq(f"dl_t4{h}_{ck}", [("tensor", tn_t4), ("vector", ve_t4), ("sync", sy_o)])

        fin(P, "delta")



MCH = 1024
NMCH = L // MCH
RKT = 48
RET = 16


def ph_l1_w1bf(P):
    hw = P.d("rw1", [6144, 2048], F32)
    hb = P.d("rw1_b", [6144, 2048], BF16)
    with contextlib.ExitStack() as es:
        tile = P.sb(es, "w1c_t", [128, 2048], BF16)
        for r in range(RKT):
            def gp(g, r=r):
                P.w(g)
                return P.sgp.inc(g.dma_start(out=tile, in_=hw[r * 128:(r + 1) * 128, :]), 16), 16

            def sy(sync, r=r):
                P.w(sync)
                return P.sld.inc(sync.dma_start(out=hb[r * 128:(r + 1) * 128, :], in_=tile), 16), 16
            P.seq(f"w1c_{r}", [("gpsimd", gp), ("sync", sy)])
        fin(P, "w1c")


def ph_l1_router(P):
    srcs = ["xn1_b"] + [f"fir{K}_f" for K in (3, 7, 15, 31)] + ["delta_f"]
    hw1 = P.d("rw1_b", [6144, 2048], BF16)
    hprob = P.d("probs_f", [20, L])
    with contextlib.ExitStack() as es:
        movs = [P.sb(es, f"rt_m{i}", [128, DT * MCH], BF16) for i in range(6)]
        wbuf = [P.sb(es, f"rt_w{i}", [128, RKT * 128], BF16) for i in range(2)]
        s_rw = [P.newsem(f"rt_w{i}") for i in range(2)]
        s_rmv = P.newsem("rt_mv")
        rec_mm = {}
        rec_sc = {}
        hmid = P.sb(es, "rt_h", [128, RET * MCH], BF16)
        rb1_c = colvec(P, es, "rb1", 2048)
        rw2 = P.sb(es, "rt_w2", [128, RET * 20], BF16)
        lg = P.sb(es, "rt_lg", [32, MCH], F32)
        tm = P.sb(es, "rt_tm", [32, MCH], F32)
        ex = P.sb(es, "rt_ex", [32, MCH], F32)
        mx = P.sb(es, "rt_mx", [32, MCH // 32 * 4], F32)
        ti20 = P.sb(es, "rt_ti", [20, 1], F32)
        sel = P.sb(es, "rt_sel", [H, 20], F32)
        lt_col = colvec(P, es, "ltau", H)
        rb2_c = colvec(P, es, "rb2", 20)

        def gp_t(g):
            P.w(g)
            g.memset(sel, 1.0)
            g.affine_select(out=sel, in_=sel, compare_op=ALU.is_ge, fill=0.0,
                            base=0, pattern=[[1, 20]], channel_multiplier=-5)
            i2 = g.affine_select(out=sel, in_=sel, compare_op=ALU.is_ge, fill=0.0,
                                 base=4, pattern=[[-1, 20]], channel_multiplier=5)
            return P.sgp.inc(i2, 1), 1

        def sc_t(s):
            P.w(s)
            return P.ssc.inc(s.activation(out=lt_col, in_=lt_col, func=AF.Exp), 1), 1

        def ve_t(v):
            P.w(v)
            v.tensor_scalar_max(out=lt_col, in0=lt_col, scalar1=0.5)
            i2 = v.reciprocal(out=lt_col, in_=lt_col)
            return P.sev.inc(i2, 1), 1

        def tn_t(t):
            P.w(t)
            return P.smm.inc(t.matmul(P.pbk[6][0:20, 0:1], sel, lt_col, start=True, stop=True), 1), 1

        def ve_t2(v):
            P.w(v)
            i2 = v.tensor_copy(out=ti20, in_=P.pbk[6][0:20, 0:1])
            return P.sev.inc(i2, 1), 1
        P.seq("rt_tau", [("gpsimd", gp_t), ("scalar", sc_t), ("vector", ve_t),
                         ("tensor", tn_t), ("vector", ve_t2)])

        def gp_w2(g):
            P.w(g)
            hv = P.d("rw2", [2048, 20], F32)
            i2 = g.dma_start(out=rw2.rearrange("p (k m) -> p k m", m=20),
                             in_=hv.rearrange("(k p) m -> p k m", p=128))
            return P.sgp.inc(i2, 16), 16
        P.seq("rt_w2l", [("gpsimd", gp_w2)])

        for mc in range(NMCH):
            def gp_m(g, mc=mc):
                if mc == 0:
                    P.w(g)
                elif (mc - 1, RET - 1) in rec_mm:
                    g.wait_ge(P.smm.h, rec_mm[(mc - 1, RET - 1)])
                for i, s in enumerate(srcs):
                    dt = BF16 if s.endswith("_b") else F32
                    hv = P.d(s, [D, L], dt)
                    i2 = g.dma_start(out=sview(movs[i], MCH),
                                     in_=cview(hv, mc, MCH))
                    s_rmv.inc(i2, 16)
                return i2, 16
            P.seq(f"rt_mv{mc}", [("gpsimd", gp_m)])

            for et in range(RET):
                wb = et % 2
                bk = (et % 2) * 2

                def sy_w(sync, mc=mc, et=et, wb=wb):
                    if mc == 0 and et < 2:
                        P.w(sync)
                    elif (mc, et - 2) in rec_mm:
                        sync.wait_ge(P.smm.h, rec_mm[(mc, et - 2)])
                    elif et < 2:
                        sync.wait_ge(P.smm.h, rec_mm[(mc - 1, RET - 2 + et)])
                    sl = hw1[:, et * 128:(et + 1) * 128]
                    i2 = sync.dma_start(out=wbuf[wb].rearrange("p (k q) -> p k q", q=128),
                                        in_=sl.rearrange("(k p) q -> p k q", p=128))
                    return s_rw[wb].inc(i2, 16), 16

                def tn_w(t, mc=mc, et=et, wb=wb, bk=bk):
                    if mc == 0 and et == 0:
                        P.w(t)
                    if et < 2:
                        t.wait_ge(P.sev.h, P.sev.n)
                    t.wait_ge(s_rw[wb].h, s_rw[wb].n)
                    t.wait_ge(s_rmv.h, s_rmv.n)
                    if (mc, et - 2) in rec_sc:
                        t.wait_ge(P.ssc.h, rec_sc[(mc, et - 2)])
                    elif (mc - 1, RET - 2 + et) in rec_sc:
                        t.wait_ge(P.ssc.h, rec_sc[(mc - 1, RET - 2 + et)])
                    for half in range(2):
                        bank = P.pbk[bk + half]
                        for k in range(RKT):
                            i2 = t.matmul(bank, wbuf[wb][:, k * 128:(k + 1) * 128],
                                          movs[k // DT][:, (k % DT) * MCH + half * 512:
                                                        (k % DT) * MCH + half * 512 + 512],
                                          start=(k == 0), stop=(k == RKT - 1))
                    P.smm.inc(i2, 1)
                    rec_mm[(mc, et)] = P.smm.n
                    return i2, 1

                def sc_w(s, mc=mc, et=et, bk=bk):
                    s.wait_ge(P.smm.h, rec_mm[(mc, et)])
                    for half in range(2):
                        i2 = s.activation(out=hmid[:, et * MCH + half * 512: et * MCH + half * 512 + 512],
                                          in_=P.pbk[bk + half], func=AF.Gelu,
                                          bias=rb1_c[:, et:et + 1])
                    P.ssc.inc(i2, 1)
                    rec_sc[(mc, et)] = P.ssc.n
                    return i2, 1

                def sy_hm(sync, mc=mc, et=et):
                    P.w(sync)
                    hdh = P.d("dbg_hm", [2048, L], BF16)
                    i2 = sync.dma_start(out=hdh[et * 128:(et + 1) * 128, mc * MCH:(mc + 1) * MCH],
                                        in_=hmid[:, et * MCH:(et + 1) * MCH])
                    return P.sld.inc(i2, 16), 16
                stp = [("sync", sy_w), ("tensor", tn_w), ("scalar", sc_w)]
                if DBG_RMS:
                    stp.append(("sync", sy_hm))
                P.seq(f"rt_e{mc}_{et}", stp)

            for half in range(2):
                def gp_pad(g, mc=mc, half=half):
                    P.w(g)
                    return P.sgp.inc(g.memset(lg[:, half * 512:(half + 1) * 512], -1e30), 1), 1

                def tn_2(t, mc=mc, half=half):
                    P.w(t)
                    for k in range(RET):
                        i2 = t.matmul(P.pbk[2][0:20, :], rw2[:, k * 20:(k + 1) * 20],
                                      hmid[:, k * MCH + half * 512: k * MCH + half * 512 + 512],
                                      start=(k == 0), stop=(k == RET - 1))
                    return P.smm.inc(i2, 1), 1

                def sc_2(v, mc=mc, half=half):
                    P.w(v)
                    sl = lg[0:20, half * 512:(half + 1) * 512]
                    v.tensor_scalar(out=sl, in0=P.pbk[2][0:20, :], scalar1=rb2_c,
                                    scalar2=None, op0=ALU.add)
                    i2 = v.tensor_scalar(out=sl, in0=sl, scalar1=ti20,
                                         scalar2=None, op0=ALU.mult)
                    return P.sev.inc(i2, 1), 1

                def sy_lg(sync, mc=mc, half=half):
                    P.w(sync)
                    hdl = P.d("dbg_lg", [20, L])
                    i2 = sync.dma_start(out=hdl[:, mc * MCH + half * 512: mc * MCH + half * 512 + 512],
                                        in_=lg[0:20, half * 512:(half + 1) * 512])
                    return P.sld.inc(i2, 16), 16

                def ve_2(v, mc=mc, half=half):
                    P.w(v)
                    i2 = v.transpose(out=tm[:, half * 512:(half + 1) * 512],
                                     in_=lg[:, half * 512:(half + 1) * 512])
                    return P.sev.inc(i2, 1), 1

                def ve_3(v, mc=mc, half=half):
                    P.w(v)
                    NG = 512 // 32
                    tv = tm[:, half * 512:(half + 1) * 512].rearrange(
                        "p (g r) -> p g r", r=32)[:, :, 0:20].rearrange(
                        "p g (hh f) -> p g hh f", f=5)
                    mv = mx[:, half * 64:(half + 1) * 64].rearrange(
                        "p (g hh) -> p g hh", hh=4)
                    v.tensor_reduce(out=mv, in_=tv, axis=AX.X, op=ALU.max)
                    i2 = v.tensor_tensor(out=tv, in0=tv,
                                         in1=mx[:, half * 64:(half + 1) * 64].rearrange(
                                             "p (g hh) -> p g hh", hh=4).broadcast_to([32, NG, 4, 5]),
                                         op=ALU.subtract)
                    return P.sev.inc(i2, 1), 1

                def sc_3(s, mc=mc, half=half):
                    P.w(s)
                    NG = 512 // 32
                    tv = tm[:, half * 512:(half + 1) * 512].rearrange(
                        "p (g r) -> p g r", r=32)[:, :, 0:20]
                    ev = ex[:, half * 512:(half + 1) * 512].rearrange(
                        "p (g r) -> p g r", r=32)[:, :, 0:20]
                    i2 = s.activation(out=ev, in_=tv, func=AF.Exp)
                    return P.ssc.inc(i2, 1), 1

                def ve_4(v, mc=mc, half=half):
                    P.w(v)
                    NG = 512 // 32
                    evh = ex[:, half * 512:(half + 1) * 512].rearrange(
                        "p (g r) -> p g r", r=32)[:, :, 0:20].rearrange(
                        "p g (hh f) -> p g hh f", f=5)
                    mv = mx[:, half * 64:(half + 1) * 64].rearrange(
                        "p (g hh) -> p g hh", hh=4)
                    v.tensor_reduce(out=mv, in_=evh, axis=AX.X, op=ALU.add)
                    v.reciprocal(out=mv, in_=mv)
                    v.tensor_tensor(out=evh, in0=evh,
                                    in1=mx[:, half * 64:(half + 1) * 64].rearrange(
                                        "p (g hh) -> p g hh", hh=4).broadcast_to([32, NG, 4, 5]),
                                    op=ALU.mult)
                    i2 = v.transpose(out=lg[:, half * 512:(half + 1) * 512],
                                     in_=ex[:, half * 512:(half + 1) * 512])
                    return P.sev.inc(i2, 1), 1

                def sy_p(sync, mc=mc, half=half):
                    P.w(sync)
                    i2 = sync.dma_start(out=hprob[:, mc * MCH + half * 512: mc * MCH + half * 512 + 512],
                                        in_=lg[0:20, half * 512:(half + 1) * 512])
                    return P.sld.inc(i2, 16), 16
                steps_sm = [("gpsimd", gp_pad), ("tensor", tn_2), ("vector", sc_2)]
                if DBG_RMS:
                    steps_sm.append(("sync", sy_lg))
                steps_sm += [("vector", ve_2), ("vector", ve_3), ("scalar", sc_3),
                             ("vector", ve_4), ("sync", sy_p)]
                P.seq(f"rt_sm{mc}_{half}", steps_sm)
        fin(P, "router")


def ph_l1_ctx(P):
    paths = [f"fir{K}_f" for K in (3, 7, 15, 31)] + ["delta_f"]
    hvs = P.d("vs_f", [D, L])
    hprob = P.d("probs_f", [20, L])
    hpadj = P.d("padj_f", [H, L])
    hs1 = P.d("s1_f", [D, L])
    with contextlib.ExitStack() as es:
        wo = load_wpack(P, es, "wo", "wo", DT, DT)
        pf = [P.sb(es, f"cx_p{i}", [128, DT * CH], F32) for i in range(5)]
        vsf = P.sb(es, "cx_v", [128, DT * CH], F32)
        prow = P.sb(es, "cx_pr", [1, 24 * CH], F32)
        reps = P.sb(es, "cx_rep", [128, 6 * CH], F32)
        ctx = P.sb(es, "cx_ctx", [128, 2 * CH], F32)
        tmp = P.sb(es, "cx_t", [128, 2 * CH], F32)
        sqb = P.sb(es, "cx_sq", [128, 2 * CH], BF16)
        rrep = P.sb(es, "cx_rr", [128, CH], F32)
        onb = P.sb(es, "cx_ob", [128, DT * CH], BF16)
        onw_c = colvec(P, es, "onw", 256)

        def ve_ow(v):
            P.w(v)
            return P.sev.inc(v.tensor_scalar_mul(out=onw_c, in0=onw_c, scalar1=16.0), 1), 1
        P.seq("cx_ow16", [("vector", ve_ow)])

        for c in range(NCH):
            def sy(sync, c=c):
                P.w(sync)
                for i, s in enumerate(paths):
                    P.sld.inc(sync.dma_start(out=sview(pf[i]), in_=cview(P.d(s, [D, L]), c)), 16)
                P.sld.inc(sync.dma_start(out=sview(vsf), in_=cview(hvs, c)), 16)
                P.sld.inc(sync.dma_start(
                    out=prow[:, 0:20 * CH].rearrange("o (r n) -> o r n", n=CH),
                    in_=hprob[:, c * CH:(c + 1) * CH].rearrange("r n -> () r n")), 16)
                return P.sld.inc(sync.dma_start(
                    out=prow[:, 20 * CH:24 * CH].rearrange("o (r n) -> o r n", n=CH),
                    in_=hpadj[:, c * CH:(c + 1) * CH].rearrange("r n -> () r n")), 16), 16
            P.seq(f"cx_ld{c}", [("sync", sy)])

            for hh in range(H):
                def tn_r(t, c=c, hh=hh):
                    P.w(t)
                    for i in range(5):
                        r = 5 * hh + i
                        t.matmul(P.pbk[i], P.ones_f1, prow[:, r * CH:(r + 1) * CH],
                                 start=True, stop=True)
                    r = 20 + hh
                    i2 = t.matmul(P.pbk[5], P.ones_f1, prow[:, r * CH:(r + 1) * CH],
                                  start=True, stop=True)
                    return P.smm.inc(i2, 1), 1

                def ve_r(v, c=c, hh=hh):
                    P.w(v)
                    for i in range(6):
                        i2 = v.tensor_copy(out=ct(reps, i), in_=P.pbk[i])
                    return P.sev.inc(i2, 1), 1
                P.seq(f"cx_rep{c}_{hh}", [("tensor", tn_r), ("vector", ve_r)])

                def ve_c(v, c=c, hh=hh):
                    P.w(v)
                    for tt in range(2):
                        m = hh * 2 + tt
                        cslice = ct(ctx, tt)
                        v.tensor_mul(cslice, ct(reps, 0), ct(pf[0], m))
                        for i in range(1, 5):
                            v.tensor_mul(ct(tmp, tt), ct(reps, i), ct(pf[i], m))
                            v.tensor_add(cslice, cslice, ct(tmp, tt))
                        v.tensor_sub(ct(tmp, tt), ct(vsf, m), cslice)
                        v.tensor_mul(ct(tmp, tt), ct(tmp, tt), ct(reps, 5))
                        i2 = v.tensor_add(cslice, cslice, ct(tmp, tt))
                    return P.sev.inc(i2, 1), 1
                P.seq(f"cx_c{c}_{hh}", [("vector", ve_c)])

                def sc_n(v, c=c, hh=hh):
                    P.w(v)
                    i2 = v.tensor_mul(sqb, ctx, ctx)
                    return P.sev.inc(i2, 1), 1

                def tn_n(t, c=c, hh=hh):
                    P.w(t)
                    for tt in range(2):
                        i2 = t.matmul(P.pbk[6], P.ones_b, ct(sqb, tt),
                                      start=(tt == 0), stop=(tt == 1))
                    return P.smm.inc(i2, 1), 1

                def ve_n(v, c=c, hh=hh):
                    P.w(v)
                    v.tensor_scalar(out=rrep, in0=P.pbk[6], scalar1=256e-5, scalar2=None,
                                    op0=ALU.add)
                    i2 = v.reciprocal(out=rrep, in_=rrep)
                    return P.sev.inc(i2, 1), 1

                def sc_n2(s, c=c, hh=hh):
                    P.w(s)
                    i2 = s.activation(out=rrep, in_=rrep, func=AF.Sqrt)
                    return P.ssc.inc(i2, 1), 1

                def ve_n2(v, c=c, hh=hh):
                    P.w(v)
                    for tt in range(2):
                        i2 = v.tensor_mul(ct(ctx, tt), ct(ctx, tt), rrep)
                    return P.sev.inc(i2, 1), 1

                def sc_n3(v, c=c, hh=hh):
                    P.w(v)
                    for tt in range(2):
                        m = hh * 2 + tt
                        i2 = v.tensor_scalar(out=ct(onb, m), in0=ct(ctx, tt),
                                             scalar1=onw_c[:, tt:tt + 1], scalar2=None,
                                             op0=ALU.mult)
                    return P.sev.inc(i2, 1), 1
                P.seq(f"cx_n{c}_{hh}", [("vector", sc_n), ("tensor", tn_n), ("vector", ve_n),
                                        ("scalar", sc_n2), ("vector", ve_n2), ("vector", sc_n3)])

            for half in range(2):
                def tn_o(t, c=c, half=half):
                    P.w(t)
                    for j in range(4):
                        m = half * 4 + j
                        for k in range(DT):
                            i2 = t.matmul(P.pbk[j], wv(wo, k, m, DT), ct(onb, k),
                                          start=(k == 0), stop=(k == DT - 1))
                    return P.smm.inc(i2, 1), 1

                def ve_o(v, c=c, half=half):
                    P.w(v)
                    for j in range(4):
                        m = half * 4 + j
                        i2 = v.tensor_copy(out=ct(vsf, m), in_=P.pbk[j])
                    return P.sev.inc(i2, 1), 1
                P.seq(f"cx_wo{c}_{half}", [("tensor", tn_o), ("vector", ve_o)])

            def sy2(sync, c=c):
                P.w(sync)
                return P.sld.inc(sync.dma_start(out=cview(hs1, c), in_=sview(vsf)), 16), 16
            P.seq(f"cx_st{c}", [("sync", sy2)])
        fin(P, "ctx")


def ph_l1_gate(P):
    hx1 = P.d("xn1_f", [D, L])
    hm = P.d("masks_f", [EXP, L])
    hmb = P.d("masks_b", [EXP, L], BF16)
    with contextlib.ExitStack() as es:
        gws = P.sb(es, "gt_w", [128, DT * EXP], F32)
        gb_c = colvec(P, es, "gb", EXP)
        mov = P.sb(es, "gt_mov", [128, DT * CH], F32)
        lg = P.sb(es, "gt_lg", [32, CH], F32)
        tm = P.sb(es, "gt_tm", [32, CH], F32)
        mx = P.sb(es, "gt_mx", [32, CH // 32], F32)
        nd = P.sb(es, "gt_nd", [32, CH // 32], F32)
        meq = P.sb(es, "gt_meq", [32, CH // 32], F32)
        mtm = P.sb(es, "gt_mtm", [32, CH], F32)
        mfm = P.sb(es, "gt_mfm", [32, CH], F32)
        mbf = P.sb(es, "gt_mbf", [EXP, CH], BF16)

        def gp_w(g):
            P.w(g)
            hv = P.d("gwT", [D, EXP], F32)
            i2 = g.dma_start(out=gws.rearrange("p (k m) -> p k m", m=EXP),
                             in_=hv.rearrange("(k p) m -> p k m", p=128))
            return P.sgp.inc(i2, 16), 16
        P.seq("gt_w", [("gpsimd", gp_w)])

        for c in range(NCH):
            def sy(sync, c=c):
                P.w(sync)
                return P.sld.inc(sync.dma_start(out=sview(mov), in_=cview(hx1, c)), 16), 16

            def gp_p(g, c=c):
                P.w(g)
                g.memset(lg, -1e30)
                return P.sgp.inc(g.memset(mtm, 0.0), 1), 1

            def tn(t, c=c):
                P.w(t)
                for k in range(DT):
                    i2 = t.matmul(P.pbk[0][0:EXP, :], gws[:, k * EXP:(k + 1) * EXP],
                                  ct(mov, k), start=(k == 0), stop=(k == DT - 1))
                return P.smm.inc(i2, 1), 1

            def sc(v, c=c):
                P.w(v)
                i2 = v.tensor_scalar(out=lg[0:EXP, :], in0=P.pbk[0][0:EXP, :],
                                     scalar1=gb_c, scalar2=None, op0=ALU.add)
                return P.sev.inc(i2, 1), 1

            def ve1(v, c=c):
                P.w(v)
                NG = CH // 32
                v.transpose(out=tm, in_=lg)
                tv8 = tm.rearrange("p (g r) -> p g r", r=32)[:, :, 0:EXP]
                v.tensor_reduce(out=mx, in_=tv8, axis=AX.X, op=ALU.max)
                tv = tm.rearrange("p (g r) -> p g r", r=32)
                mxb = mx.rearrange("p (g o) -> p g o", o=1).broadcast_to([32, NG, 32])
                v.tensor_tensor(out=mtm.rearrange("p (g r) -> p g r", r=32),
                                in0=mxb, in1=tv, op=ALU.subtract)
                v.tensor_scalar(out=mtm, in0=mtm, scalar1=0.0, scalar2=None, op0=ALU.is_gt)
                v.tensor_scalar(out=mtm, in0=mtm, scalar1=-1.0, scalar2=1.0,
                                op0=ALU.mult, op1=ALU.add)
                i2 = v.transpose(out=mfm, in_=mtm)
                return P.sev.inc(i2, 1), 1

            def sc2(s, c=c):
                P.w(s)
                i2 = s.activation(out=mbf, in_=mfm[0:EXP, :], func=AF.Copy)
                return P.ssc.inc(i2, 1), 1

            def sy2(sync, c=c):
                P.w(sync)
                P.sld.inc(sync.dma_start(out=hm[:, c * CH:(c + 1) * CH], in_=mfm[0:EXP, :]), 16)
                return P.sld.inc(sync.dma_start(out=hmb[:, c * CH:(c + 1) * CH], in_=mbf), 16), 16
            P.seq(f"gt_{c}", [("sync", sy), ("gpsimd", gp_p), ("tensor", tn), ("vector", sc),
                              ("vector", ve1), ("scalar", sc2), ("sync", sy2)])
        fin(P, "gate")


def ph_l1_moe(P):
    hx1b = P.d("xn1_b", [D, L], BF16)
    hsp = P.d("hsp_b", [HM, L], BF16)
    hm = P.d("masks_f", [EXP, L])
    hmb = P.d("masks_b", [EXP, L], BF16)
    hg = P.d("g_f", [D, L])
    hw1 = P.d("mw1T", [EXP, D, HM], F32)
    hw2 = P.d("mw2T", [EXP, HM, D], F32)
    hb1 = P.d("b1c", [EXP, 128, HMT], F32)
    hb2 = P.d("b2e", [EXP, D], F32)
    with contextlib.ExitStack() as es:
        mov = P.sb(es, "mo_x", [128, DT * CH], BF16)
        wgrp = P.sb(es, "mo_w", [128, DT * 8 * 128], BF16)
        hch = P.sb(es, "mo_h", [128, 8 * CH], BF16)
        b1t = P.sb(es, "mo_b1", [128, HMT], F32)
        b2t = P.sb(es, "mo_b2", [1, D], BF16)
        mrow = P.sb(es, "mo_mr", [1, CH], BF16)
        mrowf = P.sb(es, "mo_mf", [1, CH], F32)
        mrep = P.sb(es, "mo_rep", [128, CH], F32)
        yacc = P.sb(es, "mo_y", [128, DT * CH], F32)
        w2f = P.sb(es, "mo_w2", [128, HMT * DT * 128], BF16)
        h2 = P.sb(es, "mo_h2", [128, HMT * CH], BF16)

        for e in range(EXP):
            def gp_b(g, e=e):
                P.w(g)
                P.sgp.inc(g.dma_start(out=b1t, in_=hb1[e]), 16)
                i2 = g.dma_start(out=b2t, in_=hb2[e:e + 1, :])
                return P.sgp.inc(i2, 16), 16
            P.seq(f"mo_b{e}", [("gpsimd", gp_b)])

            for mg in range(4):
                def gp_w(g, e=e, mg=mg):
                    P.w(g)
                    sl = hw1[e][:, mg * 1024:(mg + 1) * 1024]
                    i2 = g.dma_start(out=wgrp.rearrange("p (k m q) -> p k m q", m=8, q=128),
                                     in_=sl.rearrange("(k p) (m q) -> p k m q", p=128, q=128))
                    return P.sgp.inc(i2, 16), 16
                P.seq(f"mo_w{e}_{mg}", [("gpsimd", gp_w)])

                for c in range(NCH):
                    def sy_x(sync, e=e, mg=mg, c=c):
                        P.w(sync)
                        return P.sld.inc(sync.dma_start(out=sview(mov), in_=cview(hx1b, c)), 16), 16

                    def tn_a(t, e=e, mg=mg, c=c):
                        P.w(t)
                        for j in range(4):
                            for k in range(DT):
                                i2 = t.matmul(P.pbk[j], wv(wgrp, k, j, 8), ct(mov, k),
                                              start=(k == 0), stop=(k == DT - 1))
                        return P.smm.inc(i2, 1), 1

                    def sc_a(s, e=e, mg=mg, c=c):
                        P.w(s)
                        for j in range(4):
                            i2 = s.activation(out=ct(hch, j), in_=P.pbk[j], func=AF.Relu,
                                              bias=b1t[:, mg * 8 + j: mg * 8 + j + 1])
                        return P.ssc.inc(i2, 1), 1

                    def tn_b(t, e=e, mg=mg, c=c):
                        P.w(t)
                        for j in range(4, 8):
                            for k in range(DT):
                                i2 = t.matmul(P.pbk[j - 4], wv(wgrp, k, j, 8), ct(mov, k),
                                              start=(k == 0), stop=(k == DT - 1))
                        return P.smm.inc(i2, 1), 1

                    def sc_b(s, e=e, mg=mg, c=c):
                        P.w(s)
                        for j in range(4, 8):
                            i2 = s.activation(out=ct(hch, j), in_=P.pbk[j - 4], func=AF.Relu,
                                              bias=b1t[:, mg * 8 + j: mg * 8 + j + 1])
                        return P.ssc.inc(i2, 1), 1

                    def sy_h(sync, e=e, mg=mg, c=c):
                        P.w(sync)
                        dst = hsp[mg * 1024:(mg + 1) * 1024, c * CH:(c + 1) * CH]
                        i2 = sync.dma_start(out=dst.rearrange("(k p) n -> p k n", p=128),
                                            in_=sview(hch))
                        return P.sld.inc(i2, 16), 16
                    P.seq(f"mo_m1_{e}_{mg}_{c}", [("sync", sy_x), ("tensor", tn_a), ("scalar", sc_a),
                                                  ("tensor", tn_b), ("scalar", sc_b),
                                                  ("sync", sy_h)])

            def gp_w2(g, e=e):
                P.w(g)
                i2 = g.dma_start(out=w2f.rearrange("p (k m q) -> p k m q", m=DT, q=128),
                                 in_=hw2[e].rearrange("(k p) (m q) -> p k m q", p=128, q=128))
                return P.sgp.inc(i2, 16), 16
            P.seq(f"mo_w2_{e}", [("gpsimd", gp_w2)])

            for c in range(NCH):
                def sy_m(sync, e=e, c=c):
                    P.w(sync)
                    P.sld.inc(sync.dma_start(out=mrow, in_=hmb[e:e + 1, c * CH:(c + 1) * CH]), 16)
                    P.sld.inc(sync.dma_start(out=mrowf, in_=hm[e:e + 1, c * CH:(c + 1) * CH]), 16)
                    sl = hsp[:, c * CH:(c + 1) * CH]
                    return P.sld.inc(sync.dma_start(out=sview(h2),
                                                    in_=sl.rearrange("(k p) n -> p k n", p=128)), 16), 16

                def tn_r(t, e=e, c=c):
                    P.w(t)
                    i2 = t.matmul(P.pbk[4], P.ones_f1, mrowf, start=True, stop=True)
                    return P.smm.inc(i2, 1), 1

                def ve_r(v, e=e, c=c):
                    P.w(v)
                    i2 = v.tensor_copy(out=mrep, in_=P.pbk[4])
                    return P.sev.inc(i2, 1), 1
                P.seq(f"mo_mr_{e}_{c}", [("sync", sy_m), ("tensor", tn_r), ("vector", ve_r)])

                for half in range(2):
                    def tn_2(t, e=e, c=c, half=half):
                        P.w(t)
                        for j in range(4):
                            m = half * 4 + j
                            for k in range(HMT):
                                t.matmul(P.pbk[j], wv(w2f, k, m, DT), ct(h2, k),
                                         start=(k == 0), stop=False)
                            i2 = t.matmul(P.pbk[j], b2t[:, m * 128:(m + 1) * 128], mrow,
                                          start=False, stop=True)
                        return P.smm.inc(i2, 1), 1

                    def ve_y(v, e=e, c=c, half=half):
                        P.w(v)
                        for j in range(4):
                            m = half * 4 + j
                            i2 = v.tensor_mul(ct(yacc, m), P.pbk[j], mrep)
                        return P.sev.inc(i2, 1), 1
                    P.seq(f"mo_m2_{e}_{c}_{half}", [("tensor", tn_2), ("vector", ve_y)])

                def gp_acc(g, e=e, c=c):
                    P.w(g)
                    kw = {} if e == 0 else {"accum_op": ALU.add}
                    i2 = g.dma_start(out=cview(hg, c), in_=sview(yacc), **kw)
                    return P.sgp.inc(i2, 16), 16
                P.seq(f"mo_y_{e}_{c}", [("gpsimd", gp_acc)])
        fin(P, "moe")


def ph_final(P):
    hx1 = P.d("x1_f", [D, L])
    hs1 = P.d("s1_f", [D, L])
    hg = P.d("g_f", [D, L])
    hout = P.d("out", [D, L])
    with contextlib.ExitStack() as es:
        a = P.sb(es, "fn_a", [128, DT * CH], F32)
        b = P.sb(es, "fn_b", [128, DT * CH], F32)
        g2 = P.sb(es, "fn_g", [128, DT * CH], F32)
        for c in range(NCH):
            def sy(sync, c=c):
                P.w(sync)
                P.sld.inc(sync.dma_start(out=sview(a), in_=cview(hx1, c)), 16)
                P.sld.inc(sync.dma_start(out=sview(b), in_=cview(hs1, c)), 16)
                return P.sld.inc(sync.dma_start(out=sview(g2), in_=cview(hg, c)), 16), 16

            def sc(s, c=c):
                P.w(s)
                return P.ssc.inc(s.activation(out=sview(g2), in_=sview(g2), func=AF.Silu), 1), 1

            def ve(v, c=c):
                P.w(v)
                for k in range(DT):
                    v.tensor_mul(ct(b, k), ct(b, k), ct(g2, k))
                    i2 = v.tensor_add(ct(a, k), ct(a, k), ct(b, k))
                return P.sev.inc(i2, 1), 1

            def sy2(sync, c=c):
                P.w(sync)
                return P.sld.inc(sync.dma_start(out=cview(hout, c), in_=sview(a)), 16), 16
            P.seq(f"fn_{c}", [("sync", sy), ("scalar", sc), ("vector", ve), ("sync", sy2)])
        fin(P, "final")



def ph_l1_moe2(P):
    hx1b = P.d("xn1_b", [D, L], BF16)
    hsp = P.d("hsp_b", [HM, L], BF16)
    hm = P.d("masks_f", [EXP, L])
    hmb = P.d("masks_b", [EXP, L], BF16)
    hg = P.d("g_f", [D, L])
    hw1 = P.d("mw1T", [EXP, D, HM], F32)
    hw2 = P.d("mw2T", [EXP, HM, D], F32)
    hb1 = P.d("b1c", [EXP, 128, HMT], F32)
    hb2 = P.d("b2e", [EXP, D], F32)
    with contextlib.ExitStack() as es:
        mov = [P.sb(es, f"m2_x{i}", [128, DT * CH], BF16) for i in range(2)]
        wgrp = [P.sb(es, f"m2_w{i}", [128, DT * 8 * 128], BF16) for i in range(2)]
        hch = [P.sb(es, f"m2_h{i}", [128, 8 * CH], BF16) for i in range(2)]
        h2s = [P.sb(es, f"m2_g{i}", [128, 8 * CH], BF16) for i in range(2)]
        yacc = [P.sb(es, f"m2_y{i}", [128, DT * CH], F32) for i in range(2)]
        b1t = P.sb(es, "m2_b1", [128, HMT], F32)
        b2t = P.sb(es, "m2_b2", [1, D], BF16)
        mrow = P.sb(es, "m2_mr", [1, CH], BF16)
        mrowf = P.sb(es, "m2_mf", [1, CH], F32)
        mrep = P.sb(es, "m2_rep", [128, CH], F32)
        w2f = P.sb(es, "m2_w2", [128, HMT * DT * 128], BF16)
        s_mv = [P.newsem(f"mo_mv{i}") for i in range(2)]
        s_wg = [P.newsem(f"mo_wg{i}") for i in range(2)]
        s_hst = [P.newsem(f"mo_hs{i}") for i in range(2)]
        s_h2 = [P.newsem(f"mo_h2{i}") for i in range(2)]
        s_w2 = P.newsem("mo_w2")
        s_msk = P.newsem("mo_mk")
        s_acc = P.newsem("mo_ac")
        s_b = P.newsem("mo_b")

        rec = {"sc": {}, "mm1": {}, "mm2": {}, "vy": {}, "mmlast": 0}

        def m1_iter(e, it, mg, c):
            mb, wb, hb = it % 2, mg % 2, it % 2

            def sy_x(sync):
                if (e, it - 2) in rec["mm1"]:
                    sync.wait_ge(P.smm.h, rec["mm1"][(e, it - 2)])
                elif e > 0 and it < 2:
                    sync.wait_ge(P.smm.h, rec["mm1"][(e - 1, 30 + it)])
                return P.sld.inc(sync.dma_start(out=sview(mov[mb]), in_=cview(hx1b, c)),
                                 0) and None or (s_mv[mb].inc(
                    sync.dma_start(out=sview(mov[mb]), in_=cview(hx1b, c)), 16), 16)
            raise RuntimeError

        mm1 = {}
        scA = {}
        scB = {}
        hst = {}
        vy = {}
        mm2 = {}
        for e in range(EXP):
            def gp_b(g, e=e):
                P.w(g)
                s_b.inc(g.dma_start(out=b1t, in_=hb1[e]), 16)
                return s_b.inc(g.dma_start(out=b2t, in_=hb2[e:e + 1, :]), 16), 16
            P.seq(f"m2b{e}", [("gpsimd", gp_b)])

            for mg in range(4):
                def gp_w(g, e=e, mg=mg):
                    wb = mg % 2
                    if (e, (mg - 2) * 8 + 7) in mm1:
                        g.wait_ge(P.smm.h, mm1[(e, (mg - 2) * 8 + 7)])
                    elif e > 0:
                        g.wait_ge(P.smm.h, mm1[(e - 1, (mg + 2) * 8 + 7)])
                    sl = hw1[e][:, mg * 1024:(mg + 1) * 1024]
                    i2 = g.dma_start(out=wgrp[wb].rearrange("p (k m q) -> p k m q", m=8, q=128),
                                     in_=sl.rearrange("(k p) (m q) -> p k m q", p=128, q=128))
                    s_wg[wb].inc(i2, 16)
                    if True:
                        qsl = hw2[e][mg * 1024:(mg + 1) * 1024, :]
                        if mg == 0 and e > 0:
                            g.wait_ge(P.smm.h, rec["mmlast"])
                        i2 = g.dma_start(
                            out=w2f.rearrange("p (k m q) -> p k m q", m=DT, q=128)[:, mg * 8:(mg + 1) * 8],
                            in_=qsl.rearrange("(k p) (m q) -> p k m q", p=128, q=128))
                        s_w2.inc(i2, 16)
                    return i2, 16
                P.seq(f"m2w{e}_{mg}", [("gpsimd", gp_w)])

                for c in range(NCH):
                    it = mg * 8 + c
                    mb, wb, hb = it % 2, mg % 2, it % 2

                    def sy_x(sync, e=e, it=it, c=c, mb=mb):
                        if (e, it - 2) in mm1:
                            sync.wait_ge(P.smm.h, mm1[(e, it - 2)])
                        elif e > 0:
                            sync.wait_ge(P.smm.h, mm1[(e - 1, 30 + (it % 2))])
                        i2 = sync.dma_start(out=sview(mov[mb]), in_=cview(hx1b, c))
                        return s_mv[mb].inc(i2, 16), 16

                    def tn_a(t, e=e, it=it, mb=mb, wb=wb):
                        if e == 0 and it == 0:
                            P.w(t)
                        t.wait_ge(s_mv[mb].h, s_mv[mb].n)
                        t.wait_ge(s_wg[wb].h, s_wg[wb].n)
                        if (e, it - 1) in scA:
                            t.wait_ge(P.ssc.h, scA[(e, it - 1)])
                        elif e > 0:
                            t.wait_ge(P.ssc.h, scA[(e - 1, 31)])
                        for j in range(4):
                            for k in range(DT):
                                i2 = t.matmul(P.pbk[j], wv(wgrp[wb], k, j, 8), ct(mov[mb], k),
                                              start=(k == 0), stop=(k == DT - 1))
                        return P.smm.inc(i2, 1), 1

                    def sc_a(s, e=e, it=it, hb=hb):
                        s.wait_ge(P.smm.h, P.smm.n)
                        s.wait_ge(s_b.h, s_b.n)
                        sl = s_hst[hb]
                        s.wait_ge(sl.h, sl.n)
                        for j in range(4):
                            i2 = s.activation(out=ct(hch[hb], j), in_=P.pbk[j], func=AF.Relu,
                                              bias=b1t[:, (it // 8) * 8 + j: (it // 8) * 8 + j + 1])
                        P.ssc.inc(i2, 1)
                        scA[(e, it)] = P.ssc.n
                        return i2, 1

                    def tn_b(t, e=e, it=it, mb=mb, wb=wb):
                        if (e, it - 1) in scB:
                            t.wait_ge(P.sev.h, scB[(e, it - 1)])
                        elif e > 0:
                            t.wait_ge(P.sev.h, scB[(e - 1, 31)])
                        for j in range(4, 8):
                            for k in range(DT):
                                i2 = t.matmul(P.pbk[j], wv(wgrp[wb], k, j, 8), ct(mov[mb], k),
                                              start=(k == 0), stop=(k == DT - 1))
                        P.smm.inc(i2, 1)
                        mm1[(e, it)] = P.smm.n
                        return i2, 1

                    def sc_b(v, e=e, it=it, hb=hb):
                        v.wait_ge(P.smm.h, P.smm.n)
                        v.wait_ge(s_b.h, s_b.n)
                        for j in range(4, 8):
                            i2 = v.tensor_scalar(out=ct(hch[hb], j), in0=P.pbk[j],
                                                 scalar1=b1t[:, (it // 8) * 8 + j: (it // 8) * 8 + j + 1],
                                                 scalar2=0.0, op0=ALU.add, op1=ALU.max)
                        P.sev.inc(i2, 1)
                        scB[(e, it)] = P.sev.n
                        return i2, 1

                    def sy_h(sync, e=e, it=it, c=c, mg=mg, hb=hb):
                        sync.wait_ge(P.ssc.h, scA[(e, it)])
                        sync.wait_ge(P.sev.h, scB[(e, it)])
                        dst = hsp[mg * 1024:(mg + 1) * 1024, c * CH:(c + 1) * CH]
                        i2 = sync.dma_start(out=dst.rearrange("(k p) n -> p k n", p=128),
                                            in_=sview(hch[hb]))
                        s_hst[hb].inc(i2, 16)
                        hst[(e, it)] = s_hst[hb].n
                        return i2, 16
                    P.seq(f"m2m1_{e}_{it}", [("sync", sy_x), ("tensor", tn_a), ("scalar", sc_a),
                                             ("tensor", tn_b), ("vector", sc_b), ("sync", sy_h)])

            for c in range(NCH):
                yb = c % 2

                def sy_m(sync, e=e, c=c):
                    if c == 0:
                        sync.wait_ge(s_hst[0].h, s_hst[0].n)
                        sync.wait_ge(s_hst[1].h, s_hst[1].n)
                        sync.wait_ge(s_w2.h, s_w2.n)
                    P.sld.inc(sync.dma_start(out=mrow, in_=hmb[e:e + 1, c * CH:(c + 1) * CH]), 16)
                    i2 = sync.dma_start(out=mrowf, in_=hm[e:e + 1, c * CH:(c + 1) * CH])
                    return s_msk.inc(i2, 16), 16

                def tn_r(t, e=e, c=c):
                    t.wait_ge(s_msk.h, s_msk.n)
                    t.wait_ge(s_b.h, s_b.n)
                    if c == 0:
                        t.wait_ge(P.ssc.h, P.ssc.n)
                    if (e, c - 1) in vy:
                        t.wait_ge(P.sev.h, vy[(e, c - 1)])
                    elif e > 0:
                        t.wait_ge(P.sev.h, vy[(e - 1, 7)])
                    i2 = t.matmul(P.pbk[0], P.ones_f1, mrowf, start=True, stop=True)
                    return P.smm.inc(i2, 1), 1

                def ve_r(v, e=e, c=c):
                    v.wait_ge(P.smm.h, P.smm.n)
                    i2 = v.tensor_copy(out=mrep, in_=P.pbk[0])
                    return P.sev.inc(i2, 1), 1
                P.seq(f"m2mr_{e}_{c}", [("sync", sy_m), ("tensor", tn_r), ("vector", ve_r)])

                for kg in range(4):
                    gb2 = kg % 2

                    def sy_g(sync, e=e, c=c, kg=kg, gb2=gb2):
                        key = (e, c, kg - 2) if kg >= 2 else (e, c - 1, kg + 2)
                        if key in mm2:
                            sync.wait_ge(P.smm.h, mm2[key])
                        elif e > 0:
                            sync.wait_ge(P.smm.h, mm2[(e - 1, 7, kg)])
                        sl = hsp[kg * 1024:(kg + 1) * 1024, c * CH:(c + 1) * CH]
                        i2 = sync.dma_start(out=sview(h2s[gb2]),
                                            in_=sl.rearrange("(k p) n -> p k n", p=128))
                        return s_h2[gb2].inc(i2, 16), 16

                    def tn_2(t, e=e, c=c, kg=kg, gb2=gb2):
                        t.wait_ge(s_h2[gb2].h, s_h2[gb2].n)
                        if kg == 0:
                            t.wait_ge(P.sev.h, P.sev.n)
                        for m in range(DT):
                            for k in range(8):
                                i2 = t.matmul(P.pbk[m], wv(w2f, kg * 8 + k, m, DT), ct(h2s[gb2], k),
                                              start=(kg == 0 and k == 0), stop=False)
                        P.smm.inc(i2, 1)
                        mm2[(e, c, kg)] = P.smm.n
                        return i2, 1
                    P.seq(f"m2m2_{e}_{c}_{kg}", [("sync", sy_g), ("tensor", tn_2)])

                def tn_bias(t, e=e, c=c):
                    for m in range(DT):
                        i2 = t.matmul(P.pbk[m], b2t[:, m * 128:(m + 1) * 128], mrow,
                                      start=False, stop=True)
                    P.smm.inc(i2, 1)
                    rec["mmlast"] = P.smm.n
                    return i2, 1

                def ve_y(v, e=e, c=c, yb=yb):
                    v.wait_ge(P.smm.h, P.smm.n)
                    tgt = max(s_acc.n - 16, 0)
                    if tgt:
                        v.wait_ge(s_acc.h, tgt)
                    for m in range(DT):
                        i2 = v.tensor_mul(ct(yacc[yb], m), P.pbk[m], mrep)
                    P.sev.inc(i2, 1)
                    vy[(e, c)] = P.sev.n
                    return i2, 1

                def gp_acc(g, e=e, c=c, yb=yb):
                    g.wait_ge(P.sev.h, vy[(e, c)])
                    g.wait_ge(s_acc.h, s_acc.n)
                    kw = {} if e == 0 else {"accum_op": ALU.add}
                    i2 = g.dma_start(out=cview(hg, c), in_=sview(yacc[yb]), **kw)
                    return s_acc.inc(i2, 16), 16
                P.seq(f"m2y_{e}_{c}", [("tensor", tn_bias), ("vector", ve_y), ("gpsimd", gp_acc)])
        fin(P, "moe2")



def ph_l1_delta2(P):
    hq = P.d("qs_f", [D, L])
    hk = P.d("ks_f", [D, L])
    hv = P.d("vs_f", [D, L])
    hbeta = P.d("beta_f", [H, L])
    hdelta = P.d("delta_f", [D, L])
    NT = 2
    with contextlib.ExitStack() as es:
        qn = P.sb(es, "d2_qn", [128, NT * L], BF16)
        kn = P.sb(es, "d2_kn", [128, NT * L], BF16)
        vb = P.sb(es, "d2_vb", [128, NT * L], BF16)
        kb = P.sb(es, "d2_kb", [128, NT * L], BF16)
        uwa = P.sb(es, "d2_uw", [128, NCK * 512], BF16)
        atta = P.sb(es, "d2_at", [128, NCK * 128], BF16)
        ktma = P.sb(es, "d2_kt", [128, NCK * 256], BF16)
        wfma = P.sb(es, "d2_wf", [128, NCK * 256], BF16)
        xf = P.sb(es, "d2_xf", [128, NT * CH], F32)
        xf2 = P.sb(es, "d2_xf2", [128, NT * CH], F32)
        sq = P.sb(es, "d2_sq", [128, NT * CH], BF16)
        brow = P.sb(es, "d2_br", [1, L], F32)
        brep = P.sb(es, "d2_bp", [128, CH], F32)
        rrep = P.sb(es, "d2_rr", [128, CH], F32)
        S_f = P.sb(es, "d2_Sf", [128, NT * 256], F32)
        S_b = P.sb(es, "d2_Sb", [128, NT * 256], BF16)
        npw = [P.sb(es, f"d2_np{i}", [128, 2 * 128], BF16) for i in range(2)]
        spw = [P.sb(es, f"d2_sp{i}", [128, 7 * 128], BF16) for i in range(2)]
        xpp = [P.sb(es, f"d2_x{i}", [128, 2 * 512], BF16) for i in range(2)]
        uadj = P.sb(es, "d2_ua", [128, 256], BF16)
        otm = P.sb(es, "d2_ot", [128, 256], F32)
        ofm = P.sb(es, "d2_of", [128, 256], F32)
        idb = P.ident_b
        rA = {}
        rB = {}

        def qc(buf, tt, ck, w=CK):
            return buf[:, tt * L + ck * CK: tt * L + ck * CK + w]

        for h in range(H):
            P.scope = f"dp{h}"
            def syb(sync, h=h):
                P.w(sync)
                return P.sld.inc(sync.dma_start(out=brow, in_=hbeta[h:h + 1, :]), 16), 16
            P.seq(f"d2b{h}", [("sync", syb)])

            for c in range(NCH):
                def sy0(sync, h=h, c=c):
                    P.w(sync)
                    src = hq[h * 256:(h + 1) * 256, c * CH:(c + 1) * CH]
                    P.sld.inc(sync.dma_start(out=xf.rearrange("p (t n) -> p t n", n=CH),
                                             in_=src.rearrange("(t p) n -> p t n", p=128)), 16)
                    src2 = hk[h * 256:(h + 1) * 256, c * CH:(c + 1) * CH]
                    return P.sld.inc(sync.dma_start(
                        out=xf2.rearrange("p (t n) -> p t n", n=CH),
                        in_=src2.rearrange("(t p) n -> p t n", p=128)), 16), 16

                def tn_r(t, h=h, c=c):
                    P.w(t)
                    i2 = t.matmul(P.pbk[2], P.ones_f1, brow[:, c * CH:(c + 1) * CH],
                                  start=True, stop=True)
                    return P.smm.inc(i2, 1), 1

                def sc_q(v, h=h, c=c):
                    P.w(v)
                    i2 = v.tensor_mul(sview(sq, CH), sview(xf, CH), sview(xf, CH))
                    return P.sev.inc(i2, 1), 1

                def tn_q(t, h=h, c=c):
                    P.w(t)
                    for tt in range(NT):
                        i2 = t.matmul(P.pbk[0], P.ones_b, ct(sq, tt, CH),
                                      start=(tt == 0), stop=(tt == NT - 1))
                    return P.smm.inc(i2, 1), 1

                def ve_q(v, h=h, c=c):
                    P.w(v)
                    v.tensor_copy(out=brep, in_=P.pbk[2])
                    v.tensor_scalar(out=rrep, in0=P.pbk[0], scalar1=1e-6, scalar2=None,
                                    op0=ALU.add)
                    i2 = v.reciprocal(out=rrep, in_=rrep)
                    return P.sev.inc(i2, 1), 1

                def sc_q2(s, h=h, c=c):
                    P.w(s)
                    i2 = s.activation(out=rrep, in_=rrep, func=AF.Sqrt)
                    return P.ssc.inc(i2, 1), 1

                def ve_q3(v, h=h, c=c):
                    P.w(v)
                    for tt in range(NT):
                        i2 = v.tensor_mul(qn[:, tt * L + c * CH: tt * L + (c + 1) * CH],
                                          ct(xf, tt, CH), rrep)
                    return P.sev.inc(i2, 1), 1
                P.seq(f"d2q{h}_{c}", [("sync", sy0), ("tensor", tn_r), ("vector", sc_q),
                                      ("tensor", tn_q), ("vector", ve_q), ("scalar", sc_q2),
                                      ("vector", ve_q3)])

                def sc_k(v, h=h, c=c):
                    P.w(v)
                    i2 = v.tensor_mul(sview(sq, CH), sview(xf2, CH), sview(xf2, CH))
                    return P.sev.inc(i2, 1), 1

                def tn_k(t, h=h, c=c):
                    P.w(t)
                    for tt in range(NT):
                        i2 = t.matmul(P.pbk[0], P.ones_b, ct(sq, tt, CH),
                                      start=(tt == 0), stop=(tt == NT - 1))
                    return P.smm.inc(i2, 1), 1

                def ve_k(v, h=h, c=c):
                    P.w(v)
                    v.tensor_scalar(out=rrep, in0=P.pbk[0], scalar1=1e-6, scalar2=None,
                                    op0=ALU.add)
                    i2 = v.reciprocal(out=rrep, in_=rrep)
                    return P.sev.inc(i2, 1), 1

                def sc_k2(s, h=h, c=c):
                    P.w(s)
                    i2 = s.activation(out=rrep, in_=rrep, func=AF.Sqrt)
                    return P.ssc.inc(i2, 1), 1

                def ve_k3(v, h=h, c=c):
                    P.w(v)
                    for tt in range(NT):
                        v.tensor_mul(ct(xf2, tt, CH), ct(xf2, tt, CH), rrep)
                        v.tensor_copy(out=kn[:, tt * L + c * CH: tt * L + (c + 1) * CH],
                                      in_=ct(xf2, tt, CH))
                        i2 = v.tensor_mul(kb[:, tt * L + c * CH: tt * L + (c + 1) * CH],
                                          ct(xf2, tt, CH), brep)
                    return P.sev.inc(i2, 1), 1
                P.seq(f"d2k{h}_{c}", [("vector", sc_k), ("tensor", tn_k), ("vector", ve_k),
                                      ("scalar", sc_k2), ("vector", ve_k3)])

                def sy_v(sync, h=h, c=c):
                    P.w(sync)
                    src = hv[h * 256:(h + 1) * 256, c * CH:(c + 1) * CH]
                    return P.sld.inc(sync.dma_start(
                        out=xf.rearrange("p (t n) -> p t n", n=CH),
                        in_=src.rearrange("(t p) n -> p t n", p=128)), 16), 16

                def ve_v(v, h=h, c=c):
                    P.w(v)
                    for tt in range(NT):
                        i2 = v.tensor_mul(vb[:, tt * L + c * CH: tt * L + (c + 1) * CH],
                                          ct(xf, tt, CH), brep)
                    return P.sev.inc(i2, 1), 1
                P.seq(f"d2v{h}_{c}", [("sync", sy_v), ("vector", ve_v)])

            P.scope = f"da{h}"
            def passA_seqs(ck):
                seqs = []
                _add = seqs.append
                _emit_one(ck, _add)
                return seqs

            def _emit_pair(h):
                for ck0 in range(0, NCK, 2):
                    sa = passA_seqs(ck0)
                    sb = passA_seqs(ck0 + 1)
                    n = max(len(sa), len(sb))
                    for j in range(n):
                        if j < len(sa):
                            P.seq(*sa[j])
                        if j < len(sb):
                            P.seq(*sb[j])

            def _emit_one(ck, _add):
                par = ck % 2
                bA = P.pbk[0 + par]
                bD = P.pbk[2 + par]
                bDT = P.pbk[2 + par].bitcast(BF16)
                bX = P.pbk[4 + par]

                def tA(t, h=h, ck=ck, par=par, bA=bA):
                    if ck == 0:
                        t.wait_ge(P.sev.h, P.sev.n)
                        t.wait_ge(P.ssc.h, P.ssc.n)
                    if (h, ck - 2, "mask") in rA:
                        t.wait_ge(P.sev.h, rA[(h, ck - 2, "mask")])
                    for tt in range(NT):
                        t.matmul(bA[:, 0:128], qc(kb, tt, ck), qc(kn, tt, ck),
                                 start=(tt == 0), stop=(tt == NT - 1))
                    for tt in range(NT):
                        i2 = t.matmul(bA[:, 128:256], qc(kn, tt, ck), qc(qn, tt, ck),
                                      start=(tt == 0), stop=(tt == NT - 1))
                    P.smm.inc(i2, 1)
                    rA[(h, ck, "A")] = P.smm.n
                    return i2, 1

                def vM(v, h=h, ck=ck, par=par, bA=bA):
                    v.wait_ge(P.smm.h, rA[(h, ck, "A")])
                    if (h, ck - 2, "d6") in rA:
                        v.wait_ge(P.sev.h, rA[(h, ck - 2, "d6")])
                    v.tensor_mul(npw[par][:, 0:128], bA[:, 0:128], P.mask_lo)
                    i2 = v.tensor_mul(atta[:, ck * 128:(ck + 1) * 128], bA[:, 128:256], P.mask_upi)
                    P.sev.inc(i2, 1)
                    rA[(h, ck, "mask")] = P.sev.n
                    return i2, 1
                _add((f"d2A{h}_{ck}", [("tensor", tA), ("vector", vM)]))

                def tT(t, h=h, ck=ck, par=par):
                    t.wait_ge(P.sev.h, rA[(h, ck, "mask")])
                    if (h, ck - 1, "trev") in rA:
                        t.wait_ge(P.sev.h, rA[(h, ck - 1, "trev")])
                    t.transpose(P.pbt[:, 0:128], npw[par][:, 0:128], idb)
                    for tt in range(NT):
                        t.transpose(P.pbt[:, (1 + tt) * 128:(2 + tt) * 128], qc(vb, tt, ck), idb)
                    for tt in range(NT):
                        t.transpose(P.pbt[:, (3 + tt) * 128:(4 + tt) * 128], qc(kb, tt, ck), idb)
                    for tt in range(NT):
                        i2 = t.transpose(P.pbt[:, (5 + tt) * 128:(6 + tt) * 128], qc(kn, tt, ck), idb)
                    P.smm.inc(i2, 1)
                    rA[(h, ck, "tr")] = P.smm.n
                    return i2, 1

                def sT(v, h=h, ck=ck, par=par):
                    v.wait_ge(P.smm.h, rA[(h, ck, "tr")])
                    if (h, ck - 2, "x6") in rA:
                        v.wait_ge(P.sev.h, rA[(h, ck - 2, "x6")])
                    v.tensor_copy(out=spw[par][:, 0:128], in_=P.pbt[:, 0:128])
                    v.tensor_copy(out=xpp[par][:, 0:512], in_=P.pbt[:, 128:640])
                    i2 = v.tensor_copy(out=ktma[:, ck * 256:(ck + 1) * 256],
                                       in_=P.pbt[:, 640:896])
                    P.sev.inc(i2, 1)
                    rA[(h, ck, "trev")] = P.sev.n
                    return i2, 1
                _add((f"d2T{h}_{ck}", [("tensor", tT), ("vector", sT)]))

                for j in range(1, 7):
                    def tD(t, h=h, ck=ck, par=par, j=j, bD=bD):
                        t.wait_ge(P.sev.h, rA[(h, ck, "trev" if j == 1 else f"d{j - 1}")])
                        cn = npw[par][:, ((j - 1) % 2) * 128:((j - 1) % 2 + 1) * 128]
                        sp = spw[par][:, (j - 1) * 128:j * 128]
                        t.matmul(bD[:, 0:128], sp, cn, start=True, stop=True)
                        i2 = t.matmul(bD[:, 128:256], cn, sp, start=True, stop=True)
                        P.smm.inc(i2, 1)
                        rA[(h, ck, f"dm{j}")] = P.smm.n
                        return i2, 1

                    def sD(v, h=h, ck=ck, par=par, j=j, bD=bD):
                        v.wait_ge(P.smm.h, rA[(h, ck, f"dm{j}")])
                        v.tensor_copy(out=npw[par][:, (j % 2) * 128:(j % 2 + 1) * 128],
                                      in_=bD[:, 0:128])
                        i2 = v.tensor_copy(out=spw[par][:, j * 128:(j + 1) * 128],
                                           in_=bD[:, 128:256])
                        P.sev.inc(i2, 1)
                        rA[(h, ck, f"d{j}")] = P.sev.n
                        return i2, 1
                    _add((f"d2D{h}_{ck}_{j}", [("tensor", tD), ("vector", sD)]))

                for jj, j in enumerate([6, 5, 4, 3, 2]):
                    def tX(t, h=h, ck=ck, par=par, j=j, jj=jj, bX=bX):
                        t.wait_ge(P.sev.h, rA[(h, ck, "d6" if jj == 0 else f"x{jj - 1}")])
                        src_ = xpp[par][:, (jj % 2) * 512:(jj % 2) * 512 + 512]
                        t.matmul(bX, spw[par][:, j * 128:(j + 1) * 128], src_,
                                 start=True, stop=False)
                        i2 = t.matmul(bX, idb, src_, start=False, stop=True)
                        P.smm.inc(i2, 1)
                        rA[(h, ck, f"xm{jj}")] = P.smm.n
                        return i2, 1

                    def sX(v, h=h, ck=ck, par=par, jj=jj, bX=bX):
                        v.wait_ge(P.smm.h, rA[(h, ck, f"xm{jj}")])
                        i2 = v.tensor_copy(out=xpp[par][:, ((jj + 1) % 2) * 512:((jj + 1) % 2) * 512 + 512],
                                           in_=bX)
                        P.sev.inc(i2, 1)
                        rA[(h, ck, f"x{jj}")] = P.sev.n
                        return i2, 1
                    _add((f"d2X{h}_{ck}_{jj}", [("tensor", tX), ("vector", sX)]))

                def tX1(t, h=h, ck=ck, par=par, bX=bX):
                    t.wait_ge(P.sev.h, rA[(h, ck, "x4")])
                    src_ = xpp[par][:, 512:1024]
                    t.matmul(bX, spw[par][:, 128:256], src_, start=True, stop=False)
                    i2 = t.matmul(bX, idb, src_, start=False, stop=True)
                    P.smm.inc(i2, 1)
                    rA[(h, ck, "xm5")] = P.smm.n
                    return i2, 1

                def sX1(v, h=h, ck=ck, par=par, bX=bX):
                    v.wait_ge(P.smm.h, rA[(h, ck, "xm5")])
                    i2 = v.tensor_copy(out=xpp[par][:, 0:512], in_=bX)
                    P.sev.inc(i2, 1)
                    rA[(h, ck, "x5")] = P.sev.n
                    return i2, 1
                _add((f"d2X1{h}_{ck}", [("tensor", tX1), ("vector", sX1)]))

                def tX0(t, h=h, ck=ck, par=par, bX=bX):
                    t.wait_ge(P.sev.h, rA[(h, ck, "x5")])
                    i2 = t.matmul(bX, spw[par][:, 0:128], xpp[par][:, 0:512],
                                  start=True, stop=True)
                    P.smm.inc(i2, 1)
                    rA[(h, ck, "xm6")] = P.smm.n
                    return i2, 1

                def vX0(v, h=h, ck=ck, par=par, bX=bX):
                    v.wait_ge(P.smm.h, rA[(h, ck, "xm6")])
                    i2 = v.tensor_sub(uwa[:, ck * 512:(ck + 1) * 512], xpp[par][:, 0:512], bX)
                    P.sev.inc(i2, 1)
                    rA[(h, ck, "x6")] = P.sev.n
                    return i2, 1
                _add((f"d2X0{h}_{ck}", [("tensor", tX0), ("vector", vX0)]))

                def tW(t, h=h, ck=ck, par=par, bDT=bDT):
                    t.wait_ge(P.sev.h, rA[(h, ck, "x6")])
                    if (h, ck, "d6") in rA:
                        t.wait_ge(P.sev.h, rA[(h, ck, "d6")])
                    for tt in range(NT):
                        i2 = t.transpose(bDT[:, tt * 128:(tt + 1) * 128],
                                         uwa[:, ck * 512 + 256 + tt * 128: ck * 512 + 256 + (tt + 1) * 128],
                                         idb)
                    P.smm.inc(i2, 1)
                    rA[(h, ck, "wm")] = P.smm.n
                    return i2, 1

                def sW(v, h=h, ck=ck, par=par, bDT=bDT):
                    v.wait_ge(P.smm.h, rA[(h, ck, "wm")])
                    i2 = v.tensor_copy(out=wfma[:, ck * 256:(ck + 1) * 256],
                                       in_=bDT[:, 0:256])
                    P.sev.inc(i2, 1)
                    rA[(h, ck, "wev")] = P.sev.n
                    return i2, 1
                _add((f"d2W{h}_{ck}", [("tensor", tW), ("vector", sW)]))

            _emit_pair(h)

            P.scope = f"db{h}"
            def gp_s(g, h=h):
                P.w(g)
                g.memset(S_f, 0.0)
                return P.sgp.inc(g.memset(S_b, 0.0), 1), 1
            P.seq(f"d2s0_{h}", [("gpsimd", gp_s)])

            for ck in range(NCK):
                def tU(t, h=h, ck=ck):
                    if ck == 0:
                        P.w(t)
                    else:
                        t.wait_ge(P.sev.h, rB[(h, ck - 1, "cast")])
                    for tt in range(NT):
                        i2 = t.matmul(P.pbk[6][:, 0:256], wfma[:, ck * 256 + tt * 128: ck * 256 + (tt + 1) * 128],
                                      S_b[:, tt * 256:(tt + 1) * 256],
                                      start=(tt == 0), stop=(tt == NT - 1))
                    P.smm.inc(i2, 1)
                    rB[(h, ck, "u")] = P.smm.n
                    return i2, 1

                def vU(v, h=h, ck=ck):
                    v.wait_ge(P.smm.h, rB[(h, ck, "u")])
                    i2 = v.tensor_sub(uadj, uwa[:, ck * 512: ck * 512 + 256], P.pbk[6][:, 0:256])
                    P.sev.inc(i2, 1)
                    rB[(h, ck, "ua")] = P.sev.n
                    return i2, 1
                P.seq(f"d2U{h}_{ck}", [("tensor", tU), ("vector", vU)])

                def tO(t, h=h, ck=ck):
                    t.wait_ge(P.sev.h, rB[(h, ck, "ua")])
                    for tt in range(NT):
                        t.matmul(P.pbk[6][:, 256:512], qc(qn, tt, ck), S_b[:, tt * 256:(tt + 1) * 256],
                                 start=(tt == 0), stop=False)
                    t.matmul(P.pbk[6][:, 256:512], atta[:, ck * 128:(ck + 1) * 128], uadj,
                             start=False, stop=True)
                    t.matmul(P.pbk[0][:, 0:256], ktma[:, ck * 256: ck * 256 + 128], uadj,
                             start=True, stop=True)
                    i2 = t.matmul(P.pbk[0][:, 256:512], ktma[:, ck * 256 + 128: ck * 256 + 256], uadj,
                                  start=True, stop=True)
                    P.smm.inc(i2, 1)
                    rB[(h, ck, "o")] = P.smm.n
                    return i2, 1

                def vO(v, h=h, ck=ck):
                    v.wait_ge(P.smm.h, rB[(h, ck, "o")])
                    v.tensor_copy(out=otm, in_=P.pbk[6][:, 256:512])
                    v.tensor_add(S_f[:, 0:256], S_f[:, 0:256], P.pbk[0][:, 0:256])
                    i2 = v.tensor_add(S_f[:, 256:512], S_f[:, 256:512], P.pbk[0][:, 256:512])
                    P.sev.inc(i2, 1)
                    rB[(h, ck, "sf")] = P.sev.n
                    return i2, 1

                def sS(v, h=h, ck=ck):
                    i2 = v.tensor_copy(out=S_b, in_=S_f)
                    P.sev.inc(i2, 1)
                    rB[(h, ck, "cast")] = P.sev.n
                    return i2, 1
                P.seq(f"d2O{h}_{ck}", [("tensor", tO), ("vector", vO), ("vector", sS)])

                def tF(t, h=h, ck=ck):
                    t.wait_ge(P.sev.h, rB[(h, ck, "sf")])
                    if (h, ck - 1, "oev") in rB:
                        t.wait_ge(P.sev.h, rB[(h, ck - 1, "oev")])
                    for tt in range(NT):
                        i2 = t.transpose(P.pbk[1][:, tt * 128:(tt + 1) * 128],
                                         otm[:, tt * 128:(tt + 1) * 128], P.ident_f)
                    P.smm.inc(i2, 1)
                    rB[(h, ck, "of")] = P.smm.n
                    return i2, 1

                def vF(v, h=h, ck=ck):
                    v.wait_ge(P.smm.h, rB[(h, ck, "of")])
                    if (h, ck - 1, "st") in rB:
                        v.wait_ge(P.sld.h, rB[(h, ck - 1, "st")])
                    i2 = v.tensor_copy(out=ofm, in_=P.pbk[1][:, 0:256])
                    P.sev.inc(i2, 1)
                    rB[(h, ck, "oev")] = P.sev.n
                    return i2, 1

                def syF(sync, h=h, ck=ck):
                    sync.wait_ge(P.sev.h, rB[(h, ck, "oev")])
                    dst = hdelta[h * 256:(h + 1) * 256, ck * CK: (ck + 1) * CK]
                    i2 = sync.dma_start(out=dst.rearrange("(t p) n -> p t n", p=128),
                                        in_=ofm.rearrange("p (t n) -> p t n", n=128))
                    P.sld.inc(i2, 16)
                    rB[(h, ck, "st")] = P.sld.n
                    return i2, 16
                P.seq(f"d2F{h}_{ck}", [("tensor", tF), ("vector", vF), ("sync", syF)])

        fin(P, "delta2")


def build_l0(ext_out=("x1_f",), dbg=(), upto=99):
    W = ["x0", "wgT", "wcT", "w1T", "w3T", "w2T", "lnw", "ffnw",
         "wgate", "wup", "wdown"]
    P = Prog(ext_in=W, ext_out=set(ext_out) | set(dbg))
    P.open()
    ph_const(P)
    P.lnw_c = colvec(P, P.es, "lnw", D)
    P.ffnw_c = colvec(P, P.es, "ffnw", D)
    phases = [
        lambda: ph_ln(P, "x0", "xn_f", "xn_b", "ln0"),
        lambda: ph_rms(P, "xn_f", P.lnw_c, "nin_b", "rms0", 1e-6),
        lambda: ph_l0_projscan(P),
        lambda: ph_rms(P, "hx_f", P.ffnw_c, "hn_b", "rms1", 1e-6),
        lambda: ph_l0_mlp1(P),
        lambda: ph_l0_mlp2(P),
        lambda: [P._emit(nm, st) for nm, st in gen_fp4_weights(P)],
        lambda: ph_fp4_x(P),
        lambda: ph_fp4_mlp2(P),
        lambda: ph_l0_down(P),
    ]
    for i, f in enumerate(phases):
        if i < upto:
            f()
    P.close()
    return P


def host_l0_inputs(inputs, b):
    T = lambda a: np.ascontiguousarray(np.asarray(a, np.float32).T)
    A = lambda a: np.ascontiguousarray(np.asarray(a, np.float32))
    return {
        "x0": T(inputs["L0_attn_inp"][b]),
        "wgT": T(inputs["L0_attn_Wg"]),
        "wcT": T(inputs["L0_attn_Wc"]),
        "w1T": T(inputs["L0_attn_w1"]),
        "w3T": T(inputs["L0_attn_w3"]),
        "w2T": T(inputs["L0_attn_w2"]),
        "lnw": host_colvec(inputs["L0_attn_ln_w"]),
        "ffnw": host_colvec(inputs["L0_attn_ffn_w"]),
        "wgate": A(inputs["L0_ffn_w_gate"]),
        "wup": A(inputs["L0_ffn_w_up"]),
        "wdown": A(inputs["L0_ffn_w_down"]),
    }


L1_W = ["x1_f", "wq", "wk", "wv", "wb", "igw", "igb", "qcw", "kcw", "vcw",
        "fird", "rw1", "rb1", "rw2", "rb2",
        "ltau", "onw", "wo", "gwT", "gb", "mw1T", "b1c", "mw2T", "b2e"]


def build_l1(ext_out=("out",), dbg=(), upto=99):
    P = Prog(ext_in=L1_W, ext_out=set(ext_out) | set(dbg))
    P.open()
    ph_const(P)
    phases = [
        lambda: ph_ln(P, "x1_f", "xn1_f", "xn1_b", "ln1"),
        lambda: ph_l1_qkv(P),
        lambda: ph_l1_conv(P),
        lambda: ph_l1_fir2(P),
        lambda: ph_l1_delta(P),
        lambda: ph_l1_w1bf(P),
        lambda: ph_l1_router(P),
        lambda: ph_l1_ctx(P),
        lambda: ph_l1_gate(P),
        lambda: ph_l1_moe(P),
        lambda: ph_final(P),
    ]
    for i, f in enumerate(phases):
        if i < upto:
            f()
    P.close()
    return P


def host_l1_inputs(inputs, b, x1=None):
    T = lambda a: np.ascontiguousarray(np.asarray(a, np.float32).T)
    A = lambda a: np.ascontiguousarray(np.asarray(a, np.float32))
    g = lambda k: inputs["L1_attn_" + k]
    d = {
        "wq": A(g("Wq")), "wk": A(g("Wk")), "wv": A(g("Wv")), "wb": A(g("Wb")),
        "igw": A(g("id_gate_W")), "igb": A(g("id_gate_b")),
        "qcw": host_colvec2d(g("q_conv_w")), "kcw": host_colvec2d(g("k_conv_w")),
        "vcw": host_colvec2d(g("v_conv_w")),
        "fird": host_fir_diag(inputs),
        "rw1": A(g("router_W1")), "rb1": host_colvec(g("router_b1")),
        "rw2": A(g("router_W2")), "rb2": A(g("router_b2")),
        "ltau": A(g("log_tau")), "onw": host_colvec(g("o_norm_w")),
        "wo": A(g("Wo")),
        "gwT": T(inputs["L1_ffn_gate_w"]), "gb": A(inputs["L1_ffn_gate_b"]),
        "mw1T": np.ascontiguousarray(np.asarray(inputs["L1_ffn_w1"], np.float32).transpose(0, 2, 1)),
        "b1c": np.stack([host_colvec(b).reshape(128, HMT)
                         for b in np.asarray(inputs["L1_ffn_b1"], np.float32)]),
        "mw2T": np.ascontiguousarray(np.asarray(inputs["L1_ffn_w2"], np.float32).transpose(0, 2, 1)),
        "b2e": A(inputs["L1_ffn_b2"]),
    }
    if x1 is not None:
        d["x1_f"] = np.ascontiguousarray(x1)
    return d


def build_full():
    W0 = ["x0", "wgT", "wcT", "w1T", "w3T", "w2T", "lnw", "ffnw",
          "wgate", "wup", "wdown"]
    P = Prog(ext_in=set(W0) | (set(L1_W) - {"x1_f"}), ext_out={"out"})
    P.open()
    def start_drip():
        P._drip = gen_fp4_weights(P)
        P._drip_scope = "fp4w"
        P._drip_k = 2
        save = P.scope
        P.scope = "fp4w"
        P._dripping = True
        try:
            nm, st = next(P._drip)
            P._emit(nm, st)
        finally:
            P.scope = save
            P._dripping = False
    for tag, fn in [
        ("const", lambda: (ph_const(P),
                           setattr(P, "lnw_c", colvec(P, P.es, "lnw", D)),
                           setattr(P, "ffnw_c", colvec(P, P.es, "ffnw", D)))),
        ("drip0", start_drip),
        ("ln0", lambda: ph_ln(P, "x0", "xn_f", "xn_b", "ln0")),
        ("rms0", lambda: ph_rms(P, "xn_f", P.lnw_c, "nin_b", "rms0", 1e-6)),
        ("pscan", lambda: ph_l0_projscan(P)),
        ("rms1", lambda: ph_rms(P, "hx_f", P.ffnw_c, "hn_b", "rms1", 1e-6)),
        ("mlp1", lambda: ph_l0_mlp1(P)),
        ("mlp2", lambda: ph_l0_mlp2(P)),
        ("fp4w", lambda: P.drip_flush()),
        ("fp4x", lambda: ph_fp4_x(P)),
        ("fp4mlp", lambda: ph_fp4_mlp2(P)),
        ("l0down", lambda: ph_l0_down(P)),
        ("ln1", lambda: ph_ln(P, "x1_f", "xn1_f", "xn1_b", "ln1")),
        ("qkv", lambda: ph_l1_qkv(P)),
        ("conv", lambda: ph_l1_conv(P)),
        ("fir", lambda: ph_l1_fir2(P)),
        ("delta", lambda: ph_l1_delta2(P)),
        ("w1bf", lambda: ph_l1_w1bf(P)),
        ("router", lambda: ph_l1_router(P)),
        ("ctx", lambda: ph_l1_ctx(P)),
        ("gate", lambda: ph_l1_gate(P)),
        ("moe", lambda: ph_l1_moe3(P)),
        ("final", lambda: ph_final(P)),
    ]:
        P.scope = tag
        fn()
    P.close()
    return P


def kernel(**inputs):
    n = 8
    P = build_full()
    maps = []
    for b in range(n):
        m = host_l0_inputs(inputs, b)
        m.update(host_l1_inputs(inputs, b))
        maps.append(m)
    r = run_bass_kernel_spmd(P.nc, maps, core_ids=list(range(n)))
    out = np.stack([np.ascontiguousarray(r.results[b]["out"].T) for b in range(n)])
    return out.astype(np.float32)


def ph_l1_moe3(P):
    hx1b = P.d("xn1_b", [D, L], BF16)
    hsp = P.d("hsp_b", [HM, L], BF16)
    hmb = P.d("masks_b", [EXP, L], BF16)
    hg = P.d("g_f", [D, L])
    hw1 = P.d("mw1T", [EXP, D, HM], F32)
    hw2 = P.d("mw2T", [EXP, HM, D], F32)
    hb1 = P.d("b1c", [EXP, 128, HMT], F32)
    hb2 = P.d("b2e", [EXP, D], F32)
    with contextlib.ExitStack() as es:
        mov = [P.sb(es, f"m3_x{i}", [128, DT * CH], BF16) for i in range(2)]
        wgrp = [P.sb(es, f"m3_w{i}", [128, DT * 8 * 128], BF16) for i in range(2)]
        hch = [P.sb(es, f"m3_h{i}", [128, 8 * CH], BF16) for i in range(2)]
        h2s = [P.sb(es, f"m3_g{i}", [128, 8 * CH], BF16) for i in range(4)]
        yac = [P.sb(es, f"m3_y{i}", [128, 4 * CH], F32) for i in range(2)]
        b1t = P.sb(es, "m3_b1", [128, HMT], F32)
        b2t = P.sb(es, "m3_b2", [1, D], BF16)
        mrw = [P.sb(es, f"m3_mr{i}", [1, L], BF16) for i in range(2)]
        mrep = [P.sb(es, f"m3_rp{i}", [128, CH], F32) for i in range(2)]
        w2f = P.sb(es, "m3_w2f", [128, HMT * DT * 128], BF16)
        s_mv = [P.newsem(f"m3mv{i}") for i in range(2)]
        s_wg = [P.newsem(f"m3wg{i}") for i in range(2)]
        s_hst = [P.newsem(f"m3hs{i}") for i in range(2)]
        s_h2 = [P.newsem(f"m3h2{i}") for i in range(4)]
        s_w2 = P.newsem("m3w2")
        s_b = P.newsem("m3b")
        s_acc = P.newsem("m3ac")

        gmm1 = {}
        scA = {}
        scB = {}
        hst = {}
        p2s = {}
        pm = {}
        vye = {}
        ver = {}
        sacc = {}
        rec = {"mmlast": 0}

        def mk_mrep(e, c):
            def tn_r(t, e=e, c=c):
                if c == 0:
                    t.wait_ge(P.ssc.h, scA[(e, 31)])
                    t.wait_ge(s_b.h, s_b.n)
                else:
                    t.wait_ge(P.sev.h, ver[(e, c - 1)])
                i2 = t.matmul(P.pbk[0], P.ones_b1, mrw[e % 2][:, c * CH:(c + 1) * CH],
                              start=True, stop=True)
                return P.smm.inc(i2, 1), 1

            def ve_r(v, e=e, c=c):
                v.wait_ge(P.smm.h, P.smm.n)
                i2 = v.tensor_copy(out=mrep[c % 2], in_=P.pbk[0])
                P.sev.inc(i2, 1)
                ver[(e, c)] = P.sev.n
                return i2, 1
            P.seq(f"m3r_{e}_{c}", [("tensor", tn_r), ("vector", ve_r)])

        for e in range(EXP):
            def gp_b(g, e=e):
                if e == 0:
                    P.w(g)
                else:
                    g.wait_ge(P.smm.h, rec["mmlast"])
                s_b.inc(g.dma_start(out=b1t, in_=hb1[e]), 16)
                s_b.inc(g.dma_start(out=b2t, in_=hb2[e:e + 1, :]), 16)
                i2 = g.dma_start(out=mrw[e % 2], in_=hmb[e:e + 1, :])
                return s_b.inc(i2, 16), 16
            P.seq(f"m3b{e}", [("gpsimd", gp_b)])

            for mg in range(4):
                def gp_w(g, e=e, mg=mg):
                    wb = mg % 2
                    gm = e * 4 + mg
                    if gm >= 2:
                        g.wait_ge(P.smm.h, gmm1[(gm - 2) * 8 + 7])
                    sl = hw1[e][:, mg * 1024:(mg + 1) * 1024]
                    i2 = g.dma_start(out=wgrp[wb].rearrange("p (k m q) -> p k m q", m=8, q=128),
                                     in_=sl.rearrange("(k p) (m q) -> p k m q", p=128, q=128))
                    s_wg[wb].inc(i2, 16)
                    qsl = hw2[e][mg * 1024:(mg + 1) * 1024, :]
                    if mg == 0 and e > 0:
                        g.wait_ge(P.smm.h, rec["mmlast"])
                    i2 = g.dma_start(
                        out=w2f.rearrange("p (k m q) -> p k m q", m=DT, q=128)[:, mg * 8:(mg + 1) * 8],
                        in_=qsl.rearrange("(k p) (m q) -> p k m q", p=128, q=128))
                    s_w2.inc(i2, 16)
                    return i2, 16
                P.seq(f"m3w{e}_{mg}", [("gpsimd", gp_w)])

                for c in range(NCH):
                    it = mg * 8 + c
                    git = e * 32 + it
                    mb, wb, hb = it % 2, mg % 2, it % 2

                    def sy_x(sync, c=c, mb=mb, git=git):
                        if git - 2 >= 0:
                            sync.wait_ge(P.smm.h, gmm1[git - 2])
                        i2 = sync.dma_start(out=sview(mov[mb]), in_=cview(hx1b, c))
                        return s_mv[mb].inc(i2, 16), 16

                    def tn_a(t, e=e, it=it, mb=mb, wb=wb, git=git):
                        if git == 0:
                            P.w(t)
                        t.wait_ge(s_mv[mb].h, s_mv[mb].n)
                        t.wait_ge(s_wg[wb].h, s_wg[wb].n)
                        if (e, it - 1) in scA:
                            t.wait_ge(P.ssc.h, scA[(e, it - 1)])
                        elif e > 0:
                            t.wait_ge(P.ssc.h, scA[(e - 1, 31)])
                            t.wait_ge(P.sev.h, ver[(e - 1, 7)])
                        for j in range(4):
                            for k in range(DT):
                                i2 = t.matmul(P.pbk[j], wv(wgrp[wb], k, j, 8), ct(mov[mb], k),
                                              start=(k == 0), stop=(k == DT - 1))
                        return P.smm.inc(i2, 1), 1

                    def sc_a(s, e=e, it=it, hb=hb):
                        s.wait_ge(P.smm.h, P.smm.n)
                        s.wait_ge(s_b.h, s_b.n)
                        sl = s_hst[hb]
                        s.wait_ge(sl.h, sl.n)
                        for j in range(4):
                            i2 = s.activation(out=ct(hch[hb], j), in_=P.pbk[j], func=AF.Relu,
                                              bias=b1t[:, (it // 8) * 8 + j: (it // 8) * 8 + j + 1])
                        P.ssc.inc(i2, 1)
                        scA[(e, it)] = P.ssc.n
                        return i2, 1

                    def tn_b(t, e=e, it=it, mb=mb, wb=wb, git=git):
                        if (e, it - 1) in scB:
                            t.wait_ge(P.sev.h, scB[(e, it - 1)])
                        elif e > 0:
                            t.wait_ge(P.sev.h, vye[(e - 1, 7, 1, 3)])
                        for j in range(4, 8):
                            for k in range(DT):
                                i2 = t.matmul(P.pbk[j], wv(wgrp[wb], k, j, 8), ct(mov[mb], k),
                                              start=(k == 0), stop=(k == DT - 1))
                        P.smm.inc(i2, 1)
                        gmm1[git] = P.smm.n
                        return i2, 1

                    def sc_b(v, e=e, it=it, hb=hb):
                        v.wait_ge(P.smm.h, P.smm.n)
                        v.wait_ge(s_b.h, s_b.n)
                        v.wait_ge(s_hst[hb].h, s_hst[hb].n)
                        for j in range(4, 8):
                            i2 = v.tensor_scalar(out=ct(hch[hb], j), in0=P.pbk[j],
                                                 scalar1=b1t[:, (it // 8) * 8 + j: (it // 8) * 8 + j + 1],
                                                 scalar2=0.0, op0=ALU.add, op1=ALU.max)
                        P.sev.inc(i2, 1)
                        scB[(e, it)] = P.sev.n
                        return i2, 1

                    def st_h(s, e=e, it=it, mg=mg, c=c, hb=hb):
                        s.wait_ge(P.sev.h, scB[(e, it)])
                        dst = hsp[mg * 1024:(mg + 1) * 1024, c * CH:(c + 1) * CH]
                        i2 = s.dma_start(out=dst.rearrange("(k p) n -> p k n", p=128),
                                         in_=sview(hch[hb]))
                        s_hst[hb].inc(i2, 16)
                        hst[(e, it)] = s_hst[hb].n
                        return i2, 16
                    P.seq(f"m3m1_{e}_{it}", [("sync", sy_x), ("tensor", tn_a), ("scalar", sc_a),
                                             ("tensor", tn_b), ("vector", sc_b), ("scalar", st_h)])

            for c in range(NCH):
                if c == 0:
                    mk_mrep(e, 0)
                for kg in range(4):
                    def sy_g(sync, e=e, c=c, kg=kg):
                        key = (e, c - 1, kg) if c > 0 else (e - 1, 7, kg)
                        if key in p2s:
                            sync.wait_ge(P.smm.h, p2s[key])
                        sh = s_hst[(kg * 8 + c) % 2]
                        sync.wait_ge(sh.h, hst[(e, kg * 8 + c)])
                        sl = hsp[kg * 1024:(kg + 1) * 1024, c * CH:(c + 1) * CH]
                        i2 = sync.dma_start(out=sview(h2s[kg]),
                                            in_=sl.rearrange("(k p) n -> p k n", p=128))
                        return s_h2[kg].inc(i2, 16), 16
                    P.seq(f"m3g_{e}_{c}_{kg}", [("sync", sy_g)])

                for p in range(2):
                    def tn_p(t, e=e, c=c, p=p):
                        if c == 0 and p == 0:
                            t.wait_ge(s_w2.h, s_w2.n)
                            t.wait_ge(s_b.h, s_b.n)
                        for m in range(4):
                            mt = p * 4 + m
                            if p == 1:
                                t.wait_ge(P.sev.h, vye[(e, c, 0, m)])
                            elif c > 0:
                                t.wait_ge(P.sev.h, vye[(e, c - 1, 1, m)])
                            elif m == 0:
                                if e > 0:
                                    t.wait_ge(P.sev.h, vye[(e - 1, 7, 1, 3)])
                                t.wait_ge(P.sev.h, scB[(e, 31)])
                            for k in range(HMT):
                                if p == 0 and m == 0 and k % 8 == 0:
                                    kg = k // 8
                                    t.wait_ge(s_h2[kg].h, s_h2[kg].n)
                                i2 = t.matmul(P.pbk[4 + m], wv(w2f, k, mt, DT),
                                              ct(h2s[k // 8], k % 8),
                                              start=(k == 0), stop=False)
                                if p == 1 and m == 3 and k % 8 == 7:
                                    P.smm.inc(i2, 1)
                                    p2s[(e, c, k // 8)] = P.smm.n
                            i2 = t.matmul(P.pbk[4 + m], b2t[:, mt * 128:(mt + 1) * 128],
                                          mrw[e % 2][:, c * CH:(c + 1) * CH],
                                          start=False, stop=True)
                            P.smm.inc(i2, 1)
                            pm[(e, c, p, m)] = P.smm.n
                        if p == 1 and c == NCH - 1:
                            rec["mmlast"] = P.smm.n
                        return i2, 1

                    def ve_y(v, e=e, c=c, p=p):
                        if (e, c - 1, p) in sacc:
                            v.wait_ge(s_acc.h, sacc[(e, c - 1, p)])
                        elif (e - 1, NCH - 1, p) in sacc:
                            v.wait_ge(s_acc.h, sacc[(e - 1, NCH - 1, p)])
                        for m in range(4):
                            v.wait_ge(P.smm.h, pm[(e, c, p, m)])
                            i2 = v.tensor_mul(ct(yac[p], m), P.pbk[4 + m], mrep[c % 2])
                            P.sev.inc(i2, 1)
                            vye[(e, c, p, m)] = P.sev.n
                        return i2, 1

                    def gp_acc(g, e=e, c=c, p=p):
                        g.wait_ge(P.sev.h, vye[(e, c, p, 3)])
                        g.wait_ge(s_acc.h, s_acc.n)
                        rows = hg[p * 512:(p + 1) * 512, c * CH:(c + 1) * CH]
                        kw = {} if e == 0 else {"accum_op": ALU.add}
                        i2 = g.dma_start(out=rows.rearrange("(k q) n -> q k n", q=128),
                                         in_=yac[p].rearrange("q (k n) -> q k n", n=CH), **kw)
                        s_acc.inc(i2, 16)
                        sacc[(e, c, p)] = s_acc.n
                        return i2, 16
                    P.seq(f"m3p_{e}_{c}_{p}", [("tensor", tn_p), ("vector", ve_y),
                                               ("gpsimd", gp_acc)])
                    if p == 0 and c + 1 < NCH:
                        mk_mrep(e, c + 1)

        def sy_end(sync):
            sync.wait_ge(s_acc.h, s_acc.n)
            return sync.nop(), 1
        P.seq("m3end", [("sync", sy_end)])
        fin(P, "moe3")


def emit_quant_steps3(P, steps, xin, xout, tmps, nblk, roll, guard_fn, tag):
    W = nblk * 16
    xa, sg, vm, cnt, r1, scl, rs = (tmps[k] for k in
                                    ("xa", "sg", "vm", "cnt", "r1", "scl", "rs"))

    def sA(v):
        v.tensor_scalar_mul(out=xa[:, 0:W], in0=xin, scalar1=-1.0)
        return P.sev.inc(v.tensor_max(xa[:, 0:W], xa[:, 0:W], xin), 1), 1

    def vA(v):
        if roll["sB"]:
            v.wait_ge(P.ssc.h, roll["sB"])
        xav = xa[:, 0:W].rearrange("p (b s) -> p b s", s=16)
        v.tensor_reduce(out=scl[:, 0:nblk], in_=xav, axis=AX.X, op=ALU.max,
                        apply_absolute_value=True)
        v.tensor_scalar(out=scl[:, 0:nblk], in0=scl[:, 0:nblk], scalar1=1.0 / 6.0,
                        scalar2=1e-12, op0=ALU.mult, op1=ALU.max)
        v.reciprocal(out=rs[:, 0:nblk], in_=scl[:, 0:nblk])
        rsb = rs[:, 0:nblk].broadcast_to([128, nblk, 16])
        vmv = vm[:, 0:W].rearrange("p (b s) -> p b s", s=16)
        v.tensor_tensor(out=vmv, in0=xav, in1=rsb, op=ALU.mult)
        v.tensor_scalar(out=cnt[:, 0:W], in0=vm[:, 0:W], scalar1=0.25, scalar2=None,
                        op0=ALU.is_gt)
        for t in QTHR:
            v.scalar_tensor_tensor(out=cnt[:, 0:W], in0=vm[:, 0:W], scalar=t,
                                   in1=cnt[:, 0:W], op0=ALU.is_gt, op1=ALU.add)
        i2 = v.scalar_tensor_tensor(out=cnt[:, 0:W], in0=vm[:, 0:W], scalar=5.0,
                                    in1=cnt[:, 0:W], op0=ALU.is_gt, op1=ALU.add)
        P.sev.inc(i2, 1)
        roll["vA"] = P.sev.n
        return i2, 1

    def sB(s):
        s.wait_ge(P.sev.h, roll["vA"])
        if roll["vB"]:
            s.wait_ge(P.sev.h, roll["vB"])
        s.activation(out=r1[:, 0:W], in_=cnt[:, 0:W], func=AF.Relu, bias=P.cc(-4.0))
        i2 = s.activation(out=vm[:, 0:W], in_=cnt[:, 0:W], func=AF.Relu, bias=P.cc(-6.0))
        P.ssc.inc(i2, 1)
        roll["sB"] = P.ssc.n
        return i2, 1

    def vB(v):
        v.wait_ge(P.ssc.h, roll["sB"])
        qtm_guard = guard_fn() if guard_fn else None
        if qtm_guard:
            v.wait_ge(P.smm.h, qtm_guard)
        v.tensor_scalar(out=sg[:, 0:W], in0=xin, scalar1=0.0, scalar2=None,
                        op0=ALU.is_gt)
        v.tensor_add(cnt[:, 0:W], cnt[:, 0:W], r1[:, 0:W])
        v.scalar_tensor_tensor(out=cnt[:, 0:W], in0=cnt[:, 0:W], scalar=0.5,
                               in1=vm[:, 0:W], op0=ALU.mult, op1=ALU.add)
        hsb = scl[:, 0:nblk].broadcast_to([128, nblk, 16])
        cntv = cnt[:, 0:W].rearrange("p (b s) -> p b s", s=16)
        v.tensor_tensor(out=cntv, in0=cntv, in1=hsb, op=ALU.mult)
        v.tensor_mul(r1[:, 0:W], cnt[:, 0:W], sg[:, 0:W])
        i2 = v.scalar_tensor_tensor(out=xout, in0=r1[:, 0:W], scalar=2.0,
                                    in1=cnt[:, 0:W], op0=ALU.mult, op1=ALU.subtract)
        P.sev.inc(i2, 1)
        roll["vB"] = P.sev.n
        return i2, 1

    steps += [(f"{tag}a", "vector", sA), (f"{tag}A", "vector", vA),
              (f"{tag}b", "scalar", sB), (f"{tag}B", "vector", vB)]


def ph_fp4_mlp2(P):
    hxq = P.d("xq_b", [D, L], BF16)
    hqg = P.d("qgT_b", [D, IFF], BF16)
    hqu = P.d("quT_b", [D, IFF], BF16)
    hout = P.d("hq_b", [IFF, L], BF16)
    MN = 1
    NW = (ITT + MN - 1) // MN
    QH = IFF // 2
    G1 = [(g, min(4, ITT - g)) for g in range(0, ITT, 4)]
    G2 = [(g, min(8, ITT - g)) for g in range(0, ITT, 8)]

    def wslice(h, m0, mn):
        return h[:, m0 * 128:(m0 + mn) * 128].rearrange(
            "(k p) (m q) -> p k m q", p=128, q=128)

    with contextlib.ExitStack() as es:
        mov = [P.sb(es, f"f2_x{i}", [128, DT * CH], BF16) for i in range(2)]
        wgw = [P.sb(es, f"f2_g{i}", [128, DT * MN * 128], BF16) for i in range(2)]
        wuw = [P.sb(es, f"f2_u{i}", [128, DT * MN * 128], BF16) for i in range(2)]
        gf = P.sb(es, "f2_gf", [128, 2 * CH], F32)
        hf = [P.sb(es, f"f2_hf{i}", [128, ITT * CH], F32) for i in range(2)]
        htm = P.sb(es, "f2_htm", [128, IFF], F32)
        qtm = P.sb(es, "f2_qtm", [128, IFF], BF16)
        outb = [P.sb(es, f"f2_ob{i}", [128, ITT * 128], BF16) for i in range(2)]
        tmps = mk_qtmp(P, es, "f2_t", QH)
        s_mv = [P.newsem(f"f2mv{i}") for i in range(2)]
        s_wv = [P.newsem(f"f2wv{i}") for i in range(2)]
        s_st = [P.newsem(f"f2st{i}") for i in range(2)]

        pb2 = [P.pbk[6].bitcast(BF16), P.pbt]
        rWu = {}; rSG = {}; rVU = {}; rT1 = {}; rT2 = {}
        roll = {"ve10": 0, "ve11": 0, "ve20": 0, "ve21": 0,
                "vA": 0, "sB": 0, "vB": 0}

        def wave_elems(c):
            elems = []

            def sy_x(sync, c=c):
                if c == 0:
                    P.w(sync)
                g2 = (c - 1) * NW - 1
                if g2 >= 0:
                    sync.wait_ge(P.smm.h, rWu[g2])
                i2 = sync.dma_start(out=sview(mov[c % 2]), in_=cview(hxq, c))
                return s_mv[c % 2].inc(i2, 16), 16
            elems.append((f"f2x{c}", [("sync", sy_x)]))

            for w in range(NW):
                gw = c * NW + w
                m0 = w * MN
                mn = min(MN, ITT - m0)

                def syw(sync, gw=gw, m0=m0, mn=mn):
                    if gw - 2 >= 0:
                        sync.wait_ge(P.smm.h, rWu[gw - 2])
                    sl = s_wv[gw % 2]
                    dg = wgw[gw % 2][:, 0:DT * mn * 128].rearrange(
                        "p (k m q) -> p k m q", m=mn, q=128)
                    du = wuw[gw % 2][:, 0:DT * mn * 128].rearrange(
                        "p (k m q) -> p k m q", m=mn, q=128)
                    sl.inc(sync.dma_start(out=dg, in_=wslice(hqg, m0, mn)), 16)
                    return sl.inc(sync.dma_start(out=du, in_=wslice(hqu, m0, mn)), 16), 16

                def tn_g(t, c=c, gw=gw):
                    if gw == 0:
                        P.w(t)
                    t.wait_ge(s_wv[gw % 2].h, s_wv[gw % 2].n)
                    t.wait_ge(s_mv[c % 2].h, s_mv[c % 2].n)
                    if gw - 2 in rSG:
                        t.wait_ge(P.ssc.h, rSG[gw - 2])
                    for k in range(DT):
                        i2 = t.matmul(P.pbk[gw % 2], wv(wgw[gw % 2], k, 0, 1),
                                      ct(mov[c % 2], k),
                                      start=(k == 0), stop=(k == DT - 1))
                    return P.smm.inc(i2, 1), 1

                def sc_g(s, gw=gw):
                    s.wait_ge(P.smm.h, P.smm.n)
                    if gw - 2 in rVU:
                        s.wait_ge(P.sev.h, rVU[gw - 2])
                    i2 = s.activation(out=ct(gf, gw % 2), in_=P.pbk[gw % 2], func=AF.Silu)
                    P.ssc.inc(i2, 1)
                    rSG[gw] = P.ssc.n
                    return i2, 1

                def tn_u(t, c=c, gw=gw):
                    if gw - 2 in rVU:
                        t.wait_ge(P.sev.h, rVU[gw - 2])
                    for k in range(DT):
                        i2 = t.matmul(P.pbk[2 + gw % 2], wv(wuw[gw % 2], k, 0, 1),
                                      ct(mov[c % 2], k),
                                      start=(k == 0), stop=(k == DT - 1))
                    P.smm.inc(i2, 1)
                    rWu[gw] = P.smm.n
                    return i2, 1

                def ve_u(v, c=c, gw=gw, m0=m0):
                    v.wait_ge(P.smm.h, P.smm.n)
                    v.wait_ge(P.ssc.h, rSG[gw])
                    if c > 1:
                        v.wait_ge(P.smm.h, rT1[(c - 2, 3, 4 * (m0 // 4))])
                    i2 = v.tensor_mul(ct(hf[c % 2], m0), ct(gf, gw % 2), P.pbk[2 + gw % 2])
                    P.sev.inc(i2, 1)
                    rVU[gw] = P.sev.n
                    return i2, 1

                elems.append((f"f2w{gw}", [("sync", syw), ("tensor", tn_g), ("scalar", sc_g),
                                           ("tensor", tn_u), ("vector", ve_u)]))
            return elems

        def quant_elems(c):
            elems = []
            for lt in range(4):
                for gi, (g0, gn) in enumerate(G1):
                    bk = 4 + gi % 2
                    def tn1(t, c=c, lt=lt, g0=g0, gn=gn, bk=bk, gi=gi):
                        lastw = min((g0 + gn - 1) // MN, NW - 1)
                        t.wait_ge(P.sev.h, rVU[c * NW + lastw])
                        if roll[f"ve1{gi % 2}"]:
                            t.wait_ge(P.sev.h, roll[f"ve1{gi % 2}"])
                        for j in range(gn):
                            k = g0 + j
                            i2 = t.transpose(P.pbk[bk][:, j * 128:(j + 1) * 128],
                                             hf[c % 2][:, k * CH + lt * 128: k * CH + lt * 128 + 128],
                                             P.ident_f)
                        P.smm.inc(i2, 1)
                        rT1[(c, lt, g0)] = P.smm.n
                        return i2, 1

                    def ve1(v, g0=g0, gn=gn, bk=bk, gi=gi):
                        v.wait_ge(P.smm.h, P.smm.n)
                        i2 = v.tensor_copy(out=htm[:, g0 * 128:(g0 + gn) * 128],
                                           in_=P.pbk[bk][:, 0:gn * 128])
                        P.sev.inc(i2, 1)
                        roll[f"ve1{gi % 2}"] = P.sev.n
                        return i2, 1
                    elems.append((f"q1_{c}_{lt}_{g0}",
                                  [("tensor", tn1), ("vector", ve1)]))

                def qg(c=c, lt=lt):
                    return rT2.get((c, lt - 1)) if lt > 0 else rT2.get((c - 1, 3))
                halves = []
                for h in range(2):
                    qsteps = []
                    emit_quant_steps3(P, qsteps, htm[:, h * QH:(h + 1) * QH],
                                      qtm[:, h * QH:(h + 1) * QH], tmps, QH // 16,
                                      roll, qg, f"q_{c}_{lt}_{h}")
                    halves.append(qsteps)
                h0, h1 = halves
                order = [h0[0], h0[1], h0[2], h1[0], h0[3], h1[1], h1[2], h1[3]]
                elems += [(nm, [(en, f)]) for nm, en, f in order]

                for gi, (g0, gn) in enumerate(G2):
                    def tn2(t, c=c, lt=lt, g0=g0, gn=gn, gi=gi):
                        t.wait_ge(P.sev.h, roll["vB"])
                        if roll[f"ve2{gi % 2}"]:
                            t.wait_ge(P.sev.h, roll[f"ve2{gi % 2}"])
                        pb = pb2[gi % 2]
                        for j in range(gn):
                            k = g0 + j
                            i2 = t.transpose(pb[:, j * 128:(j + 1) * 128],
                                             qtm[:, k * 128:(k + 1) * 128], P.ident_b)
                        P.smm.inc(i2, 1)
                        rT2[(c, lt)] = P.smm.n
                        return i2, 1

                    def ve2(v, c=c, lt=lt, g0=g0, gn=gn, gi=gi):
                        v.wait_ge(P.smm.h, P.smm.n)
                        sl = s_st[lt % 2]
                        v.wait_ge(sl.h, sl.n)
                        pb = pb2[gi % 2]
                        for j in range(gn):
                            k = g0 + j
                            i2 = v.tensor_copy(out=outb[lt % 2][:, k * 128:(k + 1) * 128],
                                               in_=pb[:, j * 128:(j + 1) * 128])
                        P.sev.inc(i2, 1)
                        roll[f"ve2{gi % 2}"] = P.sev.n
                        return i2, 1
                    elems.append((f"q2_{c}_{lt}_{g0}",
                                  [("tensor", tn2), ("vector", ve2)]))

                def st(sync, c=c, lt=lt):
                    sync.wait_ge(P.sev.h, max(roll["ve20"], roll["ve21"]))
                    dst = hout[:, c * CH + lt * 128: c * CH + (lt + 1) * 128]
                    i2 = sync.dma_start(out=dst.rearrange("(k p) n -> p k n", p=128),
                                        in_=outb[lt % 2].rearrange("p (k n) -> p k n", n=128))
                    return s_st[lt % 2].inc(i2, 16), 16
                elems.append((f"qst_{c}_{lt}", [("sync", st)]))
            return elems

        def emit_merged(A, B):
            na, nb = len(A), len(B)
            ia = ib = 0
            while ia < na or ib < nb:
                if ib >= nb or (ia < na and ia * nb <= ib * na):
                    nm, st = A[ia]; ia += 1
                else:
                    nm, st = B[ib]; ib += 1
                P.seq(nm, st)

        MERGE = True
        for c in range(NCH):
            if MERGE:
                emit_merged(wave_elems(c), quant_elems(c - 1) if c > 0 else [])
            else:
                for nm, st in wave_elems(c):
                    P.seq(nm, st)
                for nm, st in quant_elems(c):
                    P.seq(nm, st)
        if MERGE:
            for nm, st in quant_elems(NCH - 1):
                P.seq(nm, st)

        def sy_end(sync):
            sync.wait_ge(s_st[0].h, s_st[0].n)
            sync.wait_ge(s_st[1].h, s_st[1].n)
            return sync.nop(), 1
        P.seq("f2end", [("sync", sy_end)])
        fin(P, "fm")


def host_fir_diag(inputs):
    ws = []
    for K in (3, 7, 15, 31):
        w = np.asarray(inputs["L1_attn_fir_w" + str(K)], np.float32)
        ws.append(w.reshape(-1, w.shape[-1]))
    out = np.zeros((DT, 56, 128, 128), np.float32)
    idx = np.arange(128)
    for dt in range(DT):
        t0 = 0
        for w in ws:
            K = w.shape[-1]
            for k in range(K):
                out[dt, t0 + k, idx, idx] = w[dt * 128:(dt + 1) * 128, k]
            t0 += K
    return out


def ph_l1_fir2(P):
    PAD = 32
    hs = P.d("vs_f", [D, L])
    hfd = P.d("fird", [DT, 56, 128, 128], F32)
    KS = (3, 7, 15, 31)
    houts = {K: P.d(f"fir{K}_f", [D, L]) for K in KS}
    with contextlib.ExitStack() as es:
        xt = [P.sb(es, f"fr_x{i}", [128, PAD + L], BF16) for i in range(2)]
        wdg = [P.sb(es, f"fr_w{i}", [128, 56 * 128], BF16) for i in range(2)]
        outb = [P.sb(es, f"fr_o{i}", [128, 4 * CH], F32) for i in range(2)]
        s_xt = [P.newsem(f"frx{i}") for i in range(2)]
        s_wd = [P.newsem(f"frw{i}") for i in range(2)]
        s_st = [P.newsem(f"frs{i}") for i in range(2)]
        rMM = {}; rEV = {}; rST = {}

        def gp0(g):
            P.w(g)
            g.memset(xt[0][:, 0:PAD], 0.0)
            return P.sgp.inc(g.memset(xt[1][:, 0:PAD], 0.0), 1), 1
        P.seq("fr_pad", [("gpsimd", gp0)])

        for dt in range(DT):
            def gp_w(g, dt=dt):
                if dt == 0:
                    P.w(g)
                if dt - 2 in rMM:
                    g.wait_ge(P.smm.h, rMM[dt - 2])
                i2 = g.dma_start(out=wdg[dt % 2].rearrange("p (t q) -> p t q", q=128),
                                 in_=hfd[dt].rearrange("t p q -> p t q"))
                s_wd[dt % 2].inc(i2, 16)
                i2 = g.dma_start(out=xt[dt % 2][:, PAD:PAD + L],
                                 in_=hs[dt * 128:(dt + 1) * 128, :])
                return s_xt[dt % 2].inc(i2, 16), 16
            P.seq(f"fr_ld{dt}", [("gpsimd", gp_w)])

            for c in range(NCH):
                def tn(t, dt=dt, c=c):
                    if c == 0:
                        t.wait_ge(s_wd[dt % 2].h, s_wd[dt % 2].n)
                        t.wait_ge(s_xt[dt % 2].h, s_xt[dt % 2].n)
                    key = (dt, c - 2) if c >= 2 else (dt - 1, NCH - 2 + c)
                    if key in rEV:
                        t.wait_ge(P.sev.h, rEV[key])
                    t0 = 0
                    for fi, K in enumerate(KS):
                        bank = (c % 2) * 4 + fi
                        for k in range(K):
                            base = PAD - (K - 1) + k + c * CH
                            i2 = t.matmul(P.pbk[bank],
                                          wdg[dt % 2][:, (t0 + k) * 128:(t0 + k + 1) * 128],
                                          xt[dt % 2][:, base:base + CH],
                                          start=(k == 0), stop=(k == K - 1))
                        t0 += K
                    P.smm.inc(i2, 1)
                    rMM[dt] = P.smm.n
                    return i2, 1

                def ve(v, dt=dt, c=c):
                    v.wait_ge(P.smm.h, P.smm.n)
                    key = (dt, c - 2) if c >= 2 else (dt - 1, NCH - 2 + c)
                    if key in rST:
                        v.wait_ge(s_st[c % 2].h, rST[key])
                    for fi in range(4):
                        i2 = v.tensor_copy(out=ct(outb[c % 2], fi),
                                           in_=P.pbk[(c % 2) * 4 + fi])
                    P.sev.inc(i2, 1)
                    rEV[(dt, c)] = P.sev.n
                    return i2, 1

                def sy2(sync, dt=dt, c=c):
                    sync.wait_ge(P.sev.h, rEV[(dt, c)])
                    for fi, K in enumerate(KS):
                        i2 = sync.dma_start(
                            out=houts[K][dt * 128:(dt + 1) * 128, c * CH:(c + 1) * CH],
                            in_=ct(outb[c % 2], fi))
                        s_st[c % 2].inc(i2, 16)
                    rST[(dt, c)] = s_st[c % 2].n
                    return i2, 16
                P.seq(f"fr_{dt}_{c}", [("tensor", tn), ("vector", ve), ("sync", sy2)])

        def sy_end(sync):
            sync.wait_ge(s_st[0].h, s_st[0].n)
            sync.wait_ge(s_st[1].h, s_st[1].n)
            return sync.nop(), 1
        P.seq("fr_end", [("sync", sy_end)])
        fin(P, "fir")


# version 25
# speedup vs baseline: 1.0240x; 1.0240x over previous
import contextlib
import numpy as np
import concourse.bass as bass
import concourse.mybir as mybir
from concourse.bass_utils import run_bass_kernel_spmd

F32 = mybir.dt.float32
BF16 = mybir.dt.bfloat16
AF = mybir.ActivationFunctionType
ALU = mybir.AluOpType
AX = mybir.AxisListType

D = 1024
L = 4096
CH = 512
NCH = L // CH
DT = D // 128
IFF = 2816
ITT = IFF // 128
H, DKQ, DVV = 4, 256, 256
CK = 128
NCK = L // CK
EXP = 8
HM = 4096
HMT = HM // 128

DBG_RMS = False
QTHR = (0.75, 1.25, 1.75, 2.5, 3.5)
CCVALS = (1e-5, 1e-6, -4.0, -6.0)


class Sem:
    def __init__(self, es, nc, name):
        self.h = es.enter_context(nc.semaphore(name))
        self.n = 0

    def inc(self, instr, k=1):
        instr.then_inc(self.h, k)
        self.n += k
        return instr


class Prog:
    def __init__(self, ext_in, ext_out):
        self.nc = bass.Bass("TRN2", target_bir_lowering=False, debug=False)
        self.es = contextlib.ExitStack()
        self.ext_in = set(ext_in)
        self.ext_out = set(ext_out)
        self.dram = {}

    def d(self, name, shape, dtype=F32):
        if name not in self.dram:
            kind = "Internal"
            if name in self.ext_in:
                kind = "ExternalInput"
            elif name in self.ext_out:
                kind = "ExternalOutput"
            self.dram[name] = self.nc.dram_tensor(name, list(shape), dtype, kind=kind).ap()
        return self.dram[name]

    def sb(self, es, name, shape, dtype=F32):
        return es.enter_context(self.nc.sbuf_tensor(name, list(shape), dtype)).ap()

    def ps(self, es, name, shape, dtype=F32):
        return es.enter_context(self.nc.psum_tensor(name, list(shape), dtype)).ap()

    def open(self):
        self.es.__enter__()
        self.sld = Sem(self.es, self.nc, "sld")
        self.sgp = Sem(self.es, self.nc, "sgp")
        self.smm = Sem(self.es, self.nc, "smm")
        self.sev = Sem(self.es, self.nc, "sev")
        self.ssc = Sem(self.es, self.nc, "ssc")
        self.pb = self.sld
        g = self.es
        self.pbk = [self.ps(g, f"pbk{i}", [128, 512], F32) for i in range(8)]
        self.pbt = self.pbk[7].bitcast(BF16)
        self.ones_b = self.sb(g, "ones_b", [128, 128], BF16)
        self.ones_f1 = self.sb(g, "ones_f1", [1, 128], F32)
        self.ones_b1 = self.sb(g, "ones_b1", [1, 128], BF16)
        self.ident_b = self.sb(g, "ident_b", [128, 128], BF16)
        self.ident_f = self.sb(g, "ident_f", [128, 128], F32)
        self.mask_lo = self.sb(g, "mask_lo", [128, 128], F32)
        self.mask_ups = self.sb(g, "mask_ups", [128, 128], F32)
        self.mask_upi = self.sb(g, "mask_upi", [128, 128], F32)
        self.ccol_t = self.sb(g, "ccol", [128, len(CCVALS)], F32)
        self.scope = None
        self.block = self.es.enter_context(self.nc.Block())
        sems = [self.sld, self.sgp, self.smm, self.sev, self.ssc]

        def gp0(g2):
            for s in sems:
                g2.sem_clear(s.h)

        def wait0(eng):
            for s in sems:
                eng.wait_op(s.h, 0, "sem-eq")

        self.seq("sem_init", [("gpsimd", gp0), ("sync", wait0), ("vector", wait0),
                              ("scalar", wait0), ("tensor", wait0)])

    def close(self):
        self.es.__exit__(None, None, None)

    def _emit(self, name, steps):
        block = self.block
        reg = {"sync": block.sync, "gpsimd": block.gpsimd, "vector": block.vector,
               "scalar": block.scalar, "tensor": block.tensor}
        tag = name.rstrip("0123456789").rstrip("_")
        tag = tag.split("_")[0] if tag else name
        for ename, f in steps:
            def body(eng, f=f):
                with self.nc.named_scope(self.scope or tag):
                    f(eng)
            reg[ename](body)

    def seq(self, name, steps):
        self._emit(name, steps)
        drip = getattr(self, "_drip", None)
        if drip is not None and not getattr(self, "_dripping", False):
            self._dripping = True
            save = self.scope
            self.scope = getattr(self, "_drip_scope", None)
            try:
                for _ in range(getattr(self, "_drip_k", 1)):
                    try:
                        nm, st = next(drip)
                    except StopIteration:
                        self._drip = None
                        break
                    self._emit(nm, st)
            finally:
                self.scope = save
                self._dripping = False

    def drip_flush(self):
        drip = getattr(self, "_drip", None)
        if drip is None:
            return
        self._dripping = True
        save = self.scope
        self.scope = getattr(self, "_drip_scope", None)
        try:
            for nm, st in drip:
                self._emit(nm, st)
        finally:
            self.scope = save
            self._dripping = False
            self._drip = None

    def newsem(self, name):
        return Sem(self.es, self.nc, name)

    def cc(self, val):
        i = CCVALS.index(val)
        return self.ccol_t[:, i:i + 1]

    def w(self, eng, *sems):
        for s in (self.sld, self.sgp, self.smm, self.sev, self.ssc):
            if s.n:
                eng.wait_ge(s.h, s.n)


def wpack_hbm(whbm):
    return whbm.rearrange("(k p) (m q) -> p k m q", p=128, q=128)


def wv(wsb, k, m, nm):
    c = (k * nm + m) * 128
    return wsb[:, c:c + 128]


def cview(hbm2d, c, w=CH):
    return hbm2d[:, c * w:(c + 1) * w].rearrange("(k p) n -> p k n", p=128)


def sview(sb2d, w=CH):
    return sb2d.rearrange("p (k n) -> p k n", n=w)


def ct(buf, k, w=CH):
    return buf[:, k * w:(k + 1) * w]



def fin(P, tag):
    def syf(sync):
        P.w(sync, P.sld)
        return sync.nop(), 1
    P.seq(f"{tag}_fin", [("sync", syf)])


def ph_const(P):
    def gp(g):
        g.memset(P.ones_b, 1.0)
        g.memset(P.ones_f1, 1.0)
        g.memset(P.ones_b1, 1.0)
        for t in (P.ident_b, P.ident_f):
            g.memset(t, 0.0)
            g.affine_select(out=t, in_=t, compare_op=ALU.not_equal, fill=1.0,
                            base=0, pattern=[[-1, 128]], channel_multiplier=1)
        g.memset(P.mask_lo, 1.0)
        g.affine_select(out=P.mask_lo, in_=P.mask_lo, compare_op=ALU.is_gt,
                        fill=0.0, base=0, pattern=[[-1, 128]], channel_multiplier=1)
        g.memset(P.mask_upi, 1.0)
        g.affine_select(out=P.mask_upi, in_=P.mask_upi, compare_op=ALU.is_ge,
                        fill=0.0, base=0, pattern=[[1, 128]], channel_multiplier=-1)
        g.memset(P.mask_ups, 1.0)
        g.affine_select(out=P.mask_ups, in_=P.mask_ups, compare_op=ALU.is_gt,
                        fill=0.0, base=0, pattern=[[1, 128]], channel_multiplier=-1)
        for i, v in enumerate(CCVALS):
            last = g.memset(P.ccol_t[:, i:i + 1], v)
        return P.sgp.inc(last, 1), 1
    P.seq("const", [("gpsimd", gp)])


def colvec(P, es, name, n, dtype=F32):
    nt = (n + 127) // 128
    pp = 128 if n >= 128 else n
    hv = P.d(name, [n], F32)
    out = P.sb(es, f"cv_{name}", [pp, nt], dtype)

    def sy(sync):
        P.w(sync)
        src_ = hv.rearrange("(p t) -> p t", t=nt)
        return Sem.inc(P.sld, sync.dma_start(out=out, in_=src_), 16), 16

    P.seq(f"cv_{name}", [("sync", sy)])
    return out


def host_colvec(vec):
    v = np.asarray(vec, np.float32)
    n = v.shape[0]
    if n < 128:
        return np.ascontiguousarray(v)
    return np.ascontiguousarray(v.reshape(n // 128, 128).T.reshape(-1))


def load_wpack(P, es, name, hname, nk, nm):
    wsb = P.sb(es, f"wp_{name}", [128, nk * nm * 128], BF16)
    hv = P.d(hname, [nk * 128, nm * 128], F32)

    def gp(g):
        P.w(g)
        dst = wsb.rearrange("p (k m q) -> p k m q", m=nm, q=128)
        return P.sgp.inc(g.dma_start(out=dst, in_=wpack_hbm(hv)), 16), 16
    P.seq(f"wp_{name}", [("gpsimd", gp)])
    return wsb


def ph_ln(P, src, dst_f, dst_b, tag):
    hsrc = P.d(src, [D, L])
    hf = P.d(dst_f, [D, L])
    hb = P.d(dst_b, [D, L], BF16)
    with contextlib.ExitStack() as es:
        xf = P.sb(es, f"{tag}_xf", [128, DT * CH], F32)
        xb = P.sb(es, f"{tag}_xb", [128, DT * CH], BF16)
        sq = P.sb(es, f"{tag}_sq", [128, DT * CH], BF16)
        st = P.sb(es, f"{tag}_st", [128, 3 * CH], F32)
        outf = P.sb(es, f"{tag}_of", [128, DT * CH], F32)
        outb = P.sb(es, f"{tag}_ob", [128, DT * CH], BF16)
        m_t, var_t, inv_t = st[:, 0:CH], st[:, CH:2 * CH], st[:, 2 * CH:3 * CH]

        for c in range(NCH):
            def sy(sync, c=c):
                P.w(sync, P.sev, P.ssc)
                return P.sld.inc(sync.dma_start(out=sview(xf), in_=cview(hsrc, c)), 16), 16

            def gp(g, c=c):
                P.w(g, P.ssc, P.smm)
                return P.sgp.inc(g.dma_start(out=sview(xb), in_=cview(hsrc, c)), 16), 16

            def sc1(v, c=c):
                P.w(v)
                return P.sev.inc(v.tensor_mul(sview(sq), sview(xf), sview(xf)), 1), 1

            def tn(t, c=c):
                P.w(t, P.ssc, P.sev, P.sgp)
                for k in range(DT):
                    t.matmul(P.pbk[0], P.ones_b, ct(xb, k), start=(k == 0), stop=(k == DT - 1))
                for k in range(DT):
                    i2 = t.matmul(P.pbk[1], P.ones_b, ct(sq, k), start=(k == 0), stop=(k == DT - 1))
                return P.smm.inc(i2, 1), 1

            def ve1(v, c=c):
                P.w(v, P.smm)
                v.tensor_scalar_mul(out=m_t, in0=P.pbk[0], scalar1=1.0 / D)
                v.tensor_scalar_mul(out=var_t, in0=P.pbk[1], scalar1=1.0 / D)
                v.scalar_tensor_tensor(out=inv_t, in0=m_t, scalar=-1.0, in1=m_t,
                                       op0=ALU.mult, op1=ALU.mult)
                return P.sev.inc(v.tensor_add(var_t, var_t, inv_t), 1), 1

            def sc2(s, c=c):
                P.w(s, P.sev)
                return P.ssc.inc(s.activation(out=inv_t, in_=var_t, func=AF.Sqrt, bias=P.cc(1e-5)), 1), 1

            def ve2(v, c=c):
                P.w(v, P.ssc)
                v.reciprocal(out=inv_t, in_=inv_t)
                for k in range(DT):
                    v.tensor_sub(ct(outf, k), ct(xf, k), m_t)
                    i2 = v.tensor_mul(ct(outf, k), ct(outf, k), inv_t)
                return P.sev.inc(i2, 1), 1

            def sc3(v, c=c):
                P.w(v)
                i2 = v.tensor_copy(out=outb, in_=outf)
                return P.sev.inc(i2, 1), 1

            def sy2(sync, c=c):
                P.w(sync, P.sev, P.ssc)
                P.sld.inc(sync.dma_start(out=cview(hf, c), in_=sview(outf)), 16)
                return P.sld.inc(sync.dma_start(out=cview(hb, c), in_=sview(outb)), 16), 16

            P.seq(f"{tag}_c{c}", [("sync", sy), ("gpsimd", gp), ("vector", sc1),
                                  ("tensor", tn), ("vector", ve1), ("scalar", sc2),
                                  ("vector", ve2), ("vector", sc3), ("sync", sy2)])

        fin(P, tag)


def ph_rms(P, src_f, wcol, dst_b, tag, eps):
    hsf = P.d(src_f, [D, L])
    hd = P.d(dst_b, [D, L], BF16)
    with contextlib.ExitStack() as es:
        xf = P.sb(es, f"{tag}_xf", [128, DT * CH], F32)
        sq = P.sb(es, f"{tag}_sq", [128, DT * CH], BF16)
        st = P.sb(es, f"{tag}_st", [128, 2 * CH], F32)
        outf = P.sb(es, f"{tag}_of", [128, DT * CH], F32)
        outb = P.sb(es, f"{tag}_ob", [128, DT * CH], BF16)
        inv_t, tmp_t = st[:, 0:CH], st[:, CH:2 * CH]

        for c in range(NCH):
            def sy(sync, c=c):
                P.w(sync, P.sev, P.ssc)
                return P.sld.inc(sync.dma_start(out=sview(xf), in_=cview(hsf, c)), 16), 16

            def sc1(v, c=c):
                P.w(v)
                return P.sev.inc(v.tensor_mul(sview(sq), sview(xf), sview(xf)), 1), 1

            def tn(t, c=c):
                P.w(t, P.ssc, P.sev)
                for k in range(DT):
                    i2 = t.matmul(P.pbk[0], P.ones_b, ct(sq, k), start=(k == 0), stop=(k == DT - 1))
                return P.smm.inc(i2, 1), 1

            def ve1(v, c=c):
                P.w(v, P.smm)
                return P.sev.inc(v.tensor_scalar_mul(out=tmp_t, in0=P.pbk[0], scalar1=1.0 / D), 1), 1

            def sc2(s, c=c):
                P.w(s, P.sev)
                return P.ssc.inc(s.activation(out=inv_t, in_=tmp_t, func=AF.Sqrt, bias=P.cc(eps)), 1), 1

            def ve2(v, c=c):
                P.w(v, P.ssc)
                v.reciprocal(out=inv_t, in_=inv_t)
                for k in range(DT):
                    i2 = v.tensor_mul(ct(outf, k), ct(xf, k), inv_t)
                return P.sev.inc(i2, 1), 1

            def sc3(v, c=c):
                P.w(v)
                for k in range(DT):
                    i2 = v.tensor_scalar(out=ct(outb, k), in0=ct(outf, k),
                                         scalar1=wcol[:, k:k + 1], scalar2=None,
                                         op0=ALU.mult)
                return P.sev.inc(i2, 1), 1

            def sy2(sync, c=c):
                P.w(sync, P.ssc)
                i2 = P.sld.inc(sync.dma_start(out=cview(hd, c), in_=sview(outb)), 16)
                if DBG_RMS and c == 0:
                    hdbg = P.d(f"dbg_{tag}", [128, 2 * CH])
                    i2 = P.sld.inc(sync.dma_start(out=hdbg, in_=st), 16)
                return i2, 16

            P.seq(f"{tag}_c{c}", [("sync", sy), ("vector", sc1), ("tensor", tn),
                                  ("vector", ve1), ("scalar", sc2), ("vector", ve2),
                                  ("vector", sc3), ("sync", sy2)])

        fin(P, tag)


def ph_l0_projscan(P):
    hnin = P.d("nin_b", [D, L], BF16)
    hhx = P.d("hx_f", [D, L])
    with contextlib.ExitStack() as es:
        wg = load_wpack(P, es, "wg", "wgT", DT, DT)
        wc = load_wpack(P, es, "wc", "wcT", DT, DT)
        mov = P.sb(es, "psn_mov", [128, DT * CH], BF16)
        beta = P.sb(es, "psn_beta", [128, DT * CH], F32)
        af = P.sb(es, "psn_a", [128, DT * CH], F32)
        xs = P.sb(es, "psn_xs", [128, DT * CH], F32)
        hx = P.sb(es, "psn_hx", [128, DT * CH], F32)
        carry = P.sb(es, "psn_carry", [128, DT], F32)

        def gp0(g):
            P.w(g)
            return P.sgp.inc(g.memset(carry, 0.0), 1), 1
        P.seq("scan_init", [("gpsimd", gp0)])

        for c in range(NCH):
            steps = []

            def sy(sync, c=c):
                P.w(sync, P.smm)
                return P.sld.inc(sync.dma_start(out=sview(mov), in_=cview(hnin, c)), 16), 16
            steps.append(("sync", sy))

            for half in range(2):
                def tn_b(t, c=c, half=half):
                    P.w(t, P.sld, P.sgp, P.ssc, P.sev)
                    for j in range(4):
                        m = half * 4 + j
                        for k in range(DT):
                            i2 = t.matmul(P.pbk[j], wv(wg, k, m, DT), ct(mov, k),
                                          start=(k == 0), stop=(k == DT - 1))
                    return P.smm.inc(i2, 1), 1

                def sc_b(s, c=c, half=half):
                    P.w(s, P.smm)
                    for j in range(4):
                        m = half * 4 + j
                        i2 = s.activation(out=ct(beta, m), in_=P.pbk[j], func=AF.Sigmoid)
                    return P.ssc.inc(i2, 1), 1
                steps += [("tensor", tn_b), ("scalar", sc_b)]

            for half in range(2):
                def tn_h(t, c=c, half=half):
                    P.w(t, P.ssc, P.sev)
                    for j in range(4):
                        m = half * 4 + j
                        for k in range(DT):
                            i2 = t.matmul(P.pbk[j], wv(wc, k, m, DT), ct(mov, k),
                                          start=(k == 0), stop=(k == DT - 1))
                    return P.smm.inc(i2, 1), 1

                def ve_h(v, c=c, half=half):
                    P.w(v, P.smm)
                    for j in range(4):
                        m = half * 4 + j
                        i2 = v.tensor_mul(ct(xs, m), ct(beta, m), P.pbk[j])
                    return P.sev.inc(i2, 1), 1
                steps += [("tensor", tn_h), ("vector", ve_h)]

            def ve_s(v, c=c):
                P.w(v)
                for m in range(DT):
                    v.tensor_scalar(out=ct(af, m), in0=ct(beta, m), scalar1=-1.0,
                                    scalar2=1.0, op0=ALU.mult, op1=ALU.add)
                for m in range(DT):
                    i2 = v.tensor_tensor_scan(out=ct(hx, m), data0=ct(af, m),
                                              data1=ct(xs, m), initial=carry[:, m:m + 1],
                                              op0=ALU.mult, op1=ALU.add)
                return P.sev.inc(i2, 1), 1
            steps.append(("vector", ve_s))

            def sc_carry(s, c=c):
                P.w(s)
                lastcols = hx.rearrange("p (m n) -> p m n", n=CH)[:, :, CH - 1:CH]
                i2 = s.activation(out=carry.rearrange("p (m o) -> p m o", o=1),
                                  in_=lastcols, func=AF.Copy)
                return P.ssc.inc(i2, 1), 1
            steps.append(("scalar", sc_carry))

            def sy2(sync, c=c):
                P.w(sync, P.sev)
                i2 = P.sld.inc(sync.dma_start(out=cview(hhx, c), in_=sview(hx)), 16)
                if DBG_RMS:
                    hdc = P.d("dbg_carry", [NCH, 128, DT])
                    i2 = P.sld.inc(sync.dma_start(out=hdc[c], in_=carry), 16)
                return i2, 16
            steps.append(("sync", sy2))
            P.seq(f"pscan_c{c}", steps)

        fin(P, "pscan")


def ph_l0_mlp1(P):
    hhn = P.d("hn_b", [D, L], BF16)
    hout = P.d("ffh_b", [D, L], BF16)
    with contextlib.ExitStack() as es:
        w1 = load_wpack(P, es, "w1", "w1T", DT, DT)
        w3 = load_wpack(P, es, "w3", "w3T", DT, DT)
        mov = P.sb(es, "m1_mov", [128, DT * CH], BF16)
        u1 = P.sb(es, "m1_u1", [128, DT * CH], F32)
        outb = P.sb(es, "m1_ob", [128, DT * CH], BF16)

        for c in range(NCH):
            steps = []

            def sy(sync, c=c):
                P.w(sync, P.smm)
                return P.sld.inc(sync.dma_start(out=sview(mov), in_=cview(hhn, c)), 16), 16
            steps.append(("sync", sy))

            for half in range(2):
                def tn1(t, c=c, half=half):
                    P.w(t, P.sld, P.sgp, P.ssc, P.sev)
                    for j in range(4):
                        m = half * 4 + j
                        for k in range(DT):
                            i2 = t.matmul(P.pbk[j], wv(w1, k, m, DT), ct(mov, k),
                                          start=(k == 0), stop=(k == DT - 1))
                    return P.smm.inc(i2, 1), 1

                def sc1(s, c=c, half=half):
                    P.w(s, P.smm)
                    for j in range(4):
                        m = half * 4 + j
                        i2 = s.activation(out=ct(u1, m), in_=P.pbk[j], func=AF.Silu)
                    return P.ssc.inc(i2, 1), 1
                steps += [("tensor", tn1), ("scalar", sc1)]

            for half in range(2):
                def tn3(t, c=c, half=half):
                    P.w(t, P.ssc, P.sev)
                    for j in range(4):
                        m = half * 4 + j
                        for k in range(DT):
                            i2 = t.matmul(P.pbk[j], wv(w3, k, m, DT), ct(mov, k),
                                          start=(k == 0), stop=(k == DT - 1))
                    return P.smm.inc(i2, 1), 1

                def ve3(v, c=c, half=half):
                    P.w(v, P.smm)
                    for j in range(4):
                        m = half * 4 + j
                        i2 = v.tensor_mul(ct(outb, m), ct(u1, m), P.pbk[j])
                    return P.sev.inc(i2, 1), 1
                steps += [("tensor", tn3), ("vector", ve3)]

            def sy2(sync, c=c):
                P.w(sync, P.sev)
                return P.sld.inc(sync.dma_start(out=cview(hout, c), in_=sview(outb)), 16), 16
            steps.append(("sync", sy2))
            P.seq(f"mlp1_c{c}", steps)

        fin(P, "mlp1")


def ph_l0_mlp2(P):
    hff = P.d("ffh_b", [D, L], BF16)
    hhx = P.d("hx_f", [D, L])
    hs0 = P.d("s0_f", [D, L])
    with contextlib.ExitStack() as es:
        w2 = load_wpack(P, es, "w2", "w2T", DT, DT)
        mov = P.sb(es, "m2_mov", [128, DT * CH], BF16)
        hxs = P.sb(es, "m2_hx", [128, DT * CH], F32)
        outf = P.sb(es, "m2_of", [128, DT * CH], F32)

        for c in range(NCH):
            steps = []

            def sy(sync, c=c):
                P.w(sync, P.smm, P.sev)
                P.sld.inc(sync.dma_start(out=sview(mov), in_=cview(hff, c)), 16)
                return P.sld.inc(sync.dma_start(out=sview(hxs), in_=cview(hhx, c)), 16), 16
            steps.append(("sync", sy))

            for half in range(2):
                def tn(t, c=c, half=half):
                    P.w(t, P.sld, P.sgp, P.sev)
                    for j in range(4):
                        m = half * 4 + j
                        for k in range(DT):
                            i2 = t.matmul(P.pbk[j], wv(w2, k, m, DT), ct(mov, k),
                                          start=(k == 0), stop=(k == DT - 1))
                    return P.smm.inc(i2, 1), 1

                def ve(v, c=c, half=half):
                    P.w(v, P.smm)
                    for j in range(4):
                        m = half * 4 + j
                        i2 = v.tensor_add(ct(outf, m), ct(hxs, m), P.pbk[j])
                    return P.sev.inc(i2, 1), 1
                steps += [("tensor", tn), ("vector", ve)]

            def sy2(sync, c=c):
                P.w(sync, P.sev)
                return P.sld.inc(sync.dma_start(out=cview(hs0, c), in_=sview(outf)), 16), 16
            steps.append(("sync", sy2))
            P.seq(f"mlp2_c{c}", steps)

        fin(P, "mlp2")



def emit_quant_steps2(P, steps, xin, xout, tmps, nblk, pp=128):
    W = nblk * 16
    xa, sg, vm, cnt, r1, scl, rs = (tmps[k] for k in
                                    ("xa", "sg", "vm", "cnt", "r1", "scl", "rs"))

    def sA(v):
        P.w(v)
        v.tensor_scalar_mul(out=xa[:, 0:W], in0=xin, scalar1=-1.0)
        v.tensor_max(xa[:, 0:W], xa[:, 0:W], xin)
        return P.sev.inc(v.tensor_scalar(out=sg[:, 0:W], in0=xin, scalar1=0.0,
                                         scalar2=None, op0=ALU.is_gt), 1), 1

    def vA(v):
        P.w(v, P.ssc)
        xav = xa[:, 0:W].rearrange("p (b s) -> p b s", s=16)
        v.tensor_reduce(out=scl[:, 0:nblk], in_=xav, axis=AX.X, op=ALU.max,
                        apply_absolute_value=True)
        v.tensor_scalar(out=scl[:, 0:nblk], in0=scl[:, 0:nblk], scalar1=1.0 / 6.0,
                        scalar2=1e-12, op0=ALU.mult, op1=ALU.max)
        v.reciprocal(out=rs[:, 0:nblk], in_=scl[:, 0:nblk])
        rsb = rs[:, 0:nblk].broadcast_to([pp, nblk, 16])
        vmv = vm[:, 0:W].rearrange("p (b s) -> p b s", s=16)
        v.tensor_tensor(out=vmv, in0=xav, in1=rsb, op=ALU.mult)
        v.tensor_scalar(out=cnt[:, 0:W], in0=vm[:, 0:W], scalar1=0.25, scalar2=None,
                        op0=ALU.is_gt)
        for t in QTHR:
            v.scalar_tensor_tensor(out=cnt[:, 0:W], in0=vm[:, 0:W], scalar=t,
                                   in1=cnt[:, 0:W], op0=ALU.is_gt, op1=ALU.add)
        i2 = v.scalar_tensor_tensor(out=cnt[:, 0:W], in0=vm[:, 0:W], scalar=5.0,
                                    in1=cnt[:, 0:W], op0=ALU.is_gt, op1=ALU.add)
        return P.sev.inc(i2, 1), 1

    def sB(s):
        P.w(s, P.sev)
        s.activation(out=r1[:, 0:W], in_=cnt[:, 0:W], func=AF.Relu, bias=P.cc(-4.0))
        return P.ssc.inc(s.activation(out=vm[:, 0:W], in_=cnt[:, 0:W], func=AF.Relu, bias=P.cc(-6.0)), 1), 1

    def vB(v):
        P.w(v, P.ssc)
        v.tensor_add(cnt[:, 0:W], cnt[:, 0:W], r1[:, 0:W])
        v.scalar_tensor_tensor(out=cnt[:, 0:W], in0=cnt[:, 0:W], scalar=0.5,
                               in1=vm[:, 0:W], op0=ALU.mult, op1=ALU.add)
        hsb = scl[:, 0:nblk].broadcast_to([pp, nblk, 16])
        cntv = cnt[:, 0:W].rearrange("p (b s) -> p b s", s=16)
        v.tensor_tensor(out=cntv, in0=cntv, in1=hsb, op=ALU.mult)
        v.tensor_mul(r1[:, 0:W], cnt[:, 0:W], sg[:, 0:W])
        return P.sev.inc(v.scalar_tensor_tensor(out=xout, in0=r1[:, 0:W], scalar=2.0,
                                                in1=cnt[:, 0:W], op0=ALU.mult,
                                                op1=ALU.subtract), 1), 1

    steps += [("vector", sA), ("vector", vA), ("scalar", sB), ("vector", vB)]


def mk_qtmp(P, es, tag, W, pp=128):
    t = {}
    for nm in ("xa", "sg", "vm", "cnt", "r1"):
        t[nm] = P.sb(es, f"{tag}_{nm}", [pp, W], F32)
    for nm in ("scl", "rs"):
        t[nm] = P.sb(es, f"{tag}_{nm}", [pp, W // 16], F32)
    return t


def gen_fp4_weights(P):
    specs = [("wgate", IFF, D, "qgT_b"), ("wup", IFF, D, "quT_b"), ("wdown", D, IFF, "qdT_b")]
    with contextlib.ExitStack() as es0:
        wq = P.sb(es0, "q_w", [128, 22 * 1024], BF16)
        wfull = P.sb(es0, "f_w", [128, IFF], F32)
        tmps = mk_qtmp(P, es0, "qt_w", IFF // 2)
        rowbuf = P.sb(es0, "rb_w", [128, IFF], BF16)
        s_wf = P.newsem("fwld")
        s_rb = P.newsem("fwrb")
        pb2 = [P.pbk[6].bitcast(BF16), P.pbt]
        roll = {"vA": 0, "sB": 0, "vB": 0, "ve20": 0, "ve21": 0}
        rTR = {}
        first = [True]
        for si, (hname, R, Cc, oname) in enumerate(specs):
            nr, ncc = R // 128, Cc // 128
            hsrc = P.d(hname, [R, Cc], F32)
            hdst = P.d(oname, [Cc, R], BF16)
            wf = wfull[:, 0:Cc]

            for i in range(nr):
                steps = []

                def sy(sync, i=i, hsrc=hsrc, wf=wf):
                    if first[0]:
                        P.w(sync)
                        first[0] = False
                    sync.wait_ge(P.sev.h, roll["vB"])
                    src = hsrc[i * 128:(i + 1) * 128, :]
                    return s_wf.inc(sync.dma_start(out=wf, in_=src), 16), 16
                steps.append((f"fwl_{hname}_{i}", "sync", sy))
                nh = 2 if Cc > IFF // 2 else 1
                hw = Cc // nh

                def qg(si=si):
                    return rTR.get(si - 1)
                for h in range(nh):
                    qsteps = []
                    emit_quant_steps3(P, qsteps, wf[:, h * hw:(h + 1) * hw],
                                      wq[:, i * Cc + h * hw: i * Cc + (h + 1) * hw],
                                      tmps, hw // 16, roll, qg, f"fwq_{hname}_{i}_{h}")
                    if h == 0:
                        nm0, en0, f0 = qsteps[0]

                        def wrapped(v, f0=f0):
                            v.wait_ge(s_wf.h, s_wf.n)
                            return f0(v)
                        qsteps[0] = (nm0, en0, wrapped)
                    steps += qsteps
                for nm, en, f in steps:
                    yield nm, [(en, f)]

            for m in range(ncc):
                for gi, w0 in enumerate(range(0, nr, 8)):
                    wn = min(8, nr - w0)

                    def tn(t, m=m, w0=w0, wn=wn, gi=gi, Cc=Cc, si=si):
                        t.wait_ge(P.sev.h, roll["vB"])
                        if roll[f"ve2{gi % 2}"]:
                            t.wait_ge(P.sev.h, roll[f"ve2{gi % 2}"])
                        pb = pb2[gi % 2]
                        for j in range(wn):
                            i = w0 + j
                            i2 = t.transpose(pb[:, j * 128:(j + 1) * 128],
                                             wq[:, i * Cc + m * 128: i * Cc + (m + 1) * 128],
                                             P.ident_b)
                        P.smm.inc(i2, 1)
                        rTR[si] = P.smm.n
                        return i2, 1

                    def ve(v, m=m, w0=w0, wn=wn, gi=gi):
                        v.wait_ge(P.smm.h, P.smm.n)
                        v.wait_ge(s_rb.h, s_rb.n)
                        i2 = v.tensor_copy(out=rowbuf[:, w0 * 128:(w0 + wn) * 128],
                                           in_=pb2[gi % 2][:, 0:wn * 128])
                        P.sev.inc(i2, 1)
                        roll[f"ve2{gi % 2}"] = P.sev.n
                        return i2, 1
                    yield f"fwt_{hname}_{m}_{w0}", [("tensor", tn), ("vector", ve)]

                def sy2(sync, m=m, hdst=hdst, R=R):
                    sync.wait_ge(P.sev.h, max(roll["ve20"], roll["ve21"]))
                    dst = hdst[m * 128:(m + 1) * 128, :]
                    i2 = sync.dma_start(out=dst, in_=rowbuf[:, 0:R])
                    return s_rb.inc(i2, 16), 16
                yield f"fws_{hname}_{m}", [("sync", sy2)]

        def syf(sync):
            sync.wait_ge(s_rb.h, s_rb.n)
            return sync.nop(), 1
        yield "fw_fin", [("sync", syf)]


def ph_fp4_x(P):
    hx = P.d("xn_f", [D, L])
    hq = P.d("xq_b", [D, L], BF16)
    with contextlib.ExitStack() as es:
        xf = P.sb(es, "qx_xf", [128, DT * CH], F32)
        xtm = P.sb(es, "qx_tm", [128, D], F32)
        qtm = P.sb(es, "qx_qtm", [128, D], BF16)
        outb = P.sb(es, "qx_ob", [128, DT * CH], BF16)
        tmps = mk_qtmp(P, es, "qx_t", D)

        for c in range(NCH):
            def sy(sync, c=c):
                P.w(sync, P.smm)
                return P.sld.inc(sync.dma_start(out=sview(xf), in_=cview(hx, c)), 16), 16
            P.seq(f"qx_ld{c}", [("sync", sy)])

            for lt in range(4):
                def tn1(t, c=c, lt=lt):
                    P.w(t, P.sld, P.sev)
                    for k in range(DT):
                        bank = P.pbk[4 + k // 4]
                        i2 = t.transpose(bank[:, (k % 4) * 128:(k % 4 + 1) * 128],
                                         xf[:, k * CH + lt * 128: k * CH + lt * 128 + 128],
                                         P.ident_f)
                    return P.smm.inc(i2, 1), 1

                def ve1(v, c=c, lt=lt):
                    P.w(v, P.smm)
                    v.tensor_copy(out=xtm[:, 0:512], in_=P.pbk[4])
                    i2 = v.tensor_copy(out=xtm[:, 512:1024], in_=P.pbk[5])
                    return P.sev.inc(i2, 1), 1
                steps = [("tensor", tn1), ("vector", ve1)]
                emit_quant_steps2(P, steps, xtm, qtm, tmps, D // 16)

                def tn2(t, c=c, lt=lt):
                    P.w(t, P.sev)
                    for k in range(DT):
                        i2 = t.transpose(P.pbt[:, k * 128:(k + 1) * 128],
                                         qtm[:, k * 128:(k + 1) * 128], P.ident_b)
                    return P.smm.inc(i2, 1), 1

                def ve2(v, c=c, lt=lt):
                    P.w(v, P.smm)
                    for k in range(DT):
                        i2 = v.tensor_copy(out=outb[:, k * CH + lt * 128: k * CH + lt * 128 + 128],
                                           in_=P.pbt[:, k * 128:(k + 1) * 128])
                    return P.sev.inc(i2, 1), 1
                steps += [("tensor", tn2), ("vector", ve2)]
                P.seq(f"qx_c{c}_l{lt}", steps)

            def sy2(sync, c=c):
                P.w(sync, P.sev)
                return P.sld.inc(sync.dma_start(out=cview(hq, c), in_=sview(outb)), 16), 16
            P.seq(f"qx_st{c}", [("sync", sy2)])

        fin(P, "qx")


def ph_fp4_mlp(P):
    hxq = P.d("xq_b", [D, L], BF16)
    hqg = P.d("qgT_b", [D, IFF], BF16)
    hqu = P.d("quT_b", [D, IFF], BF16)
    hout = P.d("hq_b", [IFF, L], BF16)
    with contextlib.ExitStack() as es:
        mov = P.sb(es, "fm_mov", [128, DT * CH], BF16)
        wgw = P.sb(es, "fm_wg", [128, DT * 4 * 128], BF16)
        wuw = P.sb(es, "fm_wu", [128, DT * 4 * 128], BF16)
        gf = P.sb(es, "fm_gf", [128, 4 * CH], F32)
        hf = P.sb(es, "fm_hf", [128, ITT * CH], F32)
        htm = P.sb(es, "fm_htm", [128, IFF], F32)
        qtm = P.sb(es, "fm_qtm", [128, IFF], BF16)
        outb = P.sb(es, "fm_ob", [128, ITT * CH], BF16)
        tmps = mk_qtmp(P, es, "fm_t", IFF)
        NW = (ITT + 3) // 4

        def wslice(h, m0, mn):
            return h[:, m0 * 128:(m0 + mn) * 128].rearrange(
                "(k p) (m q) -> p k m q", p=128, q=128)

        for c in range(NCH):
            def sy(sync, c=c):
                P.w(sync, P.smm)
                return P.sld.inc(sync.dma_start(out=sview(mov), in_=cview(hxq, c)), 16), 16
            P.seq(f"fm_ld{c}", [("sync", sy)])

            for w in range(NW):
                m0 = w * 4
                mn = min(4, ITT - m0)

                def syw(sync, c=c, m0=m0, mn=mn):
                    P.w(sync, P.smm)
                    dg = wgw[:, 0:DT * mn * 128].rearrange("p (k m q) -> p k m q", m=mn, q=128)
                    du = wuw[:, 0:DT * mn * 128].rearrange("p (k m q) -> p k m q", m=mn, q=128)
                    P.sld.inc(sync.dma_start(out=dg, in_=wslice(hqg, m0, mn)), 16)
                    return P.sld.inc(sync.dma_start(out=du, in_=wslice(hqu, m0, mn)), 16), 16

                def tn_g(t, c=c, m0=m0, mn=mn):
                    P.w(t, P.sld, P.ssc, P.sev)
                    for j in range(mn):
                        for k in range(DT):
                            i2 = t.matmul(P.pbk[j], wv(wgw, k, j, mn), ct(mov, k),
                                          start=(k == 0), stop=(k == DT - 1))
                    return P.smm.inc(i2, 1), 1

                def sc_g(s, c=c, m0=m0, mn=mn):
                    P.w(s, P.smm)
                    for j in range(mn):
                        i2 = s.activation(out=ct(gf, j), in_=P.pbk[j], func=AF.Silu)
                    return P.ssc.inc(i2, 1), 1

                def tn_u(t, c=c, m0=m0, mn=mn):
                    P.w(t, P.ssc, P.sev)
                    for j in range(mn):
                        for k in range(DT):
                            i2 = t.matmul(P.pbk[j], wv(wuw, k, j, mn), ct(mov, k),
                                          start=(k == 0), stop=(k == DT - 1))
                    return P.smm.inc(i2, 1), 1

                def ve_u(v, c=c, m0=m0, mn=mn):
                    P.w(v, P.smm)
                    for j in range(mn):
                        i2 = v.tensor_mul(ct(hf, m0 + j), ct(gf, j), P.pbk[j])
                    return P.sev.inc(i2, 1), 1

                P.seq(f"fm_c{c}_w{w}", [("sync", syw), ("tensor", tn_g), ("scalar", sc_g),
                                        ("tensor", tn_u), ("vector", ve_u)])

            for lt in range(4):
                steps = []
                for w0 in range(0, ITT, 4):
                    wn = min(4, ITT - w0)

                    def tn1(t, c=c, lt=lt, w0=w0, wn=wn):
                        P.w(t, P.sev)
                        for j in range(wn):
                            k = w0 + j
                            i2 = t.transpose(P.pbk[4][:, j * 128:(j + 1) * 128],
                                             hf[:, k * CH + lt * 128: k * CH + lt * 128 + 128],
                                             P.ident_f)
                        return P.smm.inc(i2, 1), 1

                    def ve1(v, c=c, lt=lt, w0=w0, wn=wn):
                        P.w(v, P.smm)
                        i2 = v.tensor_copy(out=htm[:, w0 * 128:(w0 + wn) * 128],
                                           in_=P.pbk[4][:, 0:wn * 128])
                        return P.sev.inc(i2, 1), 1
                    steps += [("tensor", tn1), ("vector", ve1)]
                emit_quant_steps2(P, steps, htm, qtm, tmps, IFF // 16)
                for w0 in range(0, ITT, 8):
                    wn = min(8, ITT - w0)

                    def tn2(t, c=c, lt=lt, w0=w0, wn=wn):
                        P.w(t, P.sev)
                        for j in range(wn):
                            k = w0 + j
                            i2 = t.transpose(P.pbt[:, j * 128:(j + 1) * 128],
                                             qtm[:, k * 128:(k + 1) * 128], P.ident_b)
                        return P.smm.inc(i2, 1), 1

                    def ve2(v, c=c, lt=lt, w0=w0, wn=wn):
                        P.w(v, P.smm)
                        for j in range(wn):
                            k = w0 + j
                            i2 = v.tensor_copy(
                                out=outb[:, k * CH + lt * 128: k * CH + lt * 128 + 128],
                                in_=P.pbt[:, j * 128:(j + 1) * 128])
                        return P.sev.inc(i2, 1), 1
                    steps += [("tensor", tn2), ("vector", ve2)]
                P.seq(f"fm_q_c{c}_l{lt}", steps)

            def sy2(sync, c=c):
                P.w(sync, P.sev)
                return P.sld.inc(sync.dma_start(
                    out=hout[:, c * CH:(c + 1) * CH].rearrange("(k p) n -> p k n", p=128),
                    in_=outb.rearrange("p (k n) -> p k n", n=CH)), 16), 16
            P.seq(f"fm_st{c}", [("sync", sy2)])

        fin(P, "fm")


def ph_l0_down(P):
    hhq = P.d("hq_b", [IFF, L], BF16)
    hqd = P.d("qdT_b", [IFF, D], BF16)
    hx0 = P.d("x0", [D, L])
    hs0 = P.d("s0_f", [D, L])
    hx1 = P.d("x1_f", [D, L])
    with contextlib.ExitStack() as es:
        qd = P.sb(es, "dn_w", [128, ITT * DT * 128], BF16)
        mov = P.sb(es, "dn_mov", [128, ITT * CH], BF16)
        x0s = P.sb(es, "dn_x0", [128, DT * CH], F32)
        s0s = P.sb(es, "dn_s0", [128, DT * CH], F32)
        gs = P.sb(es, "dn_gs", [128, 4 * CH], F32)
        outf = P.sb(es, "dn_of", [128, DT * CH], F32)

        def gpw(g):
            P.w(g)
            dst = qd.rearrange("p (k m q) -> p k m q", m=DT, q=128)
            return P.sgp.inc(g.dma_start(out=dst, in_=wpack_hbm(hqd)), 16), 16
        P.seq("dn_w", [("gpsimd", gpw)])

        for c in range(NCH):
            steps = []

            def sy(sync, c=c):
                P.w(sync, P.smm, P.sev)
                P.sld.inc(sync.dma_start(out=sview(mov), in_=cview(hhq, c)), 16)
                P.sld.inc(sync.dma_start(out=sview(x0s), in_=cview(hx0, c)), 16)
                return P.sld.inc(sync.dma_start(out=sview(s0s), in_=cview(hs0, c)), 16), 16
            steps.append(("sync", sy))

            for half in range(2):
                def tn(t, c=c, half=half):
                    P.w(t, P.sld, P.sgp, P.sev, P.ssc)
                    for j in range(4):
                        m = half * 4 + j
                        for k in range(ITT):
                            i2 = t.matmul(P.pbk[j], wv(qd, k, m, DT), ct(mov, k),
                                          start=(k == 0), stop=(k == ITT - 1))
                    return P.smm.inc(i2, 1), 1

                def sc(s, c=c, half=half):
                    P.w(s, P.smm)
                    for j in range(4):
                        i2 = s.activation(out=ct(gs, j), in_=P.pbk[j], func=AF.Silu)
                    return P.ssc.inc(i2, 1), 1

                def ve(v, c=c, half=half):
                    P.w(v, P.ssc)
                    for j in range(4):
                        m = half * 4 + j
                        v.tensor_mul(ct(outf, m), ct(s0s, m), ct(gs, j))
                        i2 = v.tensor_add(ct(outf, m), ct(outf, m), ct(x0s, m))
                    return P.sev.inc(i2, 1), 1
                steps += [("tensor", tn), ("scalar", sc), ("vector", ve)]

            def sy2(sync, c=c):
                P.w(sync, P.sev)
                return P.sld.inc(sync.dma_start(out=cview(hx1, c), in_=sview(outf)), 16), 16
            steps.append(("sync", sy2))
            P.seq(f"dn_c{c}", steps)

        fin(P, "dn")




def ph_l1_qkv(P):
    hx1 = P.d("xn1_b", [D, L], BF16)
    outs = {"q": P.d("qpre_f", [D, L]), "k": P.d("kpre_f", [D, L]),
            "v": P.d("vpre_f", [D, L])}
    hbeta = P.d("beta_f", [H, L])
    hpadj = P.d("padj_f", [H, L])
    hcmass = P.d("cmass_f", [H, L])
    with contextlib.ExitStack() as es:
        ws = {nm: load_wpack(P, es, f"l1{nm}", w, DT, DT)
              for nm, w in (("q", "wq"), ("k", "wk"), ("v", "wv"))}
        wbs = P.sb(es, "l1_wb", [128, DT * H], BF16)
        igs = P.sb(es, "l1_ig", [128, DT * H], BF16)
        igb_c = colvec(P, es, "igb", H)
        mov = P.sb(es, "l1q_mov", [128, DT * CH], BF16)
        outf = P.sb(es, "l1q_of", [128, DT * CH], F32)
        small = P.sb(es, "l1q_sm", [H, 3 * CH], F32)

        def gpw(g):
            P.w(g)
            for t, hn in ((wbs, "wb"), (igs, "igw")):
                hv = P.d(hn, [D, H], F32)
                i2 = g.dma_start(out=t.rearrange("p (k m) -> p k m", m=H),
                                 in_=hv.rearrange("(k p) m -> p k m", p=128))
                P.sgp.inc(i2, 16)
            return i2, 16
        P.seq("l1_smallw", [("gpsimd", gpw)])

        for c in range(NCH):
            def sy(sync, c=c):
                P.w(sync)
                return P.sld.inc(sync.dma_start(out=sview(mov), in_=cview(hx1, c)), 16), 16
            P.seq(f"l1q_ld{c}", [("sync", sy)])

            for nm in ("q", "k", "v"):
                for half in range(2):
                    def tn(t, c=c, nm=nm, half=half):
                        P.w(t)
                        for j in range(4):
                            m = half * 4 + j
                            for k in range(DT):
                                i2 = t.matmul(P.pbk[j], wv(ws[nm], k, m, DT), ct(mov, k),
                                              start=(k == 0), stop=(k == DT - 1))
                        return P.smm.inc(i2, 1), 1

                    def ve(v, c=c, nm=nm, half=half):
                        P.w(v)
                        for j in range(4):
                            m = half * 4 + j
                            i2 = v.tensor_copy(out=ct(outf, m), in_=P.pbk[j])
                        return P.sev.inc(i2, 1), 1
                    P.seq(f"l1q_{nm}{c}h{half}", [("tensor", tn), ("vector", ve)])

                def sy2(sync, c=c, nm=nm):
                    P.w(sync)
                    return P.sld.inc(sync.dma_start(out=cview(outs[nm], c), in_=sview(outf)), 16), 16
                P.seq(f"l1q_st_{nm}{c}", [("sync", sy2)])

            def tn_b(t, c=c):
                P.w(t)
                for k in range(DT):
                    t.matmul(P.pbk[0][0:H, :], wbs[:, k * H:(k + 1) * H], ct(mov, k),
                             start=(k == 0), stop=(k == DT - 1))
                for k in range(DT):
                    i2 = t.matmul(P.pbk[1][0:H, :], igs[:, k * H:(k + 1) * H], ct(mov, k),
                                  start=(k == 0), stop=(k == DT - 1))
                return P.smm.inc(i2, 1), 1

            def sc_b(s, c=c):
                P.w(s)
                s.activation(out=small[:, 0:CH], in_=P.pbk[0][0:H, :], func=AF.Sigmoid)
                i2 = s.activation(out=small[:, CH:2 * CH], in_=P.pbk[1][0:H, :],
                                  func=AF.Sigmoid, bias=igb_c)
                return P.ssc.inc(i2, 1), 1

            def ve_b(v, c=c):
                P.w(v)
                pv = small[:, CH:2 * CH]
                pa = small[:, 2 * CH:3 * CH]
                v.tensor_scalar(out=pa, in0=pv, scalar1=0.09, scalar2=0.9,
                                op0=ALU.mult, op1=ALU.add)
                v.tensor_mul(pa, pa, pv)
                i2 = v.tensor_scalar(out=pv, in0=pa, scalar1=-1.0, scalar2=1.0,
                                     op0=ALU.mult, op1=ALU.add)
                return P.sev.inc(i2, 1), 1

            def sy_b(sync, c=c):
                P.w(sync)
                P.sld.inc(sync.dma_start(out=hbeta[:, c * CH:(c + 1) * CH], in_=small[:, 0:CH]), 16)
                P.sld.inc(sync.dma_start(out=hpadj[:, c * CH:(c + 1) * CH], in_=small[:, 2 * CH:3 * CH]), 16)
                return P.sld.inc(sync.dma_start(out=hcmass[:, c * CH:(c + 1) * CH], in_=small[:, CH:2 * CH]), 16), 16
            P.seq(f"l1q_beta{c}", [("tensor", tn_b), ("scalar", sc_b), ("vector", ve_b), ("sync", sy_b)])

        fin(P, "l1q")


def ph_l1_conv(P):
    PAD = 32
    specs = [("qpre_f", "qcw", "qs_f"), ("kpre_f", "kcw", "ks_f"), ("vpre_f", "vcw", "vs_f")]
    with contextlib.ExitStack() as es:
        xt = P.sb(es, "cv4_x", [128, PAD + L], F32)
        acc = P.sb(es, "cv4_a", [128, L], F32)
        wc = {w: colvec2d(P, es, w, 4) for _, w, _ in specs}

        def gp0(g):
            P.w(g)
            return P.sgp.inc(g.memset(xt[:, 0:PAD], 0.0), 1), 1
        P.seq("cv4_pad", [("gpsimd", gp0)])

        for src, w, dst in specs:
            hs = P.d(src, [D, L])
            hd = P.d(dst, [D, L])
            for dt in range(DT):
                def sy(sync, src=src, dt=dt):
                    P.w(sync)
                    return P.sld.inc(sync.dma_start(
                        out=xt[:, PAD:PAD + L],
                        in_=P.d(src, [D, L])[dt * 128:(dt + 1) * 128, :]), 16), 16

                def ve(v, w=w, dt=dt):
                    P.w(v)
                    K = 4
                    for k in range(K):
                        sh = xt[:, PAD - (K - 1) + k: PAD - (K - 1) + k + L]
                        col = wc[w][:, dt * K + k: dt * K + k + 1]
                        if k == 0:
                            i2 = v.tensor_scalar(out=acc, in0=sh, scalar1=col,
                                                 scalar2=None, op0=ALU.mult)
                        else:
                            i2 = v.scalar_tensor_tensor(out=acc, in0=sh, scalar=col,
                                                        in1=acc, op0=ALU.mult, op1=ALU.add)
                    return P.sev.inc(i2, 1), 1

                def sc(s, dt=dt):
                    P.w(s)
                    return P.ssc.inc(s.activation(out=xt[:, PAD:PAD + L], in_=acc, func=AF.Silu), 1), 1

                def sy2(sync, dst=dst, dt=dt):
                    P.w(sync)
                    return P.sld.inc(sync.dma_start(
                        out=P.d(dst, [D, L])[dt * 128:(dt + 1) * 128, :],
                        in_=xt[:, PAD:PAD + L]), 16), 16
                P.seq(f"cv4_{src}_{dt}", [("sync", sy), ("vector", ve), ("scalar", sc), ("sync", sy2)])
        fin(P, "cv4")


def colvec2d(P, es, name, K):
    hv = P.d(name, [128, DT * K], F32)
    out = P.sb(es, f"cv2_{name}", [128, DT * K], F32)

    def sy(sync):
        P.w(sync)
        return Sem.inc(P.sld, sync.dma_start(out=out, in_=hv), 16), 16
    P.seq(f"cv2_{name}", [("sync", sy)])
    return out


def host_colvec2d(w):
    w = np.asarray(w, np.float32)
    if w.ndim == 3:
        w = w.reshape(-1, w.shape[-1])
    K = w.shape[-1]
    return np.ascontiguousarray(w.reshape(DT, 128, K).transpose(1, 0, 2).reshape(128, DT * K))


def ph_l1_fir2(P):
    PAD = 32
    hs = P.d("vs_f", [D, L])
    with contextlib.ExitStack() as es:
        xt = P.sb(es, "fir_x", [128, PAD + L], F32)
        acc = P.sb(es, "fir_a", [128, L], F32)
        wcs = {K: colvec2d(P, es, f"fir{K}w", K) for K in (3, 7, 15, 31)}

        def gp0(g):
            P.w(g)
            return P.sgp.inc(g.memset(xt[:, 0:PAD], 0.0), 1), 1
        P.seq("fir_pad", [("gpsimd", gp0)])

        for dt in range(DT):
            def sy(sync, dt=dt):
                P.w(sync)
                return P.sld.inc(sync.dma_start(
                    out=xt[:, PAD:PAD + L], in_=hs[dt * 128:(dt + 1) * 128, :]), 16), 16
            P.seq(f"fir_ld{dt}", [("sync", sy)])

            for K in (3, 7, 15, 31):
                hd = P.d(f"fir{K}_f", [D, L])

                def ve(v, K=K, dt=dt):
                    P.w(v)
                    for k in range(K):
                        sh = xt[:, PAD - (K - 1) + k: PAD - (K - 1) + k + L]
                        col = wcs[K][:, dt * K + k: dt * K + k + 1]
                        if k == 0:
                            i2 = v.tensor_scalar(out=acc, in0=sh, scalar1=col,
                                                 scalar2=None, op0=ALU.mult)
                        else:
                            i2 = v.scalar_tensor_tensor(out=acc, in0=sh, scalar=col,
                                                        in1=acc, op0=ALU.mult, op1=ALU.add)
                    return P.sev.inc(i2, 1), 1

                def sy2(sync, K=K, dt=dt):
                    P.w(sync)
                    return P.sld.inc(sync.dma_start(
                        out=P.d(f"fir{K}_f", [D, L])[dt * 128:(dt + 1) * 128, :],
                        in_=acc), 16), 16
                P.seq(f"fir_{K}_{dt}", [("vector", ve), ("sync", sy2)])
        fin(P, "fir")


def ph_l1_delta(P):
    hq = P.d("qs_f", [D, L])
    hk = P.d("ks_f", [D, L])
    hv = P.d("vs_f", [D, L])
    hbeta = P.d("beta_f", [H, L])
    hdelta = P.d("delta_f", [D, L])
    NT = 2
    with contextlib.ExitStack() as es:
        qn = P.sb(es, "dl_qn", [128, NT * L], BF16)
        kn = P.sb(es, "dl_kn", [128, NT * L], BF16)
        vb = P.sb(es, "dl_vb", [128, NT * L], BF16)
        kb = P.sb(es, "dl_kb", [128, NT * L], BF16)
        xf = P.sb(es, "dl_xf", [128, NT * CH], F32)
        xf2 = P.sb(es, "dl_xf2", [128, NT * CH], F32)
        sq = P.sb(es, "dl_sq", [128, NT * CH], BF16)
        brow = P.sb(es, "dl_brow", [1, L], F32)
        brep = P.sb(es, "dl_brep", [128, CH], F32)
        rrep = P.sb(es, "dl_rrep", [128, CH], F32)
        S_f = P.sb(es, "dl_Sf", [128, NT * 256], F32)
        S_b = P.sb(es, "dl_Sb", [128, NT * 256], BF16)
        npow = P.sb(es, "dl_np", [128, 128], BF16)
        npow2 = P.sb(es, "dl_np2", [128, 128], BF16)
        spow = P.sb(es, "dl_sp", [128, 7 * 128], BF16)
        attT = P.sb(es, "dl_attT", [128, 128], BF16)
        xtm = P.sb(es, "dl_xtm", [128, 512], BF16)
        xtm2 = P.sb(es, "dl_xtm2", [128, 512], BF16)
        ktm = P.sb(es, "dl_ktm", [128, 256], BF16)
        wfm = P.sb(es, "dl_wfm", [128, 256], BF16)
        uadj = P.sb(es, "dl_uadj", [128, 256], BF16)
        otm = P.sb(es, "dl_otm", [128, 256], F32)
        ofm = P.sb(es, "dl_ofm", [128, 256], F32)

        for h in range(H):
            def syb(sync, h=h):
                P.w(sync)
                return P.sld.inc(sync.dma_start(out=brow, in_=hbeta[h:h + 1, :]), 16), 16
            P.seq(f"dl_b{h}", [("sync", syb)])

            for c in range(NCH):
                sl = slice(c * CH, (c + 1) * CH)

                def sy(sync, h=h, c=c):
                    P.w(sync)
                    src = hq[h * 256:(h + 1) * 256, c * CH:(c + 1) * CH]
                    P.sld.inc(sync.dma_start(out=xf.rearrange("p (t n) -> p t n", n=CH),
                                             in_=src.rearrange("(t p) n -> p t n", p=128)), 16)
                    src2 = hk[h * 256:(h + 1) * 256, c * CH:(c + 1) * CH]
                    return P.sld.inc(sync.dma_start(out=xf2.rearrange("p (t n) -> p t n", n=CH),
                                                    in_=src2.rearrange("(t p) n -> p t n", p=128)), 16), 16

                def tn_r(t, h=h, c=c):
                    P.w(t)
                    i2 = t.matmul(P.pbk[2], P.ones_f1, brow[:, c * CH:(c + 1) * CH],
                                  start=True, stop=True)
                    return P.smm.inc(i2, 1), 1

                def sc_q(s, h=h, c=c):
                    P.w(s)
                    i2 = s.activation(out=sview(sq, CH), in_=sview(xf, CH), func=AF.Square)
                    return P.ssc.inc(i2, 1), 1

                def tn_q(t, h=h, c=c):
                    P.w(t)
                    for tt in range(NT):
                        i2 = t.matmul(P.pbk[0], P.ones_b, ct(sq, tt, CH),
                                      start=(tt == 0), stop=(tt == NT - 1))
                    return P.smm.inc(i2, 1), 1

                def ve_q(v, h=h, c=c):
                    P.w(v)
                    v.tensor_copy(out=brep, in_=P.pbk[2])
                    v.tensor_scalar(out=rrep, in0=P.pbk[0], scalar1=1e-6, scalar2=None,
                                    op0=ALU.add)
                    i2 = v.reciprocal(out=rrep, in_=rrep)
                    return P.sev.inc(i2, 1), 1

                def sc_q2(s, h=h, c=c):
                    P.w(s)
                    i2 = s.activation(out=rrep, in_=rrep, func=AF.Sqrt)
                    return P.ssc.inc(i2, 1), 1

                def ve_q3(v, h=h, c=c):
                    P.w(v)
                    for tt in range(NT):
                        i2 = v.tensor_mul(qn[:, tt * L + c * CH: tt * L + (c + 1) * CH],
                                          ct(xf, tt, CH), rrep)
                    return P.sev.inc(i2, 1), 1
                P.seq(f"dl_q{h}_{c}", [("sync", sy), ("tensor", tn_r), ("scalar", sc_q),
                                       ("tensor", tn_q), ("vector", ve_q), ("scalar", sc_q2),
                                       ("vector", ve_q3)])

                def sc_k(s, h=h, c=c):
                    P.w(s)
                    i2 = s.activation(out=sview(sq, CH), in_=sview(xf2, CH), func=AF.Square)
                    return P.ssc.inc(i2, 1), 1

                def tn_k(t, h=h, c=c):
                    P.w(t)
                    for tt in range(NT):
                        i2 = t.matmul(P.pbk[0], P.ones_b, ct(sq, tt, CH),
                                      start=(tt == 0), stop=(tt == NT - 1))
                    return P.smm.inc(i2, 1), 1

                def ve_k(v, h=h, c=c):
                    P.w(v)
                    v.tensor_scalar(out=rrep, in0=P.pbk[0], scalar1=1e-6, scalar2=None,
                                    op0=ALU.add)
                    i2 = v.reciprocal(out=rrep, in_=rrep)
                    return P.sev.inc(i2, 1), 1

                def sc_k2(s, h=h, c=c):
                    P.w(s)
                    i2 = s.activation(out=rrep, in_=rrep, func=AF.Sqrt)
                    return P.ssc.inc(i2, 1), 1

                def ve_k3(v, h=h, c=c):
                    P.w(v)
                    for tt in range(NT):
                        v.tensor_mul(ct(xf2, tt, CH), ct(xf2, tt, CH), rrep)
                        v.tensor_copy(out=kn[:, tt * L + c * CH: tt * L + (c + 1) * CH],
                                      in_=ct(xf2, tt, CH))
                        i2 = v.tensor_mul(kb[:, tt * L + c * CH: tt * L + (c + 1) * CH],
                                          ct(xf2, tt, CH), brep)
                    return P.sev.inc(i2, 1), 1
                P.seq(f"dl_k{h}_{c}", [("scalar", sc_k), ("tensor", tn_k), ("vector", ve_k),
                                       ("scalar", sc_k2), ("vector", ve_k3)])

                def sy_v(sync, h=h, c=c):
                    P.w(sync)
                    src = hv[h * 256:(h + 1) * 256, c * CH:(c + 1) * CH]
                    return P.sld.inc(sync.dma_start(out=xf.rearrange("p (t n) -> p t n", n=CH),
                                                    in_=src.rearrange("(t p) n -> p t n", p=128)), 16), 16

                def ve_v(v, h=h, c=c):
                    P.w(v)
                    for tt in range(NT):
                        i2 = v.tensor_mul(vb[:, tt * L + c * CH: tt * L + (c + 1) * CH],
                                          ct(xf, tt, CH), brep)
                    return P.sev.inc(i2, 1), 1
                P.seq(f"dl_v{h}_{c}", [("sync", sy_v), ("vector", ve_v)])

            def gp_s(g, h=h):
                P.w(g)
                g.memset(S_f, 0.0)
                return P.sgp.inc(g.memset(S_b, 0.0), 1), 1
            P.seq(f"dl_s0_{h}", [("gpsimd", gp_s)])

            for ck in range(NCK):
                col = ck * CK

                def qc(buf, tt, w=CK):
                    return buf[:, tt * L + col: tt * L + col + w]

                def tn_A(t, h=h, ck=ck):
                    P.w(t)
                    for tt in range(NT):
                        t.matmul(P.pbk[0][:, 0:128], qc(kb, tt), qc(kn, tt),
                                 start=(tt == 0), stop=(tt == NT - 1))
                    for tt in range(NT):
                        i2 = t.matmul(P.pbk[0][:, 128:256], qc(kn, tt), qc(qn, tt),
                                      start=(tt == 0), stop=(tt == NT - 1))
                    return P.smm.inc(i2, 1), 1

                def ve_A(v, h=h, ck=ck):
                    P.w(v)
                    v.tensor_mul(npow, P.pbk[0][:, 0:128], P.mask_lo)
                    i2 = v.tensor_mul(attT, P.pbk[0][:, 128:256], P.mask_upi)
                    return P.sev.inc(i2, 1), 1
                P.seq(f"dl_A{h}_{ck}", [("tensor", tn_A), ("vector", ve_A)])

                def tn_t1(t, h=h, ck=ck):
                    P.w(t)
                    i2 = t.transpose(P.pbt[:, 0:128], npow, P.ident_b)
                    return P.smm.inc(i2, 1), 1

                def ve_t1(v, h=h, ck=ck):
                    P.w(v)
                    i2 = v.tensor_copy(out=spow[:, 0:128], in_=P.pbt[:, 0:128])
                    return P.sev.inc(i2, 1), 1
                P.seq(f"dl_t1{h}_{ck}", [("tensor", tn_t1), ("vector", ve_t1)])

                cur_n = npow
                for j in range(1, 7):
                    def tn_d(t, h=h, ck=ck, j=j, cur_n=cur_n):
                        P.w(t)
                        t.matmul(P.pbk[1][:, 0:128], spow[:, (j - 1) * 128: j * 128], cur_n,
                                 start=True, stop=True)
                        i2 = t.matmul(P.pbk[1][:, 128:256], cur_n, spow[:, (j - 1) * 128: j * 128],
                                      start=True, stop=True)
                        return P.smm.inc(i2, 1), 1

                    nxt = npow2 if cur_n is npow else npow

                    def ve_d(v, h=h, ck=ck, j=j, nxt=nxt):
                        P.w(v)
                        v.tensor_copy(out=nxt, in_=P.pbk[1][:, 0:128])
                        i2 = v.tensor_copy(out=spow[:, j * 128:(j + 1) * 128], in_=P.pbk[1][:, 128:256])
                        return P.sev.inc(i2, 1), 1
                    P.seq(f"dl_d{h}_{ck}_{j}", [("tensor", tn_d), ("vector", ve_d)])
                    cur_n = nxt

                def tn_t2(t, h=h, ck=ck):
                    P.w(t)
                    for tt in range(NT):
                        t.transpose(P.pbt[:, tt * 128:(tt + 1) * 128], qc(vb, tt), P.ident_b)
                    for tt in range(NT):
                        t.transpose(P.pbt[:, 256 + tt * 128: 256 + (tt + 1) * 128], qc(kb, tt), P.ident_b)
                    for tt in range(NT):
                        i2 = t.transpose(P.pbt[:, 512 + tt * 128: 512 + (tt + 1) * 128], qc(kn, tt), P.ident_b)
                    return P.smm.inc(i2, 1), 1

                def ve_t2(v, h=h, ck=ck):
                    P.w(v)
                    v.tensor_copy(out=xtm, in_=P.pbt[:, 0:512])
                    i2 = v.tensor_copy(out=ktm, in_=P.pbt[:, 512:768])
                    return P.sev.inc(i2, 1), 1
                P.seq(f"dl_t2{h}_{ck}", [("tensor", tn_t2), ("vector", ve_t2)])

                src_x, dst_x = xtm, xtm2
                for j in range(6, 0, -1):
                    def tn_a(t, h=h, ck=ck, j=j, src_x=src_x):
                        P.w(t)
                        i2 = t.matmul(P.pbk[1], spow[:, j * 128:(j + 1) * 128], src_x,
                                      start=True, stop=True)
                        return P.smm.inc(i2, 1), 1

                    def ve_a(v, h=h, ck=ck, src_x=src_x, dst_x=dst_x):
                        P.w(v)
                        i2 = v.tensor_add(dst_x, src_x, P.pbk[1])
                        return P.sev.inc(i2, 1), 1
                    P.seq(f"dl_a{h}_{ck}_{j}", [("tensor", tn_a), ("vector", ve_a)])
                    src_x, dst_x = dst_x, src_x

                def tn_a0(t, h=h, ck=ck, src_x=src_x):
                    P.w(t)
                    i2 = t.matmul(P.pbk[1], spow[:, 0:128], src_x, start=True, stop=True)
                    return P.smm.inc(i2, 1), 1

                def ve_a0(v, h=h, ck=ck, src_x=src_x, dst_x=dst_x):
                    P.w(v)
                    i2 = v.tensor_sub(dst_x, src_x, P.pbk[1])
                    return P.sev.inc(i2, 1), 1
                P.seq(f"dl_a0{h}_{ck}", [("tensor", tn_a0), ("vector", ve_a0)])
                uw = dst_x

                def tn_t3(t, h=h, ck=ck, uw=uw):
                    P.w(t)
                    for tt in range(NT):
                        i2 = t.transpose(P.pbt[:, tt * 128:(tt + 1) * 128],
                                         uw[:, 256 + tt * 128: 256 + (tt + 1) * 128], P.ident_b)
                    return P.smm.inc(i2, 1), 1

                def ve_t3(v, h=h, ck=ck):
                    P.w(v)
                    i2 = v.tensor_copy(out=wfm, in_=P.pbt[:, 0:256])
                    return P.sev.inc(i2, 1), 1
                P.seq(f"dl_t3{h}_{ck}", [("tensor", tn_t3), ("vector", ve_t3)])

                def tn_u(t, h=h, ck=ck):
                    P.w(t)
                    for tt in range(NT):
                        i2 = t.matmul(P.pbk[1][:, 0:256], wfm[:, tt * 128:(tt + 1) * 128],
                                      S_b[:, tt * 256:(tt + 1) * 256],
                                      start=(tt == 0), stop=(tt == NT - 1))
                    return P.smm.inc(i2, 1), 1

                def ve_u(v, h=h, ck=ck, uw=uw):
                    P.w(v)
                    i2 = v.tensor_sub(uadj, uw[:, 0:256], P.pbk[1][:, 0:256])
                    return P.sev.inc(i2, 1), 1
                P.seq(f"dl_u{h}_{ck}", [("tensor", tn_u), ("vector", ve_u)])

                def tn_o(t, h=h, ck=ck):
                    P.w(t)
                    for tt in range(NT):
                        t.matmul(P.pbk[2][:, 0:256], qc(qn, tt), S_b[:, tt * 256:(tt + 1) * 256],
                                 start=(tt == 0), stop=False)
                    t.matmul(P.pbk[2][:, 0:256], attT, uadj, start=False, stop=True)
                    for tt in range(NT):
                        i2 = t.matmul(P.pbk[3], ktm[:, tt * 128:(tt + 1) * 128], uadj,
                                      start=True, stop=True)
                    return P.smm.inc(i2, 1), 1

                def tn_o2(t, h=h, ck=ck):
                    P.w(t)
                    for tt in range(NT):
                        t.matmul(P.pbk[2][:, 0:256], qc(qn, tt), S_b[:, tt * 256:(tt + 1) * 256],
                                 start=(tt == 0), stop=False)
                    t.matmul(P.pbk[2][:, 0:256], attT, uadj, start=False, stop=True)
                    t.matmul(P.pbk[3][:, 0:256], ktm[:, 0:128], uadj, start=True, stop=True)
                    i2 = t.matmul(P.pbk[4][:, 0:256], ktm[:, 128:256], uadj, start=True, stop=True)
                    return P.smm.inc(i2, 1), 1

                def ve_o(v, h=h, ck=ck):
                    P.w(v)
                    v.tensor_copy(out=otm, in_=P.pbk[2][:, 0:256])
                    v.tensor_add(S_f[:, 0:256], S_f[:, 0:256], P.pbk[3][:, 0:256])
                    i2 = v.tensor_add(S_f[:, 256:512], S_f[:, 256:512], P.pbk[4][:, 0:256])
                    return P.sev.inc(i2, 1), 1

                def sc_o(s, h=h, ck=ck):
                    P.w(s)
                    i2 = s.activation(out=S_b, in_=S_f, func=AF.Copy)
                    return P.ssc.inc(i2, 1), 1
                P.seq(f"dl_o{h}_{ck}", [("tensor", tn_o2), ("vector", ve_o), ("scalar", sc_o)])

                def tn_t4(t, h=h, ck=ck):
                    P.w(t)
                    for tt in range(NT):
                        i2 = t.transpose(P.pbk[5][:, tt * 128:(tt + 1) * 128],
                                         otm[:, tt * 128:(tt + 1) * 128], P.ident_f)
                    return P.smm.inc(i2, 1), 1

                def ve_t4(v, h=h, ck=ck):
                    P.w(v)
                    i2 = v.tensor_copy(out=ofm, in_=P.pbk[5][:, 0:256])
                    return P.sev.inc(i2, 1), 1

                def sy_o(sync, h=h, ck=ck):
                    P.w(sync)
                    dst = hdelta[h * 256:(h + 1) * 256, col:col + CK]
                    i2 = sync.dma_start(out=dst.rearrange("(t p) n -> p t n", p=128),
                                        in_=ofm.rearrange("p (t n) -> p t n", n=128))
                    return P.sld.inc(i2, 16), 16
                P.seq(f"dl_t4{h}_{ck}", [("tensor", tn_t4), ("vector", ve_t4), ("sync", sy_o)])

        fin(P, "delta")



MCH = 1024
NMCH = L // MCH
RKT = 48
RET = 16


def ph_l1_w1bf(P):
    hw = P.d("rw1", [6144, 2048], F32)
    hb = P.d("rw1_b", [6144, 2048], BF16)
    with contextlib.ExitStack() as es:
        tile = P.sb(es, "w1c_t", [128, 2048], BF16)
        for r in range(RKT):
            def gp(g, r=r):
                P.w(g)
                return P.sgp.inc(g.dma_start(out=tile, in_=hw[r * 128:(r + 1) * 128, :]), 16), 16

            def sy(sync, r=r):
                P.w(sync)
                return P.sld.inc(sync.dma_start(out=hb[r * 128:(r + 1) * 128, :], in_=tile), 16), 16
            P.seq(f"w1c_{r}", [("gpsimd", gp), ("sync", sy)])
        fin(P, "w1c")


def ph_l1_router(P):
    srcs = ["xn1_b"] + [f"fir{K}_f" for K in (3, 7, 15, 31)] + ["delta_f"]
    hw1 = P.d("rw1_b", [6144, 2048], BF16)
    hprob = P.d("probs_f", [20, L])
    with contextlib.ExitStack() as es:
        movs = [P.sb(es, f"rt_m{i}", [128, DT * MCH], BF16) for i in range(6)]
        wbuf = [P.sb(es, f"rt_w{i}", [128, RKT * 128], BF16) for i in range(2)]
        s_rw = [P.newsem(f"rt_w{i}") for i in range(2)]
        s_rmv = P.newsem("rt_mv")
        rec_mm = {}
        rec_sc = {}
        hmid = P.sb(es, "rt_h", [128, RET * MCH], BF16)
        rb1_c = colvec(P, es, "rb1", 2048)
        rw2 = P.sb(es, "rt_w2", [128, RET * 20], BF16)
        lg = P.sb(es, "rt_lg", [32, MCH], F32)
        tm = P.sb(es, "rt_tm", [32, MCH], F32)
        ex = P.sb(es, "rt_ex", [32, MCH], F32)
        mx = P.sb(es, "rt_mx", [32, MCH // 32 * 4], F32)
        ti20 = P.sb(es, "rt_ti", [20, 1], F32)
        sel = P.sb(es, "rt_sel", [H, 20], F32)
        lt_col = colvec(P, es, "ltau", H)
        rb2_c = colvec(P, es, "rb2", 20)

        def gp_t(g):
            P.w(g)
            g.memset(sel, 1.0)
            g.affine_select(out=sel, in_=sel, compare_op=ALU.is_ge, fill=0.0,
                            base=0, pattern=[[1, 20]], channel_multiplier=-5)
            i2 = g.affine_select(out=sel, in_=sel, compare_op=ALU.is_ge, fill=0.0,
                                 base=4, pattern=[[-1, 20]], channel_multiplier=5)
            return P.sgp.inc(i2, 1), 1

        def sc_t(s):
            P.w(s)
            return P.ssc.inc(s.activation(out=lt_col, in_=lt_col, func=AF.Exp), 1), 1

        def ve_t(v):
            P.w(v)
            v.tensor_scalar_max(out=lt_col, in0=lt_col, scalar1=0.5)
            i2 = v.reciprocal(out=lt_col, in_=lt_col)
            return P.sev.inc(i2, 1), 1

        def tn_t(t):
            P.w(t)
            return P.smm.inc(t.matmul(P.pbk[6][0:20, 0:1], sel, lt_col, start=True, stop=True), 1), 1

        def ve_t2(v):
            P.w(v)
            i2 = v.tensor_copy(out=ti20, in_=P.pbk[6][0:20, 0:1])
            return P.sev.inc(i2, 1), 1
        P.seq("rt_tau", [("gpsimd", gp_t), ("scalar", sc_t), ("vector", ve_t),
                         ("tensor", tn_t), ("vector", ve_t2)])

        def gp_w2(g):
            P.w(g)
            hv = P.d("rw2", [2048, 20], F32)
            i2 = g.dma_start(out=rw2.rearrange("p (k m) -> p k m", m=20),
                             in_=hv.rearrange("(k p) m -> p k m", p=128))
            return P.sgp.inc(i2, 16), 16
        P.seq("rt_w2l", [("gpsimd", gp_w2)])

        for mc in range(NMCH):
            def gp_m(g, mc=mc):
                if mc == 0:
                    P.w(g)
                elif (mc - 1, RET - 1) in rec_mm:
                    g.wait_ge(P.smm.h, rec_mm[(mc - 1, RET - 1)])
                for i, s in enumerate(srcs):
                    dt = BF16 if s.endswith("_b") else F32
                    hv = P.d(s, [D, L], dt)
                    i2 = g.dma_start(out=sview(movs[i], MCH),
                                     in_=cview(hv, mc, MCH))
                    s_rmv.inc(i2, 16)
                return i2, 16
            P.seq(f"rt_mv{mc}", [("gpsimd", gp_m)])

            for et in range(RET):
                wb = et % 2
                bk = (et % 2) * 2

                def sy_w(sync, mc=mc, et=et, wb=wb):
                    if mc == 0 and et < 2:
                        P.w(sync)
                    elif (mc, et - 2) in rec_mm:
                        sync.wait_ge(P.smm.h, rec_mm[(mc, et - 2)])
                    elif et < 2:
                        sync.wait_ge(P.smm.h, rec_mm[(mc - 1, RET - 2 + et)])
                    sl = hw1[:, et * 128:(et + 1) * 128]
                    i2 = sync.dma_start(out=wbuf[wb].rearrange("p (k q) -> p k q", q=128),
                                        in_=sl.rearrange("(k p) q -> p k q", p=128))
                    return s_rw[wb].inc(i2, 16), 16

                def tn_w(t, mc=mc, et=et, wb=wb, bk=bk):
                    if mc == 0 and et == 0:
                        P.w(t)
                    if et < 2:
                        t.wait_ge(P.sev.h, P.sev.n)
                    t.wait_ge(s_rw[wb].h, s_rw[wb].n)
                    t.wait_ge(s_rmv.h, s_rmv.n)
                    if (mc, et - 2) in rec_sc:
                        t.wait_ge(P.ssc.h, rec_sc[(mc, et - 2)])
                    elif (mc - 1, RET - 2 + et) in rec_sc:
                        t.wait_ge(P.ssc.h, rec_sc[(mc - 1, RET - 2 + et)])
                    for half in range(2):
                        bank = P.pbk[bk + half]
                        for k in range(RKT):
                            i2 = t.matmul(bank, wbuf[wb][:, k * 128:(k + 1) * 128],
                                          movs[k // DT][:, (k % DT) * MCH + half * 512:
                                                        (k % DT) * MCH + half * 512 + 512],
                                          start=(k == 0), stop=(k == RKT - 1))
                    P.smm.inc(i2, 1)
                    rec_mm[(mc, et)] = P.smm.n
                    return i2, 1

                def sc_w(s, mc=mc, et=et, bk=bk):
                    s.wait_ge(P.smm.h, rec_mm[(mc, et)])
                    for half in range(2):
                        i2 = s.activation(out=hmid[:, et * MCH + half * 512: et * MCH + half * 512 + 512],
                                          in_=P.pbk[bk + half], func=AF.Gelu,
                                          bias=rb1_c[:, et:et + 1])
                    P.ssc.inc(i2, 1)
                    rec_sc[(mc, et)] = P.ssc.n
                    return i2, 1

                def sy_hm(sync, mc=mc, et=et):
                    P.w(sync)
                    hdh = P.d("dbg_hm", [2048, L], BF16)
                    i2 = sync.dma_start(out=hdh[et * 128:(et + 1) * 128, mc * MCH:(mc + 1) * MCH],
                                        in_=hmid[:, et * MCH:(et + 1) * MCH])
                    return P.sld.inc(i2, 16), 16
                stp = [("sync", sy_w), ("tensor", tn_w), ("scalar", sc_w)]
                if DBG_RMS:
                    stp.append(("sync", sy_hm))
                P.seq(f"rt_e{mc}_{et}", stp)

            for half in range(2):
                def gp_pad(g, mc=mc, half=half):
                    P.w(g)
                    return P.sgp.inc(g.memset(lg[:, half * 512:(half + 1) * 512], -1e30), 1), 1

                def tn_2(t, mc=mc, half=half):
                    P.w(t)
                    for k in range(RET):
                        i2 = t.matmul(P.pbk[2][0:20, :], rw2[:, k * 20:(k + 1) * 20],
                                      hmid[:, k * MCH + half * 512: k * MCH + half * 512 + 512],
                                      start=(k == 0), stop=(k == RET - 1))
                    return P.smm.inc(i2, 1), 1

                def sc_2(v, mc=mc, half=half):
                    P.w(v)
                    sl = lg[0:20, half * 512:(half + 1) * 512]
                    v.tensor_scalar(out=sl, in0=P.pbk[2][0:20, :], scalar1=rb2_c,
                                    scalar2=None, op0=ALU.add)
                    i2 = v.tensor_scalar(out=sl, in0=sl, scalar1=ti20,
                                         scalar2=None, op0=ALU.mult)
                    return P.sev.inc(i2, 1), 1

                def sy_lg(sync, mc=mc, half=half):
                    P.w(sync)
                    hdl = P.d("dbg_lg", [20, L])
                    i2 = sync.dma_start(out=hdl[:, mc * MCH + half * 512: mc * MCH + half * 512 + 512],
                                        in_=lg[0:20, half * 512:(half + 1) * 512])
                    return P.sld.inc(i2, 16), 16

                def ve_2(v, mc=mc, half=half):
                    P.w(v)
                    i2 = v.transpose(out=tm[:, half * 512:(half + 1) * 512],
                                     in_=lg[:, half * 512:(half + 1) * 512])
                    return P.sev.inc(i2, 1), 1

                def ve_3(v, mc=mc, half=half):
                    P.w(v)
                    NG = 512 // 32
                    tv = tm[:, half * 512:(half + 1) * 512].rearrange(
                        "p (g r) -> p g r", r=32)[:, :, 0:20].rearrange(
                        "p g (hh f) -> p g hh f", f=5)
                    mv = mx[:, half * 64:(half + 1) * 64].rearrange(
                        "p (g hh) -> p g hh", hh=4)
                    v.tensor_reduce(out=mv, in_=tv, axis=AX.X, op=ALU.max)
                    i2 = v.tensor_tensor(out=tv, in0=tv,
                                         in1=mx[:, half * 64:(half + 1) * 64].rearrange(
                                             "p (g hh) -> p g hh", hh=4).broadcast_to([32, NG, 4, 5]),
                                         op=ALU.subtract)
                    return P.sev.inc(i2, 1), 1

                def sc_3(s, mc=mc, half=half):
                    P.w(s)
                    NG = 512 // 32
                    tv = tm[:, half * 512:(half + 1) * 512].rearrange(
                        "p (g r) -> p g r", r=32)[:, :, 0:20]
                    ev = ex[:, half * 512:(half + 1) * 512].rearrange(
                        "p (g r) -> p g r", r=32)[:, :, 0:20]
                    i2 = s.activation(out=ev, in_=tv, func=AF.Exp)
                    return P.ssc.inc(i2, 1), 1

                def ve_4(v, mc=mc, half=half):
                    P.w(v)
                    NG = 512 // 32
                    evh = ex[:, half * 512:(half + 1) * 512].rearrange(
                        "p (g r) -> p g r", r=32)[:, :, 0:20].rearrange(
                        "p g (hh f) -> p g hh f", f=5)
                    mv = mx[:, half * 64:(half + 1) * 64].rearrange(
                        "p (g hh) -> p g hh", hh=4)
                    v.tensor_reduce(out=mv, in_=evh, axis=AX.X, op=ALU.add)
                    v.reciprocal(out=mv, in_=mv)
                    v.tensor_tensor(out=evh, in0=evh,
                                    in1=mx[:, half * 64:(half + 1) * 64].rearrange(
                                        "p (g hh) -> p g hh", hh=4).broadcast_to([32, NG, 4, 5]),
                                    op=ALU.mult)
                    i2 = v.transpose(out=lg[:, half * 512:(half + 1) * 512],
                                     in_=ex[:, half * 512:(half + 1) * 512])
                    return P.sev.inc(i2, 1), 1

                def sy_p(sync, mc=mc, half=half):
                    P.w(sync)
                    i2 = sync.dma_start(out=hprob[:, mc * MCH + half * 512: mc * MCH + half * 512 + 512],
                                        in_=lg[0:20, half * 512:(half + 1) * 512])
                    return P.sld.inc(i2, 16), 16
                steps_sm = [("gpsimd", gp_pad), ("tensor", tn_2), ("vector", sc_2)]
                if DBG_RMS:
                    steps_sm.append(("sync", sy_lg))
                steps_sm += [("vector", ve_2), ("vector", ve_3), ("scalar", sc_3),
                             ("vector", ve_4), ("sync", sy_p)]
                P.seq(f"rt_sm{mc}_{half}", steps_sm)
        fin(P, "router")


def ph_l1_ctx(P):
    paths = [f"fir{K}_f" for K in (3, 7, 15, 31)] + ["delta_f"]
    hvs = P.d("vs_f", [D, L])
    hprob = P.d("probs_f", [20, L])
    hpadj = P.d("padj_f", [H, L])
    hs1 = P.d("s1_f", [D, L])
    with contextlib.ExitStack() as es:
        wo = load_wpack(P, es, "wo", "wo", DT, DT)
        pf = [P.sb(es, f"cx_p{i}", [128, DT * CH], F32) for i in range(5)]
        vsf = P.sb(es, "cx_v", [128, DT * CH], F32)
        prow = P.sb(es, "cx_pr", [1, 24 * CH], F32)
        reps = P.sb(es, "cx_rep", [128, 6 * CH], F32)
        ctx = P.sb(es, "cx_ctx", [128, 2 * CH], F32)
        tmp = P.sb(es, "cx_t", [128, 2 * CH], F32)
        sqb = P.sb(es, "cx_sq", [128, 2 * CH], BF16)
        rrep = P.sb(es, "cx_rr", [128, CH], F32)
        onb = P.sb(es, "cx_ob", [128, DT * CH], BF16)
        onw_c = colvec(P, es, "onw", 256)

        def ve_ow(v):
            P.w(v)
            return P.sev.inc(v.tensor_scalar_mul(out=onw_c, in0=onw_c, scalar1=16.0), 1), 1
        P.seq("cx_ow16", [("vector", ve_ow)])

        for c in range(NCH):
            def sy(sync, c=c):
                P.w(sync)
                for i, s in enumerate(paths):
                    P.sld.inc(sync.dma_start(out=sview(pf[i]), in_=cview(P.d(s, [D, L]), c)), 16)
                P.sld.inc(sync.dma_start(out=sview(vsf), in_=cview(hvs, c)), 16)
                P.sld.inc(sync.dma_start(
                    out=prow[:, 0:20 * CH].rearrange("o (r n) -> o r n", n=CH),
                    in_=hprob[:, c * CH:(c + 1) * CH].rearrange("r n -> () r n")), 16)
                return P.sld.inc(sync.dma_start(
                    out=prow[:, 20 * CH:24 * CH].rearrange("o (r n) -> o r n", n=CH),
                    in_=hpadj[:, c * CH:(c + 1) * CH].rearrange("r n -> () r n")), 16), 16
            P.seq(f"cx_ld{c}", [("sync", sy)])

            for hh in range(H):
                def tn_r(t, c=c, hh=hh):
                    P.w(t)
                    for i in range(5):
                        r = 5 * hh + i
                        t.matmul(P.pbk[i], P.ones_f1, prow[:, r * CH:(r + 1) * CH],
                                 start=True, stop=True)
                    r = 20 + hh
                    i2 = t.matmul(P.pbk[5], P.ones_f1, prow[:, r * CH:(r + 1) * CH],
                                  start=True, stop=True)
                    return P.smm.inc(i2, 1), 1

                def ve_r(v, c=c, hh=hh):
                    P.w(v)
                    for i in range(6):
                        i2 = v.tensor_copy(out=ct(reps, i), in_=P.pbk[i])
                    return P.sev.inc(i2, 1), 1
                P.seq(f"cx_rep{c}_{hh}", [("tensor", tn_r), ("vector", ve_r)])

                def ve_c(v, c=c, hh=hh):
                    P.w(v)
                    for tt in range(2):
                        m = hh * 2 + tt
                        cslice = ct(ctx, tt)
                        v.tensor_mul(cslice, ct(reps, 0), ct(pf[0], m))
                        for i in range(1, 5):
                            v.tensor_mul(ct(tmp, tt), ct(reps, i), ct(pf[i], m))
                            v.tensor_add(cslice, cslice, ct(tmp, tt))
                        v.tensor_sub(ct(tmp, tt), ct(vsf, m), cslice)
                        v.tensor_mul(ct(tmp, tt), ct(tmp, tt), ct(reps, 5))
                        i2 = v.tensor_add(cslice, cslice, ct(tmp, tt))
                    return P.sev.inc(i2, 1), 1
                P.seq(f"cx_c{c}_{hh}", [("vector", ve_c)])

                def sc_n(v, c=c, hh=hh):
                    P.w(v)
                    i2 = v.tensor_mul(sqb, ctx, ctx)
                    return P.sev.inc(i2, 1), 1

                def tn_n(t, c=c, hh=hh):
                    P.w(t)
                    for tt in range(2):
                        i2 = t.matmul(P.pbk[6], P.ones_b, ct(sqb, tt),
                                      start=(tt == 0), stop=(tt == 1))
                    return P.smm.inc(i2, 1), 1

                def ve_n(v, c=c, hh=hh):
                    P.w(v)
                    v.tensor_scalar(out=rrep, in0=P.pbk[6], scalar1=256e-5, scalar2=None,
                                    op0=ALU.add)
                    i2 = v.reciprocal(out=rrep, in_=rrep)
                    return P.sev.inc(i2, 1), 1

                def sc_n2(s, c=c, hh=hh):
                    P.w(s)
                    i2 = s.activation(out=rrep, in_=rrep, func=AF.Sqrt)
                    return P.ssc.inc(i2, 1), 1

                def ve_n2(v, c=c, hh=hh):
                    P.w(v)
                    for tt in range(2):
                        i2 = v.tensor_mul(ct(ctx, tt), ct(ctx, tt), rrep)
                    return P.sev.inc(i2, 1), 1

                def sc_n3(v, c=c, hh=hh):
                    P.w(v)
                    for tt in range(2):
                        m = hh * 2 + tt
                        i2 = v.tensor_scalar(out=ct(onb, m), in0=ct(ctx, tt),
                                             scalar1=onw_c[:, tt:tt + 1], scalar2=None,
                                             op0=ALU.mult)
                    return P.sev.inc(i2, 1), 1
                P.seq(f"cx_n{c}_{hh}", [("vector", sc_n), ("tensor", tn_n), ("vector", ve_n),
                                        ("scalar", sc_n2), ("vector", ve_n2), ("vector", sc_n3)])

            for half in range(2):
                def tn_o(t, c=c, half=half):
                    P.w(t)
                    for j in range(4):
                        m = half * 4 + j
                        for k in range(DT):
                            i2 = t.matmul(P.pbk[j], wv(wo, k, m, DT), ct(onb, k),
                                          start=(k == 0), stop=(k == DT - 1))
                    return P.smm.inc(i2, 1), 1

                def ve_o(v, c=c, half=half):
                    P.w(v)
                    for j in range(4):
                        m = half * 4 + j
                        i2 = v.tensor_copy(out=ct(vsf, m), in_=P.pbk[j])
                    return P.sev.inc(i2, 1), 1
                P.seq(f"cx_wo{c}_{half}", [("tensor", tn_o), ("vector", ve_o)])

            def sy2(sync, c=c):
                P.w(sync)
                return P.sld.inc(sync.dma_start(out=cview(hs1, c), in_=sview(vsf)), 16), 16
            P.seq(f"cx_st{c}", [("sync", sy2)])
        fin(P, "ctx")


def ph_l1_gate(P):
    hx1 = P.d("xn1_f", [D, L])
    hm = P.d("masks_f", [EXP, L])
    hmb = P.d("masks_b", [EXP, L], BF16)
    with contextlib.ExitStack() as es:
        gws = P.sb(es, "gt_w", [128, DT * EXP], F32)
        gb_c = colvec(P, es, "gb", EXP)
        mov = P.sb(es, "gt_mov", [128, DT * CH], F32)
        lg = P.sb(es, "gt_lg", [32, CH], F32)
        tm = P.sb(es, "gt_tm", [32, CH], F32)
        mx = P.sb(es, "gt_mx", [32, CH // 32], F32)
        nd = P.sb(es, "gt_nd", [32, CH // 32], F32)
        meq = P.sb(es, "gt_meq", [32, CH // 32], F32)
        mtm = P.sb(es, "gt_mtm", [32, CH], F32)
        mfm = P.sb(es, "gt_mfm", [32, CH], F32)
        mbf = P.sb(es, "gt_mbf", [EXP, CH], BF16)

        def gp_w(g):
            P.w(g)
            hv = P.d("gwT", [D, EXP], F32)
            i2 = g.dma_start(out=gws.rearrange("p (k m) -> p k m", m=EXP),
                             in_=hv.rearrange("(k p) m -> p k m", p=128))
            return P.sgp.inc(i2, 16), 16
        P.seq("gt_w", [("gpsimd", gp_w)])

        for c in range(NCH):
            def sy(sync, c=c):
                P.w(sync)
                return P.sld.inc(sync.dma_start(out=sview(mov), in_=cview(hx1, c)), 16), 16

            def gp_p(g, c=c):
                P.w(g)
                g.memset(lg, -1e30)
                return P.sgp.inc(g.memset(mtm, 0.0), 1), 1

            def tn(t, c=c):
                P.w(t)
                for k in range(DT):
                    i2 = t.matmul(P.pbk[0][0:EXP, :], gws[:, k * EXP:(k + 1) * EXP],
                                  ct(mov, k), start=(k == 0), stop=(k == DT - 1))
                return P.smm.inc(i2, 1), 1

            def sc(v, c=c):
                P.w(v)
                i2 = v.tensor_scalar(out=lg[0:EXP, :], in0=P.pbk[0][0:EXP, :],
                                     scalar1=gb_c, scalar2=None, op0=ALU.add)
                return P.sev.inc(i2, 1), 1

            def ve1(v, c=c):
                P.w(v)
                NG = CH // 32
                v.transpose(out=tm, in_=lg)
                tv8 = tm.rearrange("p (g r) -> p g r", r=32)[:, :, 0:EXP]
                v.tensor_reduce(out=mx, in_=tv8, axis=AX.X, op=ALU.max)
                tv = tm.rearrange("p (g r) -> p g r", r=32)
                mxb = mx.rearrange("p (g o) -> p g o", o=1).broadcast_to([32, NG, 32])
                v.tensor_tensor(out=mtm.rearrange("p (g r) -> p g r", r=32),
                                in0=mxb, in1=tv, op=ALU.subtract)
                v.tensor_scalar(out=mtm, in0=mtm, scalar1=0.0, scalar2=None, op0=ALU.is_gt)
                v.tensor_scalar(out=mtm, in0=mtm, scalar1=-1.0, scalar2=1.0,
                                op0=ALU.mult, op1=ALU.add)
                i2 = v.transpose(out=mfm, in_=mtm)
                return P.sev.inc(i2, 1), 1

            def sc2(s, c=c):
                P.w(s)
                i2 = s.activation(out=mbf, in_=mfm[0:EXP, :], func=AF.Copy)
                return P.ssc.inc(i2, 1), 1

            def sy2(sync, c=c):
                P.w(sync)
                P.sld.inc(sync.dma_start(out=hm[:, c * CH:(c + 1) * CH], in_=mfm[0:EXP, :]), 16)
                return P.sld.inc(sync.dma_start(out=hmb[:, c * CH:(c + 1) * CH], in_=mbf), 16), 16
            P.seq(f"gt_{c}", [("sync", sy), ("gpsimd", gp_p), ("tensor", tn), ("vector", sc),
                              ("vector", ve1), ("scalar", sc2), ("sync", sy2)])
        fin(P, "gate")


def ph_l1_moe(P):
    hx1b = P.d("xn1_b", [D, L], BF16)
    hsp = P.d("hsp_b", [HM, L], BF16)
    hm = P.d("masks_f", [EXP, L])
    hmb = P.d("masks_b", [EXP, L], BF16)
    hg = P.d("g_f", [D, L])
    hw1 = P.d("mw1T", [EXP, D, HM], F32)
    hw2 = P.d("mw2T", [EXP, HM, D], F32)
    hb1 = P.d("b1c", [EXP, 128, HMT], F32)
    hb2 = P.d("b2e", [EXP, D], F32)
    with contextlib.ExitStack() as es:
        mov = P.sb(es, "mo_x", [128, DT * CH], BF16)
        wgrp = P.sb(es, "mo_w", [128, DT * 8 * 128], BF16)
        hch = P.sb(es, "mo_h", [128, 8 * CH], BF16)
        b1t = P.sb(es, "mo_b1", [128, HMT], F32)
        b2t = P.sb(es, "mo_b2", [1, D], BF16)
        mrow = P.sb(es, "mo_mr", [1, CH], BF16)
        mrowf = P.sb(es, "mo_mf", [1, CH], F32)
        mrep = P.sb(es, "mo_rep", [128, CH], F32)
        yacc = P.sb(es, "mo_y", [128, DT * CH], F32)
        w2f = P.sb(es, "mo_w2", [128, HMT * DT * 128], BF16)
        h2 = P.sb(es, "mo_h2", [128, HMT * CH], BF16)

        for e in range(EXP):
            def gp_b(g, e=e):
                P.w(g)
                P.sgp.inc(g.dma_start(out=b1t, in_=hb1[e]), 16)
                i2 = g.dma_start(out=b2t, in_=hb2[e:e + 1, :])
                return P.sgp.inc(i2, 16), 16
            P.seq(f"mo_b{e}", [("gpsimd", gp_b)])

            for mg in range(4):
                def gp_w(g, e=e, mg=mg):
                    P.w(g)
                    sl = hw1[e][:, mg * 1024:(mg + 1) * 1024]
                    i2 = g.dma_start(out=wgrp.rearrange("p (k m q) -> p k m q", m=8, q=128),
                                     in_=sl.rearrange("(k p) (m q) -> p k m q", p=128, q=128))
                    return P.sgp.inc(i2, 16), 16
                P.seq(f"mo_w{e}_{mg}", [("gpsimd", gp_w)])

                for c in range(NCH):
                    def sy_x(sync, e=e, mg=mg, c=c):
                        P.w(sync)
                        return P.sld.inc(sync.dma_start(out=sview(mov), in_=cview(hx1b, c)), 16), 16

                    def tn_a(t, e=e, mg=mg, c=c):
                        P.w(t)
                        for j in range(4):
                            for k in range(DT):
                                i2 = t.matmul(P.pbk[j], wv(wgrp, k, j, 8), ct(mov, k),
                                              start=(k == 0), stop=(k == DT - 1))
                        return P.smm.inc(i2, 1), 1

                    def sc_a(s, e=e, mg=mg, c=c):
                        P.w(s)
                        for j in range(4):
                            i2 = s.activation(out=ct(hch, j), in_=P.pbk[j], func=AF.Relu,
                                              bias=b1t[:, mg * 8 + j: mg * 8 + j + 1])
                        return P.ssc.inc(i2, 1), 1

                    def tn_b(t, e=e, mg=mg, c=c):
                        P.w(t)
                        for j in range(4, 8):
                            for k in range(DT):
                                i2 = t.matmul(P.pbk[j - 4], wv(wgrp, k, j, 8), ct(mov, k),
                                              start=(k == 0), stop=(k == DT - 1))
                        return P.smm.inc(i2, 1), 1

                    def sc_b(s, e=e, mg=mg, c=c):
                        P.w(s)
                        for j in range(4, 8):
                            i2 = s.activation(out=ct(hch, j), in_=P.pbk[j - 4], func=AF.Relu,
                                              bias=b1t[:, mg * 8 + j: mg * 8 + j + 1])
                        return P.ssc.inc(i2, 1), 1

                    def sy_h(sync, e=e, mg=mg, c=c):
                        P.w(sync)
                        dst = hsp[mg * 1024:(mg + 1) * 1024, c * CH:(c + 1) * CH]
                        i2 = sync.dma_start(out=dst.rearrange("(k p) n -> p k n", p=128),
                                            in_=sview(hch))
                        return P.sld.inc(i2, 16), 16
                    P.seq(f"mo_m1_{e}_{mg}_{c}", [("sync", sy_x), ("tensor", tn_a), ("scalar", sc_a),
                                                  ("tensor", tn_b), ("scalar", sc_b),
                                                  ("sync", sy_h)])

            def gp_w2(g, e=e):
                P.w(g)
                i2 = g.dma_start(out=w2f.rearrange("p (k m q) -> p k m q", m=DT, q=128),
                                 in_=hw2[e].rearrange("(k p) (m q) -> p k m q", p=128, q=128))
                return P.sgp.inc(i2, 16), 16
            P.seq(f"mo_w2_{e}", [("gpsimd", gp_w2)])

            for c in range(NCH):
                def sy_m(sync, e=e, c=c):
                    P.w(sync)
                    P.sld.inc(sync.dma_start(out=mrow, in_=hmb[e:e + 1, c * CH:(c + 1) * CH]), 16)
                    P.sld.inc(sync.dma_start(out=mrowf, in_=hm[e:e + 1, c * CH:(c + 1) * CH]), 16)
                    sl = hsp[:, c * CH:(c + 1) * CH]
                    return P.sld.inc(sync.dma_start(out=sview(h2),
                                                    in_=sl.rearrange("(k p) n -> p k n", p=128)), 16), 16

                def tn_r(t, e=e, c=c):
                    P.w(t)
                    i2 = t.matmul(P.pbk[4], P.ones_f1, mrowf, start=True, stop=True)
                    return P.smm.inc(i2, 1), 1

                def ve_r(v, e=e, c=c):
                    P.w(v)
                    i2 = v.tensor_copy(out=mrep, in_=P.pbk[4])
                    return P.sev.inc(i2, 1), 1
                P.seq(f"mo_mr_{e}_{c}", [("sync", sy_m), ("tensor", tn_r), ("vector", ve_r)])

                for half in range(2):
                    def tn_2(t, e=e, c=c, half=half):
                        P.w(t)
                        for j in range(4):
                            m = half * 4 + j
                            for k in range(HMT):
                                t.matmul(P.pbk[j], wv(w2f, k, m, DT), ct(h2, k),
                                         start=(k == 0), stop=False)
                            i2 = t.matmul(P.pbk[j], b2t[:, m * 128:(m + 1) * 128], mrow,
                                          start=False, stop=True)
                        return P.smm.inc(i2, 1), 1

                    def ve_y(v, e=e, c=c, half=half):
                        P.w(v)
                        for j in range(4):
                            m = half * 4 + j
                            i2 = v.tensor_mul(ct(yacc, m), P.pbk[j], mrep)
                        return P.sev.inc(i2, 1), 1
                    P.seq(f"mo_m2_{e}_{c}_{half}", [("tensor", tn_2), ("vector", ve_y)])

                def gp_acc(g, e=e, c=c):
                    P.w(g)
                    kw = {} if e == 0 else {"accum_op": ALU.add}
                    i2 = g.dma_start(out=cview(hg, c), in_=sview(yacc), **kw)
                    return P.sgp.inc(i2, 16), 16
                P.seq(f"mo_y_{e}_{c}", [("gpsimd", gp_acc)])
        fin(P, "moe")


def ph_final(P):
    hx1 = P.d("x1_f", [D, L])
    hs1 = P.d("s1_f", [D, L])
    hg = P.d("g_f", [D, L])
    hout = P.d("out", [D, L])
    with contextlib.ExitStack() as es:
        a = P.sb(es, "fn_a", [128, DT * CH], F32)
        b = P.sb(es, "fn_b", [128, DT * CH], F32)
        g2 = P.sb(es, "fn_g", [128, DT * CH], F32)
        for c in range(NCH):
            def sy(sync, c=c):
                P.w(sync)
                P.sld.inc(sync.dma_start(out=sview(a), in_=cview(hx1, c)), 16)
                P.sld.inc(sync.dma_start(out=sview(b), in_=cview(hs1, c)), 16)
                return P.sld.inc(sync.dma_start(out=sview(g2), in_=cview(hg, c)), 16), 16

            def sc(s, c=c):
                P.w(s)
                return P.ssc.inc(s.activation(out=sview(g2), in_=sview(g2), func=AF.Silu), 1), 1

            def ve(v, c=c):
                P.w(v)
                for k in range(DT):
                    v.tensor_mul(ct(b, k), ct(b, k), ct(g2, k))
                    i2 = v.tensor_add(ct(a, k), ct(a, k), ct(b, k))
                return P.sev.inc(i2, 1), 1

            def sy2(sync, c=c):
                P.w(sync)
                return P.sld.inc(sync.dma_start(out=cview(hout, c), in_=sview(a)), 16), 16
            P.seq(f"fn_{c}", [("sync", sy), ("scalar", sc), ("vector", ve), ("sync", sy2)])
        fin(P, "final")



def ph_l1_moe2(P):
    hx1b = P.d("xn1_b", [D, L], BF16)
    hsp = P.d("hsp_b", [HM, L], BF16)
    hm = P.d("masks_f", [EXP, L])
    hmb = P.d("masks_b", [EXP, L], BF16)
    hg = P.d("g_f", [D, L])
    hw1 = P.d("mw1T", [EXP, D, HM], F32)
    hw2 = P.d("mw2T", [EXP, HM, D], F32)
    hb1 = P.d("b1c", [EXP, 128, HMT], F32)
    hb2 = P.d("b2e", [EXP, D], F32)
    with contextlib.ExitStack() as es:
        mov = [P.sb(es, f"m2_x{i}", [128, DT * CH], BF16) for i in range(2)]
        wgrp = [P.sb(es, f"m2_w{i}", [128, DT * 8 * 128], BF16) for i in range(2)]
        hch = [P.sb(es, f"m2_h{i}", [128, 8 * CH], BF16) for i in range(2)]
        h2s = [P.sb(es, f"m2_g{i}", [128, 8 * CH], BF16) for i in range(2)]
        yacc = [P.sb(es, f"m2_y{i}", [128, DT * CH], F32) for i in range(2)]
        b1t = P.sb(es, "m2_b1", [128, HMT], F32)
        b2t = P.sb(es, "m2_b2", [1, D], BF16)
        mrow = P.sb(es, "m2_mr", [1, CH], BF16)
        mrowf = P.sb(es, "m2_mf", [1, CH], F32)
        mrep = P.sb(es, "m2_rep", [128, CH], F32)
        w2f = P.sb(es, "m2_w2", [128, HMT * DT * 128], BF16)
        s_mv = [P.newsem(f"mo_mv{i}") for i in range(2)]
        s_wg = [P.newsem(f"mo_wg{i}") for i in range(2)]
        s_hst = [P.newsem(f"mo_hs{i}") for i in range(2)]
        s_h2 = [P.newsem(f"mo_h2{i}") for i in range(2)]
        s_w2 = P.newsem("mo_w2")
        s_msk = P.newsem("mo_mk")
        s_acc = P.newsem("mo_ac")
        s_b = P.newsem("mo_b")

        rec = {"sc": {}, "mm1": {}, "mm2": {}, "vy": {}, "mmlast": 0}

        def m1_iter(e, it, mg, c):
            mb, wb, hb = it % 2, mg % 2, it % 2

            def sy_x(sync):
                if (e, it - 2) in rec["mm1"]:
                    sync.wait_ge(P.smm.h, rec["mm1"][(e, it - 2)])
                elif e > 0 and it < 2:
                    sync.wait_ge(P.smm.h, rec["mm1"][(e - 1, 30 + it)])
                return P.sld.inc(sync.dma_start(out=sview(mov[mb]), in_=cview(hx1b, c)),
                                 0) and None or (s_mv[mb].inc(
                    sync.dma_start(out=sview(mov[mb]), in_=cview(hx1b, c)), 16), 16)
            raise RuntimeError

        mm1 = {}
        scA = {}
        scB = {}
        hst = {}
        vy = {}
        mm2 = {}
        for e in range(EXP):
            def gp_b(g, e=e):
                P.w(g)
                s_b.inc(g.dma_start(out=b1t, in_=hb1[e]), 16)
                return s_b.inc(g.dma_start(out=b2t, in_=hb2[e:e + 1, :]), 16), 16
            P.seq(f"m2b{e}", [("gpsimd", gp_b)])

            for mg in range(4):
                def gp_w(g, e=e, mg=mg):
                    wb = mg % 2
                    if (e, (mg - 2) * 8 + 7) in mm1:
                        g.wait_ge(P.smm.h, mm1[(e, (mg - 2) * 8 + 7)])
                    elif e > 0:
                        g.wait_ge(P.smm.h, mm1[(e - 1, (mg + 2) * 8 + 7)])
                    sl = hw1[e][:, mg * 1024:(mg + 1) * 1024]
                    i2 = g.dma_start(out=wgrp[wb].rearrange("p (k m q) -> p k m q", m=8, q=128),
                                     in_=sl.rearrange("(k p) (m q) -> p k m q", p=128, q=128))
                    s_wg[wb].inc(i2, 16)
                    if True:
                        qsl = hw2[e][mg * 1024:(mg + 1) * 1024, :]
                        if mg == 0 and e > 0:
                            g.wait_ge(P.smm.h, rec["mmlast"])
                        i2 = g.dma_start(
                            out=w2f.rearrange("p (k m q) -> p k m q", m=DT, q=128)[:, mg * 8:(mg + 1) * 8],
                            in_=qsl.rearrange("(k p) (m q) -> p k m q", p=128, q=128))
                        s_w2.inc(i2, 16)
                    return i2, 16
                P.seq(f"m2w{e}_{mg}", [("gpsimd", gp_w)])

                for c in range(NCH):
                    it = mg * 8 + c
                    mb, wb, hb = it % 2, mg % 2, it % 2

                    def sy_x(sync, e=e, it=it, c=c, mb=mb):
                        if (e, it - 2) in mm1:
                            sync.wait_ge(P.smm.h, mm1[(e, it - 2)])
                        elif e > 0:
                            sync.wait_ge(P.smm.h, mm1[(e - 1, 30 + (it % 2))])
                        i2 = sync.dma_start(out=sview(mov[mb]), in_=cview(hx1b, c))
                        return s_mv[mb].inc(i2, 16), 16

                    def tn_a(t, e=e, it=it, mb=mb, wb=wb):
                        if e == 0 and it == 0:
                            P.w(t)
                        t.wait_ge(s_mv[mb].h, s_mv[mb].n)
                        t.wait_ge(s_wg[wb].h, s_wg[wb].n)
                        if (e, it - 1) in scA:
                            t.wait_ge(P.ssc.h, scA[(e, it - 1)])
                        elif e > 0:
                            t.wait_ge(P.ssc.h, scA[(e - 1, 31)])
                        for j in range(4):
                            for k in range(DT):
                                i2 = t.matmul(P.pbk[j], wv(wgrp[wb], k, j, 8), ct(mov[mb], k),
                                              start=(k == 0), stop=(k == DT - 1))
                        return P.smm.inc(i2, 1), 1

                    def sc_a(s, e=e, it=it, hb=hb):
                        s.wait_ge(P.smm.h, P.smm.n)
                        s.wait_ge(s_b.h, s_b.n)
                        sl = s_hst[hb]
                        s.wait_ge(sl.h, sl.n)
                        for j in range(4):
                            i2 = s.activation(out=ct(hch[hb], j), in_=P.pbk[j], func=AF.Relu,
                                              bias=b1t[:, (it // 8) * 8 + j: (it // 8) * 8 + j + 1])
                        P.ssc.inc(i2, 1)
                        scA[(e, it)] = P.ssc.n
                        return i2, 1

                    def tn_b(t, e=e, it=it, mb=mb, wb=wb):
                        if (e, it - 1) in scB:
                            t.wait_ge(P.sev.h, scB[(e, it - 1)])
                        elif e > 0:
                            t.wait_ge(P.sev.h, scB[(e - 1, 31)])
                        for j in range(4, 8):
                            for k in range(DT):
                                i2 = t.matmul(P.pbk[j], wv(wgrp[wb], k, j, 8), ct(mov[mb], k),
                                              start=(k == 0), stop=(k == DT - 1))
                        P.smm.inc(i2, 1)
                        mm1[(e, it)] = P.smm.n
                        return i2, 1

                    def sc_b(v, e=e, it=it, hb=hb):
                        v.wait_ge(P.smm.h, P.smm.n)
                        v.wait_ge(s_b.h, s_b.n)
                        for j in range(4, 8):
                            i2 = v.tensor_scalar(out=ct(hch[hb], j), in0=P.pbk[j],
                                                 scalar1=b1t[:, (it // 8) * 8 + j: (it // 8) * 8 + j + 1],
                                                 scalar2=0.0, op0=ALU.add, op1=ALU.max)
                        P.sev.inc(i2, 1)
                        scB[(e, it)] = P.sev.n
                        return i2, 1

                    def sy_h(sync, e=e, it=it, c=c, mg=mg, hb=hb):
                        sync.wait_ge(P.ssc.h, scA[(e, it)])
                        sync.wait_ge(P.sev.h, scB[(e, it)])
                        dst = hsp[mg * 1024:(mg + 1) * 1024, c * CH:(c + 1) * CH]
                        i2 = sync.dma_start(out=dst.rearrange("(k p) n -> p k n", p=128),
                                            in_=sview(hch[hb]))
                        s_hst[hb].inc(i2, 16)
                        hst[(e, it)] = s_hst[hb].n
                        return i2, 16
                    P.seq(f"m2m1_{e}_{it}", [("sync", sy_x), ("tensor", tn_a), ("scalar", sc_a),
                                             ("tensor", tn_b), ("vector", sc_b), ("sync", sy_h)])

            for c in range(NCH):
                yb = c % 2

                def sy_m(sync, e=e, c=c):
                    if c == 0:
                        sync.wait_ge(s_hst[0].h, s_hst[0].n)
                        sync.wait_ge(s_hst[1].h, s_hst[1].n)
                        sync.wait_ge(s_w2.h, s_w2.n)
                    P.sld.inc(sync.dma_start(out=mrow, in_=hmb[e:e + 1, c * CH:(c + 1) * CH]), 16)
                    i2 = sync.dma_start(out=mrowf, in_=hm[e:e + 1, c * CH:(c + 1) * CH])
                    return s_msk.inc(i2, 16), 16

                def tn_r(t, e=e, c=c):
                    t.wait_ge(s_msk.h, s_msk.n)
                    t.wait_ge(s_b.h, s_b.n)
                    if c == 0:
                        t.wait_ge(P.ssc.h, P.ssc.n)
                    if (e, c - 1) in vy:
                        t.wait_ge(P.sev.h, vy[(e, c - 1)])
                    elif e > 0:
                        t.wait_ge(P.sev.h, vy[(e - 1, 7)])
                    i2 = t.matmul(P.pbk[0], P.ones_f1, mrowf, start=True, stop=True)
                    return P.smm.inc(i2, 1), 1

                def ve_r(v, e=e, c=c):
                    v.wait_ge(P.smm.h, P.smm.n)
                    i2 = v.tensor_copy(out=mrep, in_=P.pbk[0])
                    return P.sev.inc(i2, 1), 1
                P.seq(f"m2mr_{e}_{c}", [("sync", sy_m), ("tensor", tn_r), ("vector", ve_r)])

                for kg in range(4):
                    gb2 = kg % 2

                    def sy_g(sync, e=e, c=c, kg=kg, gb2=gb2):
                        key = (e, c, kg - 2) if kg >= 2 else (e, c - 1, kg + 2)
                        if key in mm2:
                            sync.wait_ge(P.smm.h, mm2[key])
                        elif e > 0:
                            sync.wait_ge(P.smm.h, mm2[(e - 1, 7, kg)])
                        sl = hsp[kg * 1024:(kg + 1) * 1024, c * CH:(c + 1) * CH]
                        i2 = sync.dma_start(out=sview(h2s[gb2]),
                                            in_=sl.rearrange("(k p) n -> p k n", p=128))
                        return s_h2[gb2].inc(i2, 16), 16

                    def tn_2(t, e=e, c=c, kg=kg, gb2=gb2):
                        t.wait_ge(s_h2[gb2].h, s_h2[gb2].n)
                        if kg == 0:
                            t.wait_ge(P.sev.h, P.sev.n)
                        for m in range(DT):
                            for k in range(8):
                                i2 = t.matmul(P.pbk[m], wv(w2f, kg * 8 + k, m, DT), ct(h2s[gb2], k),
                                              start=(kg == 0 and k == 0), stop=False)
                        P.smm.inc(i2, 1)
                        mm2[(e, c, kg)] = P.smm.n
                        return i2, 1
                    P.seq(f"m2m2_{e}_{c}_{kg}", [("sync", sy_g), ("tensor", tn_2)])

                def tn_bias(t, e=e, c=c):
                    for m in range(DT):
                        i2 = t.matmul(P.pbk[m], b2t[:, m * 128:(m + 1) * 128], mrow,
                                      start=False, stop=True)
                    P.smm.inc(i2, 1)
                    rec["mmlast"] = P.smm.n
                    return i2, 1

                def ve_y(v, e=e, c=c, yb=yb):
                    v.wait_ge(P.smm.h, P.smm.n)
                    tgt = max(s_acc.n - 16, 0)
                    if tgt:
                        v.wait_ge(s_acc.h, tgt)
                    for m in range(DT):
                        i2 = v.tensor_mul(ct(yacc[yb], m), P.pbk[m], mrep)
                    P.sev.inc(i2, 1)
                    vy[(e, c)] = P.sev.n
                    return i2, 1

                def gp_acc(g, e=e, c=c, yb=yb):
                    g.wait_ge(P.sev.h, vy[(e, c)])
                    g.wait_ge(s_acc.h, s_acc.n)
                    kw = {} if e == 0 else {"accum_op": ALU.add}
                    i2 = g.dma_start(out=cview(hg, c), in_=sview(yacc[yb]), **kw)
                    return s_acc.inc(i2, 16), 16
                P.seq(f"m2y_{e}_{c}", [("tensor", tn_bias), ("vector", ve_y), ("gpsimd", gp_acc)])
        fin(P, "moe2")



def ph_l1_delta2(P):
    hq = P.d("qs_f", [D, L])
    hk = P.d("ks_f", [D, L])
    hv = P.d("vs_f", [D, L])
    hbeta = P.d("beta_f", [H, L])
    hdelta = P.d("delta_f", [D, L])
    NT = 2
    with contextlib.ExitStack() as es:
        qn = P.sb(es, "d2_qn", [128, NT * L], BF16)
        kn = P.sb(es, "d2_kn", [128, NT * L], BF16)
        vb = P.sb(es, "d2_vb", [128, NT * L], BF16)
        kb = P.sb(es, "d2_kb", [128, NT * L], BF16)
        uwa = P.sb(es, "d2_uw", [128, NCK * 512], BF16)
        atta = P.sb(es, "d2_at", [128, NCK * 128], BF16)
        ktma = P.sb(es, "d2_kt", [128, NCK * 256], BF16)
        wfma = P.sb(es, "d2_wf", [128, NCK * 256], BF16)
        xf = P.sb(es, "d2_xf", [128, NT * CH], F32)
        xf2 = P.sb(es, "d2_xf2", [128, NT * CH], F32)
        sq = P.sb(es, "d2_sq", [128, NT * CH], BF16)
        brow = P.sb(es, "d2_br", [1, L], F32)
        brep = P.sb(es, "d2_bp", [128, CH], F32)
        rrep = P.sb(es, "d2_rr", [128, CH], F32)
        S_f = P.sb(es, "d2_Sf", [128, NT * 256], F32)
        S_b = P.sb(es, "d2_Sb", [128, NT * 256], BF16)
        npw = [P.sb(es, f"d2_np{i}", [128, 2 * 128], BF16) for i in range(2)]
        spw = [P.sb(es, f"d2_sp{i}", [128, 7 * 128], BF16) for i in range(2)]
        xpp = [P.sb(es, f"d2_x{i}", [128, 2 * 512], BF16) for i in range(2)]
        uadj = P.sb(es, "d2_ua", [128, 256], BF16)
        otm = P.sb(es, "d2_ot", [128, 256], F32)
        ofm = P.sb(es, "d2_of", [128, 256], F32)
        idb = P.ident_b
        rA = {}
        rB = {}

        def qc(buf, tt, ck, w=CK):
            return buf[:, tt * L + ck * CK: tt * L + ck * CK + w]

        for h in range(H):
            P.scope = f"dp{h}"
            def syb(sync, h=h):
                P.w(sync)
                return P.sld.inc(sync.dma_start(out=brow, in_=hbeta[h:h + 1, :]), 16), 16
            P.seq(f"d2b{h}", [("sync", syb)])

            for c in range(NCH):
                def sy0(sync, h=h, c=c):
                    P.w(sync)
                    src = hq[h * 256:(h + 1) * 256, c * CH:(c + 1) * CH]
                    P.sld.inc(sync.dma_start(out=xf.rearrange("p (t n) -> p t n", n=CH),
                                             in_=src.rearrange("(t p) n -> p t n", p=128)), 16)
                    src2 = hk[h * 256:(h + 1) * 256, c * CH:(c + 1) * CH]
                    return P.sld.inc(sync.dma_start(
                        out=xf2.rearrange("p (t n) -> p t n", n=CH),
                        in_=src2.rearrange("(t p) n -> p t n", p=128)), 16), 16

                def tn_r(t, h=h, c=c):
                    P.w(t)
                    i2 = t.matmul(P.pbk[2], P.ones_f1, brow[:, c * CH:(c + 1) * CH],
                                  start=True, stop=True)
                    return P.smm.inc(i2, 1), 1

                def sc_q(v, h=h, c=c):
                    P.w(v)
                    i2 = v.tensor_mul(sview(sq, CH), sview(xf, CH), sview(xf, CH))
                    return P.sev.inc(i2, 1), 1

                def tn_q(t, h=h, c=c):
                    P.w(t)
                    for tt in range(NT):
                        i2 = t.matmul(P.pbk[0], P.ones_b, ct(sq, tt, CH),
                                      start=(tt == 0), stop=(tt == NT - 1))
                    return P.smm.inc(i2, 1), 1

                def ve_q(v, h=h, c=c):
                    P.w(v)
                    v.tensor_copy(out=brep, in_=P.pbk[2])
                    v.tensor_scalar(out=rrep, in0=P.pbk[0], scalar1=1e-6, scalar2=None,
                                    op0=ALU.add)
                    i2 = v.reciprocal(out=rrep, in_=rrep)
                    return P.sev.inc(i2, 1), 1

                def sc_q2(s, h=h, c=c):
                    P.w(s)
                    i2 = s.activation(out=rrep, in_=rrep, func=AF.Sqrt)
                    return P.ssc.inc(i2, 1), 1

                def ve_q3(v, h=h, c=c):
                    P.w(v)
                    for tt in range(NT):
                        i2 = v.tensor_mul(qn[:, tt * L + c * CH: tt * L + (c + 1) * CH],
                                          ct(xf, tt, CH), rrep)
                    return P.sev.inc(i2, 1), 1
                P.seq(f"d2q{h}_{c}", [("sync", sy0), ("tensor", tn_r), ("vector", sc_q),
                                      ("tensor", tn_q), ("vector", ve_q), ("scalar", sc_q2),
                                      ("vector", ve_q3)])

                def sc_k(v, h=h, c=c):
                    P.w(v)
                    i2 = v.tensor_mul(sview(sq, CH), sview(xf2, CH), sview(xf2, CH))
                    return P.sev.inc(i2, 1), 1

                def tn_k(t, h=h, c=c):
                    P.w(t)
                    for tt in range(NT):
                        i2 = t.matmul(P.pbk[0], P.ones_b, ct(sq, tt, CH),
                                      start=(tt == 0), stop=(tt == NT - 1))
                    return P.smm.inc(i2, 1), 1

                def ve_k(v, h=h, c=c):
                    P.w(v)
                    v.tensor_scalar(out=rrep, in0=P.pbk[0], scalar1=1e-6, scalar2=None,
                                    op0=ALU.add)
                    i2 = v.reciprocal(out=rrep, in_=rrep)
                    return P.sev.inc(i2, 1), 1

                def sc_k2(s, h=h, c=c):
                    P.w(s)
                    i2 = s.activation(out=rrep, in_=rrep, func=AF.Sqrt)
                    return P.ssc.inc(i2, 1), 1

                def ve_k3(v, h=h, c=c):
                    P.w(v)
                    for tt in range(NT):
                        v.tensor_mul(ct(xf2, tt, CH), ct(xf2, tt, CH), rrep)
                        v.tensor_copy(out=kn[:, tt * L + c * CH: tt * L + (c + 1) * CH],
                                      in_=ct(xf2, tt, CH))
                        i2 = v.tensor_mul(kb[:, tt * L + c * CH: tt * L + (c + 1) * CH],
                                          ct(xf2, tt, CH), brep)
                    return P.sev.inc(i2, 1), 1
                P.seq(f"d2k{h}_{c}", [("vector", sc_k), ("tensor", tn_k), ("vector", ve_k),
                                      ("scalar", sc_k2), ("vector", ve_k3)])

                def sy_v(sync, h=h, c=c):
                    P.w(sync)
                    src = hv[h * 256:(h + 1) * 256, c * CH:(c + 1) * CH]
                    return P.sld.inc(sync.dma_start(
                        out=xf.rearrange("p (t n) -> p t n", n=CH),
                        in_=src.rearrange("(t p) n -> p t n", p=128)), 16), 16

                def ve_v(v, h=h, c=c):
                    P.w(v)
                    for tt in range(NT):
                        i2 = v.tensor_mul(vb[:, tt * L + c * CH: tt * L + (c + 1) * CH],
                                          ct(xf, tt, CH), brep)
                    return P.sev.inc(i2, 1), 1
                P.seq(f"d2v{h}_{c}", [("sync", sy_v), ("vector", ve_v)])

            P.scope = f"da{h}"
            def passA_seqs(ck):
                seqs = []
                _add = seqs.append
                _emit_one(ck, _add)
                return seqs

            def _emit_pair(h):
                for ck0 in range(0, NCK, 2):
                    sa = passA_seqs(ck0)
                    sb = passA_seqs(ck0 + 1)
                    n = max(len(sa), len(sb))
                    for j in range(n):
                        if j < len(sa):
                            P.seq(*sa[j])
                        if j < len(sb):
                            P.seq(*sb[j])

            def _emit_one(ck, _add):
                par = ck % 2
                bA = P.pbk[0 + par]
                bD = P.pbk[2 + par]
                bDT = P.pbk[2 + par].bitcast(BF16)
                bX = P.pbk[4 + par]

                def tA(t, h=h, ck=ck, par=par, bA=bA):
                    if ck == 0:
                        t.wait_ge(P.sev.h, P.sev.n)
                        t.wait_ge(P.ssc.h, P.ssc.n)
                    if (h, ck - 2, "mask") in rA:
                        t.wait_ge(P.sev.h, rA[(h, ck - 2, "mask")])
                    for tt in range(NT):
                        t.matmul(bA[:, 0:128], qc(kb, tt, ck), qc(kn, tt, ck),
                                 start=(tt == 0), stop=(tt == NT - 1))
                    for tt in range(NT):
                        i2 = t.matmul(bA[:, 128:256], qc(kn, tt, ck), qc(qn, tt, ck),
                                      start=(tt == 0), stop=(tt == NT - 1))
                    P.smm.inc(i2, 1)
                    rA[(h, ck, "A")] = P.smm.n
                    return i2, 1

                def vM(v, h=h, ck=ck, par=par, bA=bA):
                    v.wait_ge(P.smm.h, rA[(h, ck, "A")])
                    if (h, ck - 2, "d6") in rA:
                        v.wait_ge(P.sev.h, rA[(h, ck - 2, "d6")])
                    v.tensor_mul(npw[par][:, 0:128], bA[:, 0:128], P.mask_lo)
                    i2 = v.tensor_mul(atta[:, ck * 128:(ck + 1) * 128], bA[:, 128:256], P.mask_upi)
                    P.sev.inc(i2, 1)
                    rA[(h, ck, "mask")] = P.sev.n
                    return i2, 1
                _add((f"d2A{h}_{ck}", [("tensor", tA), ("vector", vM)]))

                def tT(t, h=h, ck=ck, par=par):
                    t.wait_ge(P.sev.h, rA[(h, ck, "mask")])
                    if (h, ck - 1, "trev") in rA:
                        t.wait_ge(P.sev.h, rA[(h, ck - 1, "trev")])
                    t.transpose(P.pbt[:, 0:128], npw[par][:, 0:128], idb)
                    for tt in range(NT):
                        t.transpose(P.pbt[:, (1 + tt) * 128:(2 + tt) * 128], qc(vb, tt, ck), idb)
                    for tt in range(NT):
                        t.transpose(P.pbt[:, (3 + tt) * 128:(4 + tt) * 128], qc(kb, tt, ck), idb)
                    for tt in range(NT):
                        i2 = t.transpose(P.pbt[:, (5 + tt) * 128:(6 + tt) * 128], qc(kn, tt, ck), idb)
                    P.smm.inc(i2, 1)
                    rA[(h, ck, "tr")] = P.smm.n
                    return i2, 1

                def sT(v, h=h, ck=ck, par=par):
                    v.wait_ge(P.smm.h, rA[(h, ck, "tr")])
                    if (h, ck - 2, "x6") in rA:
                        v.wait_ge(P.sev.h, rA[(h, ck - 2, "x6")])
                    v.tensor_copy(out=spw[par][:, 0:128], in_=P.pbt[:, 0:128])
                    v.tensor_copy(out=xpp[par][:, 0:512], in_=P.pbt[:, 128:640])
                    i2 = v.tensor_copy(out=ktma[:, ck * 256:(ck + 1) * 256],
                                       in_=P.pbt[:, 640:896])
                    P.sev.inc(i2, 1)
                    rA[(h, ck, "trev")] = P.sev.n
                    return i2, 1
                _add((f"d2T{h}_{ck}", [("tensor", tT), ("vector", sT)]))

                for j in range(1, 7):
                    def tD(t, h=h, ck=ck, par=par, j=j, bD=bD):
                        t.wait_ge(P.sev.h, rA[(h, ck, "trev" if j == 1 else f"d{j - 1}")])
                        cn = npw[par][:, ((j - 1) % 2) * 128:((j - 1) % 2 + 1) * 128]
                        sp = spw[par][:, (j - 1) * 128:j * 128]
                        t.matmul(bD[:, 0:128], sp, cn, start=True, stop=True)
                        i2 = t.matmul(bD[:, 128:256], cn, sp, start=True, stop=True)
                        P.smm.inc(i2, 1)
                        rA[(h, ck, f"dm{j}")] = P.smm.n
                        return i2, 1

                    def sD(v, h=h, ck=ck, par=par, j=j, bD=bD):
                        v.wait_ge(P.smm.h, rA[(h, ck, f"dm{j}")])
                        v.tensor_copy(out=npw[par][:, (j % 2) * 128:(j % 2 + 1) * 128],
                                      in_=bD[:, 0:128])
                        i2 = v.tensor_copy(out=spw[par][:, j * 128:(j + 1) * 128],
                                           in_=bD[:, 128:256])
                        P.sev.inc(i2, 1)
                        rA[(h, ck, f"d{j}")] = P.sev.n
                        return i2, 1
                    _add((f"d2D{h}_{ck}_{j}", [("tensor", tD), ("vector", sD)]))

                for jj, j in enumerate([6, 5, 4, 3, 2]):
                    def tX(t, h=h, ck=ck, par=par, j=j, jj=jj, bX=bX):
                        t.wait_ge(P.sev.h, rA[(h, ck, "d6" if jj == 0 else f"x{jj - 1}")])
                        src_ = xpp[par][:, (jj % 2) * 512:(jj % 2) * 512 + 512]
                        t.matmul(bX, spw[par][:, j * 128:(j + 1) * 128], src_,
                                 start=True, stop=False)
                        i2 = t.matmul(bX, idb, src_, start=False, stop=True)
                        P.smm.inc(i2, 1)
                        rA[(h, ck, f"xm{jj}")] = P.smm.n
                        return i2, 1

                    def sX(v, h=h, ck=ck, par=par, jj=jj, bX=bX):
                        v.wait_ge(P.smm.h, rA[(h, ck, f"xm{jj}")])
                        i2 = v.tensor_copy(out=xpp[par][:, ((jj + 1) % 2) * 512:((jj + 1) % 2) * 512 + 512],
                                           in_=bX)
                        P.sev.inc(i2, 1)
                        rA[(h, ck, f"x{jj}")] = P.sev.n
                        return i2, 1
                    _add((f"d2X{h}_{ck}_{jj}", [("tensor", tX), ("vector", sX)]))

                def tX1(t, h=h, ck=ck, par=par, bX=bX):
                    t.wait_ge(P.sev.h, rA[(h, ck, "x4")])
                    src_ = xpp[par][:, 512:1024]
                    t.matmul(bX, spw[par][:, 128:256], src_, start=True, stop=False)
                    i2 = t.matmul(bX, idb, src_, start=False, stop=True)
                    P.smm.inc(i2, 1)
                    rA[(h, ck, "xm5")] = P.smm.n
                    return i2, 1

                def sX1(v, h=h, ck=ck, par=par, bX=bX):
                    v.wait_ge(P.smm.h, rA[(h, ck, "xm5")])
                    i2 = v.tensor_copy(out=xpp[par][:, 0:512], in_=bX)
                    P.sev.inc(i2, 1)
                    rA[(h, ck, "x5")] = P.sev.n
                    return i2, 1
                _add((f"d2X1{h}_{ck}", [("tensor", tX1), ("vector", sX1)]))

                def tX0(t, h=h, ck=ck, par=par, bX=bX):
                    t.wait_ge(P.sev.h, rA[(h, ck, "x5")])
                    i2 = t.matmul(bX, spw[par][:, 0:128], xpp[par][:, 0:512],
                                  start=True, stop=True)
                    P.smm.inc(i2, 1)
                    rA[(h, ck, "xm6")] = P.smm.n
                    return i2, 1

                def vX0(v, h=h, ck=ck, par=par, bX=bX):
                    v.wait_ge(P.smm.h, rA[(h, ck, "xm6")])
                    i2 = v.tensor_sub(uwa[:, ck * 512:(ck + 1) * 512], xpp[par][:, 0:512], bX)
                    P.sev.inc(i2, 1)
                    rA[(h, ck, "x6")] = P.sev.n
                    return i2, 1
                _add((f"d2X0{h}_{ck}", [("tensor", tX0), ("vector", vX0)]))

                def tW(t, h=h, ck=ck, par=par, bDT=bDT):
                    t.wait_ge(P.sev.h, rA[(h, ck, "x6")])
                    if (h, ck, "d6") in rA:
                        t.wait_ge(P.sev.h, rA[(h, ck, "d6")])
                    for tt in range(NT):
                        i2 = t.transpose(bDT[:, tt * 128:(tt + 1) * 128],
                                         uwa[:, ck * 512 + 256 + tt * 128: ck * 512 + 256 + (tt + 1) * 128],
                                         idb)
                    P.smm.inc(i2, 1)
                    rA[(h, ck, "wm")] = P.smm.n
                    return i2, 1

                def sW(v, h=h, ck=ck, par=par, bDT=bDT):
                    v.wait_ge(P.smm.h, rA[(h, ck, "wm")])
                    i2 = v.tensor_copy(out=wfma[:, ck * 256:(ck + 1) * 256],
                                       in_=bDT[:, 0:256])
                    P.sev.inc(i2, 1)
                    rA[(h, ck, "wev")] = P.sev.n
                    return i2, 1
                _add((f"d2W{h}_{ck}", [("tensor", tW), ("vector", sW)]))

            _emit_pair(h)

            P.scope = f"db{h}"
            def gp_s(g, h=h):
                P.w(g)
                g.memset(S_f, 0.0)
                return P.sgp.inc(g.memset(S_b, 0.0), 1), 1
            P.seq(f"d2s0_{h}", [("gpsimd", gp_s)])

            for ck in range(NCK):
                def tU(t, h=h, ck=ck):
                    if ck == 0:
                        P.w(t)
                    else:
                        t.wait_ge(P.sev.h, rB[(h, ck - 1, "cast")])
                    for tt in range(NT):
                        i2 = t.matmul(P.pbk[6][:, 0:256], wfma[:, ck * 256 + tt * 128: ck * 256 + (tt + 1) * 128],
                                      S_b[:, tt * 256:(tt + 1) * 256],
                                      start=(tt == 0), stop=(tt == NT - 1))
                    P.smm.inc(i2, 1)
                    rB[(h, ck, "u")] = P.smm.n
                    return i2, 1

                def vU(v, h=h, ck=ck):
                    v.wait_ge(P.smm.h, rB[(h, ck, "u")])
                    i2 = v.tensor_sub(uadj, uwa[:, ck * 512: ck * 512 + 256], P.pbk[6][:, 0:256])
                    P.sev.inc(i2, 1)
                    rB[(h, ck, "ua")] = P.sev.n
                    return i2, 1
                P.seq(f"d2U{h}_{ck}", [("tensor", tU), ("vector", vU)])

                def tO(t, h=h, ck=ck):
                    t.wait_ge(P.sev.h, rB[(h, ck, "ua")])
                    for tt in range(NT):
                        t.matmul(P.pbk[6][:, 256:512], qc(qn, tt, ck), S_b[:, tt * 256:(tt + 1) * 256],
                                 start=(tt == 0), stop=False)
                    t.matmul(P.pbk[6][:, 256:512], atta[:, ck * 128:(ck + 1) * 128], uadj,
                             start=False, stop=True)
                    t.matmul(P.pbk[0][:, 0:256], ktma[:, ck * 256: ck * 256 + 128], uadj,
                             start=True, stop=True)
                    i2 = t.matmul(P.pbk[0][:, 256:512], ktma[:, ck * 256 + 128: ck * 256 + 256], uadj,
                                  start=True, stop=True)
                    P.smm.inc(i2, 1)
                    rB[(h, ck, "o")] = P.smm.n
                    return i2, 1

                def vO(v, h=h, ck=ck):
                    v.wait_ge(P.smm.h, rB[(h, ck, "o")])
                    v.tensor_copy(out=otm, in_=P.pbk[6][:, 256:512])
                    v.tensor_add(S_f[:, 0:256], S_f[:, 0:256], P.pbk[0][:, 0:256])
                    i2 = v.tensor_add(S_f[:, 256:512], S_f[:, 256:512], P.pbk[0][:, 256:512])
                    P.sev.inc(i2, 1)
                    rB[(h, ck, "sf")] = P.sev.n
                    return i2, 1

                def sS(v, h=h, ck=ck):
                    i2 = v.tensor_copy(out=S_b, in_=S_f)
                    P.sev.inc(i2, 1)
                    rB[(h, ck, "cast")] = P.sev.n
                    return i2, 1
                P.seq(f"d2O{h}_{ck}", [("tensor", tO), ("vector", vO), ("vector", sS)])

                def tF(t, h=h, ck=ck):
                    t.wait_ge(P.sev.h, rB[(h, ck, "sf")])
                    if (h, ck - 1, "oev") in rB:
                        t.wait_ge(P.sev.h, rB[(h, ck - 1, "oev")])
                    for tt in range(NT):
                        i2 = t.transpose(P.pbk[1][:, tt * 128:(tt + 1) * 128],
                                         otm[:, tt * 128:(tt + 1) * 128], P.ident_f)
                    P.smm.inc(i2, 1)
                    rB[(h, ck, "of")] = P.smm.n
                    return i2, 1

                def vF(v, h=h, ck=ck):
                    v.wait_ge(P.smm.h, rB[(h, ck, "of")])
                    if (h, ck - 1, "st") in rB:
                        v.wait_ge(P.sld.h, rB[(h, ck - 1, "st")])
                    i2 = v.tensor_copy(out=ofm, in_=P.pbk[1][:, 0:256])
                    P.sev.inc(i2, 1)
                    rB[(h, ck, "oev")] = P.sev.n
                    return i2, 1

                def syF(sync, h=h, ck=ck):
                    sync.wait_ge(P.sev.h, rB[(h, ck, "oev")])
                    dst = hdelta[h * 256:(h + 1) * 256, ck * CK: (ck + 1) * CK]
                    i2 = sync.dma_start(out=dst.rearrange("(t p) n -> p t n", p=128),
                                        in_=ofm.rearrange("p (t n) -> p t n", n=128))
                    P.sld.inc(i2, 16)
                    rB[(h, ck, "st")] = P.sld.n
                    return i2, 16
                P.seq(f"d2F{h}_{ck}", [("tensor", tF), ("vector", vF), ("sync", syF)])

        fin(P, "delta2")


def build_l0(ext_out=("x1_f",), dbg=(), upto=99):
    W = ["x0", "wgT", "wcT", "w1T", "w3T", "w2T", "lnw", "ffnw",
         "wgate", "wup", "wdown"]
    P = Prog(ext_in=W, ext_out=set(ext_out) | set(dbg))
    P.open()
    ph_const(P)
    P.lnw_c = colvec(P, P.es, "lnw", D)
    P.ffnw_c = colvec(P, P.es, "ffnw", D)
    phases = [
        lambda: ph_ln(P, "x0", "xn_f", "xn_b", "ln0"),
        lambda: ph_rms(P, "xn_f", P.lnw_c, "nin_b", "rms0", 1e-6),
        lambda: ph_l0_projscan(P),
        lambda: ph_rms(P, "hx_f", P.ffnw_c, "hn_b", "rms1", 1e-6),
        lambda: ph_l0_mlp1(P),
        lambda: ph_l0_mlp2(P),
        lambda: [P._emit(nm, st) for nm, st in gen_fp4_weights(P)],
        lambda: ph_fp4_x(P),
        lambda: ph_fp4_mlp2(P),
        lambda: ph_l0_down(P),
    ]
    for i, f in enumerate(phases):
        if i < upto:
            f()
    P.close()
    return P


def host_l0_inputs(inputs, b):
    T = lambda a: np.ascontiguousarray(np.asarray(a, np.float32).T)
    A = lambda a: np.ascontiguousarray(np.asarray(a, np.float32))
    return {
        "x0": T(inputs["L0_attn_inp"][b]),
        "wgT": T(inputs["L0_attn_Wg"]),
        "wcT": T(inputs["L0_attn_Wc"]),
        "w1T": T(inputs["L0_attn_w1"]),
        "w3T": T(inputs["L0_attn_w3"]),
        "w2T": T(inputs["L0_attn_w2"]),
        "lnw": host_colvec(inputs["L0_attn_ln_w"]),
        "ffnw": host_colvec(inputs["L0_attn_ffn_w"]),
        "wgate": A(inputs["L0_ffn_w_gate"]),
        "wup": A(inputs["L0_ffn_w_up"]),
        "wdown": A(inputs["L0_ffn_w_down"]),
    }


L1_W = ["x1_f", "wq", "wk", "wv", "wb", "igw", "igb", "qcw", "kcw", "vcw",
        "fird", "rw1", "rb1", "rw2", "rb2",
        "ltau", "onw", "wo", "gwT", "gb", "mw1T", "b1c", "mw2T", "b2e"]


def build_l1(ext_out=("out",), dbg=(), upto=99):
    P = Prog(ext_in=L1_W, ext_out=set(ext_out) | set(dbg))
    P.open()
    ph_const(P)
    phases = [
        lambda: ph_ln(P, "x1_f", "xn1_f", "xn1_b", "ln1"),
        lambda: ph_l1_qkv(P),
        lambda: ph_l1_conv(P),
        lambda: ph_l1_fir2(P),
        lambda: ph_l1_delta(P),
        lambda: ph_l1_w1bf(P),
        lambda: ph_l1_router(P),
        lambda: ph_l1_ctx(P),
        lambda: ph_l1_gate(P),
        lambda: ph_l1_moe(P),
        lambda: ph_final(P),
    ]
    for i, f in enumerate(phases):
        if i < upto:
            f()
    P.close()
    return P


def host_l1_inputs(inputs, b, x1=None):
    T = lambda a: np.ascontiguousarray(np.asarray(a, np.float32).T)
    A = lambda a: np.ascontiguousarray(np.asarray(a, np.float32))
    g = lambda k: inputs["L1_attn_" + k]
    d = {
        "wq": A(g("Wq")), "wk": A(g("Wk")), "wv": A(g("Wv")), "wb": A(g("Wb")),
        "igw": A(g("id_gate_W")), "igb": A(g("id_gate_b")),
        "qcw": host_colvec2d(g("q_conv_w")), "kcw": host_colvec2d(g("k_conv_w")),
        "vcw": host_colvec2d(g("v_conv_w")),
        "fird": host_fir_diag(inputs),
        "rw1": A(g("router_W1")), "rb1": host_colvec(g("router_b1")),
        "rw2": A(g("router_W2")), "rb2": A(g("router_b2")),
        "ltau": A(g("log_tau")), "onw": host_colvec(g("o_norm_w")),
        "wo": A(g("Wo")),
        "gwT": T(inputs["L1_ffn_gate_w"]), "gb": A(inputs["L1_ffn_gate_b"]),
        "mw1T": np.ascontiguousarray(np.asarray(inputs["L1_ffn_w1"], np.float32).transpose(0, 2, 1)),
        "b1c": np.stack([host_colvec(b).reshape(128, HMT)
                         for b in np.asarray(inputs["L1_ffn_b1"], np.float32)]),
        "mw2T": np.ascontiguousarray(np.asarray(inputs["L1_ffn_w2"], np.float32).transpose(0, 2, 1)),
        "b2e": A(inputs["L1_ffn_b2"]),
    }
    if x1 is not None:
        d["x1_f"] = np.ascontiguousarray(x1)
    return d


def build_full():
    W0 = ["x0", "wgT", "wcT", "w1T", "w3T", "w2T", "lnw", "ffnw",
          "wgate", "wup", "wdown"]
    P = Prog(ext_in=set(W0) | (set(L1_W) - {"x1_f"}), ext_out={"out"})
    P.open()
    def start_drip():
        P._drip = gen_fp4_weights(P)
        P._drip_scope = "fp4w"
        P._drip_k = 2
        save = P.scope
        P.scope = "fp4w"
        P._dripping = True
        try:
            nm, st = next(P._drip)
            P._emit(nm, st)
        finally:
            P.scope = save
            P._dripping = False
    for tag, fn in [
        ("const", lambda: (ph_const(P),
                           setattr(P, "lnw_c", colvec(P, P.es, "lnw", D)),
                           setattr(P, "ffnw_c", colvec(P, P.es, "ffnw", D)))),
        ("drip0", start_drip),
        ("ln0", lambda: ph_ln(P, "x0", "xn_f", "xn_b", "ln0")),
        ("rms0", lambda: ph_rms(P, "xn_f", P.lnw_c, "nin_b", "rms0", 1e-6)),
        ("pscan", lambda: ph_l0_projscan(P)),
        ("rms1", lambda: ph_rms(P, "hx_f", P.ffnw_c, "hn_b", "rms1", 1e-6)),
        ("mlp1", lambda: ph_l0_mlp1(P)),
        ("mlp2", lambda: ph_l0_mlp2(P)),
        ("fp4w", lambda: P.drip_flush()),
        ("fp4x", lambda: ph_fp4_x(P)),
        ("fp4mlp", lambda: ph_fp4_mlp2(P)),
        ("l0down", lambda: ph_l0_down(P)),
        ("ln1", lambda: ph_ln(P, "x1_f", "xn1_f", "xn1_b", "ln1")),
        ("qkv", lambda: ph_l1_qkv(P)),
        ("conv", lambda: ph_l1_conv(P)),
        ("fir", lambda: ph_l1_fir2(P)),
        ("delta", lambda: ph_l1_delta2(P)),
        ("w1bf", lambda: ph_l1_w1bf(P)),
        ("router", lambda: ph_l1_router(P)),
        ("ctx", lambda: ph_l1_ctx(P)),
        ("gate", lambda: ph_l1_gate(P)),
        ("moe", lambda: ph_l1_moe3(P)),
        ("final", lambda: ph_final(P)),
    ]:
        P.scope = tag
        fn()
    P.close()
    return P


def kernel(**inputs):
    n = 8
    P = build_full()
    maps = []
    for b in range(n):
        m = host_l0_inputs(inputs, b)
        m.update(host_l1_inputs(inputs, b))
        maps.append(m)
    r = run_bass_kernel_spmd(P.nc, maps, core_ids=list(range(n)))
    out = np.stack([np.ascontiguousarray(r.results[b]["out"].T) for b in range(n)])
    return out.astype(np.float32)


def ph_l1_moe3(P):
    hx1b = P.d("xn1_b", [D, L], BF16)
    hsp = P.d("hsp_b", [HM, L], BF16)
    hmb = P.d("masks_b", [EXP, L], BF16)
    hg = P.d("g_f", [D, L])
    hw1 = P.d("mw1T", [EXP, D, HM], F32)
    hw2 = P.d("mw2T", [EXP, HM, D], F32)
    hb1 = P.d("b1c", [EXP, 128, HMT], F32)
    hb2 = P.d("b2e", [EXP, D], F32)
    with contextlib.ExitStack() as es:
        mov = [P.sb(es, f"m3_x{i}", [128, DT * CH], BF16) for i in range(2)]
        wgrp = [P.sb(es, f"m3_w{i}", [128, DT * 8 * 128], BF16) for i in range(2)]
        hch = [P.sb(es, f"m3_h{i}", [128, 8 * CH], BF16) for i in range(2)]
        h2s = [P.sb(es, f"m3_g{i}", [128, 8 * CH], BF16) for i in range(4)]
        yac = [P.sb(es, f"m3_y{i}", [128, 4 * CH], F32) for i in range(2)]
        b1t = P.sb(es, "m3_b1", [128, HMT], F32)
        b2t = P.sb(es, "m3_b2", [1, D], BF16)
        mrw = [P.sb(es, f"m3_mr{i}", [1, L], BF16) for i in range(2)]
        mrep = [P.sb(es, f"m3_rp{i}", [128, CH], F32) for i in range(2)]
        w2f = P.sb(es, "m3_w2f", [128, HMT * DT * 128], BF16)
        s_mv = [P.newsem(f"m3mv{i}") for i in range(2)]
        s_wg = [P.newsem(f"m3wg{i}") for i in range(2)]
        s_hst = [P.newsem(f"m3hs{i}") for i in range(2)]
        s_h2 = [P.newsem(f"m3h2{i}") for i in range(4)]
        s_w2 = P.newsem("m3w2")
        s_b = P.newsem("m3b")
        s_acc = P.newsem("m3ac")

        gmm1 = {}
        scA = {}
        scB = {}
        hst = {}
        p2s = {}
        pm = {}
        vye = {}
        ver = {}
        sacc = {}
        rec = {"mmlast": 0}

        def mk_mrep(e, c):
            def tn_r(t, e=e, c=c):
                if c == 0:
                    t.wait_ge(P.ssc.h, scA[(e, 31)])
                    t.wait_ge(s_b.h, s_b.n)
                else:
                    t.wait_ge(P.sev.h, ver[(e, c - 1)])
                i2 = t.matmul(P.pbk[0], P.ones_b1, mrw[e % 2][:, c * CH:(c + 1) * CH],
                              start=True, stop=True)
                return P.smm.inc(i2, 1), 1

            def ve_r(v, e=e, c=c):
                v.wait_ge(P.smm.h, P.smm.n)
                i2 = v.tensor_copy(out=mrep[c % 2], in_=P.pbk[0])
                P.sev.inc(i2, 1)
                ver[(e, c)] = P.sev.n
                return i2, 1
            P.seq(f"m3r_{e}_{c}", [("tensor", tn_r), ("vector", ve_r)])

        for e in range(EXP):
            def gp_b(g, e=e):
                if e == 0:
                    P.w(g)
                else:
                    g.wait_ge(P.smm.h, rec["mmlast"])
                s_b.inc(g.dma_start(out=b1t, in_=hb1[e]), 16)
                s_b.inc(g.dma_start(out=b2t, in_=hb2[e:e + 1, :]), 16)
                i2 = g.dma_start(out=mrw[e % 2], in_=hmb[e:e + 1, :])
                return s_b.inc(i2, 16), 16
            P.seq(f"m3b{e}", [("gpsimd", gp_b)])

            for mg in range(4):
                def gp_w(g, e=e, mg=mg):
                    wb = mg % 2
                    gm = e * 4 + mg
                    if gm >= 2:
                        g.wait_ge(P.smm.h, gmm1[(gm - 2) * 8 + 7])
                    sl = hw1[e][:, mg * 1024:(mg + 1) * 1024]
                    i2 = g.dma_start(out=wgrp[wb].rearrange("p (k m q) -> p k m q", m=8, q=128),
                                     in_=sl.rearrange("(k p) (m q) -> p k m q", p=128, q=128))
                    s_wg[wb].inc(i2, 16)
                    qsl = hw2[e][mg * 1024:(mg + 1) * 1024, :]
                    if mg == 0 and e > 0:
                        g.wait_ge(P.smm.h, rec["mmlast"])
                    i2 = g.dma_start(
                        out=w2f.rearrange("p (k m q) -> p k m q", m=DT, q=128)[:, mg * 8:(mg + 1) * 8],
                        in_=qsl.rearrange("(k p) (m q) -> p k m q", p=128, q=128))
                    s_w2.inc(i2, 16)
                    return i2, 16
                P.seq(f"m3w{e}_{mg}", [("gpsimd", gp_w)])

                for c in range(NCH):
                    it = mg * 8 + c
                    git = e * 32 + it
                    mb, wb, hb = it % 2, mg % 2, it % 2

                    def sy_x(sync, c=c, mb=mb, git=git):
                        if git - 2 >= 0:
                            sync.wait_ge(P.smm.h, gmm1[git - 2])
                        i2 = sync.dma_start(out=sview(mov[mb]), in_=cview(hx1b, c))
                        return s_mv[mb].inc(i2, 16), 16

                    def tn_a(t, e=e, it=it, mb=mb, wb=wb, git=git):
                        if git == 0:
                            P.w(t)
                        t.wait_ge(s_mv[mb].h, s_mv[mb].n)
                        t.wait_ge(s_wg[wb].h, s_wg[wb].n)
                        if (e, it - 1) in scA:
                            t.wait_ge(P.ssc.h, scA[(e, it - 1)])
                        elif e > 0:
                            t.wait_ge(P.ssc.h, scA[(e - 1, 31)])
                            t.wait_ge(P.sev.h, ver[(e - 1, 7)])
                        for j in range(4):
                            for k in range(DT):
                                i2 = t.matmul(P.pbk[j], wv(wgrp[wb], k, j, 8), ct(mov[mb], k),
                                              start=(k == 0), stop=(k == DT - 1))
                        return P.smm.inc(i2, 1), 1

                    def sc_a(s, e=e, it=it, hb=hb):
                        s.wait_ge(P.smm.h, P.smm.n)
                        s.wait_ge(s_b.h, s_b.n)
                        sl = s_hst[hb]
                        s.wait_ge(sl.h, sl.n)
                        for j in range(4):
                            i2 = s.activation(out=ct(hch[hb], j), in_=P.pbk[j], func=AF.Relu,
                                              bias=b1t[:, (it // 8) * 8 + j: (it // 8) * 8 + j + 1])
                        P.ssc.inc(i2, 1)
                        scA[(e, it)] = P.ssc.n
                        return i2, 1

                    def tn_b(t, e=e, it=it, mb=mb, wb=wb, git=git):
                        if (e, it - 1) in scB:
                            t.wait_ge(P.sev.h, scB[(e, it - 1)])
                        elif e > 0:
                            t.wait_ge(P.sev.h, vye[(e - 1, 7, 1, 3)])
                        for j in range(4, 8):
                            for k in range(DT):
                                i2 = t.matmul(P.pbk[j], wv(wgrp[wb], k, j, 8), ct(mov[mb], k),
                                              start=(k == 0), stop=(k == DT - 1))
                        P.smm.inc(i2, 1)
                        gmm1[git] = P.smm.n
                        return i2, 1

                    def sc_b(v, e=e, it=it, hb=hb):
                        v.wait_ge(P.smm.h, P.smm.n)
                        v.wait_ge(s_b.h, s_b.n)
                        v.wait_ge(s_hst[hb].h, s_hst[hb].n)
                        for j in range(4, 8):
                            i2 = v.tensor_scalar(out=ct(hch[hb], j), in0=P.pbk[j],
                                                 scalar1=b1t[:, (it // 8) * 8 + j: (it // 8) * 8 + j + 1],
                                                 scalar2=0.0, op0=ALU.add, op1=ALU.max)
                        P.sev.inc(i2, 1)
                        scB[(e, it)] = P.sev.n
                        return i2, 1

                    def st_h(s, e=e, it=it, mg=mg, c=c, hb=hb):
                        s.wait_ge(P.sev.h, scB[(e, it)])
                        dst = hsp[mg * 1024:(mg + 1) * 1024, c * CH:(c + 1) * CH]
                        i2 = s.dma_start(out=dst.rearrange("(k p) n -> p k n", p=128),
                                         in_=sview(hch[hb]))
                        s_hst[hb].inc(i2, 16)
                        hst[(e, it)] = s_hst[hb].n
                        return i2, 16
                    P.seq(f"m3m1_{e}_{it}", [("sync", sy_x), ("tensor", tn_a), ("scalar", sc_a),
                                             ("tensor", tn_b), ("vector", sc_b), ("scalar", st_h)])

            for c in range(NCH):
                if c == 0:
                    mk_mrep(e, 0)
                for kg in range(4):
                    def sy_g(sync, e=e, c=c, kg=kg):
                        key = (e, c - 1, kg) if c > 0 else (e - 1, 7, kg)
                        if key in p2s:
                            sync.wait_ge(P.smm.h, p2s[key])
                        sh = s_hst[(kg * 8 + c) % 2]
                        sync.wait_ge(sh.h, hst[(e, kg * 8 + c)])
                        sl = hsp[kg * 1024:(kg + 1) * 1024, c * CH:(c + 1) * CH]
                        i2 = sync.dma_start(out=sview(h2s[kg]),
                                            in_=sl.rearrange("(k p) n -> p k n", p=128))
                        return s_h2[kg].inc(i2, 16), 16
                    P.seq(f"m3g_{e}_{c}_{kg}", [("sync", sy_g)])

                for p in range(2):
                    def tn_p(t, e=e, c=c, p=p):
                        if c == 0 and p == 0:
                            t.wait_ge(s_w2.h, s_w2.n)
                            t.wait_ge(s_b.h, s_b.n)
                        for m in range(4):
                            mt = p * 4 + m
                            if p == 1:
                                t.wait_ge(P.sev.h, vye[(e, c, 0, m)])
                            elif c > 0:
                                t.wait_ge(P.sev.h, vye[(e, c - 1, 1, m)])
                            elif m == 0:
                                if e > 0:
                                    t.wait_ge(P.sev.h, vye[(e - 1, 7, 1, 3)])
                                t.wait_ge(P.sev.h, scB[(e, 31)])
                            for k in range(HMT):
                                if p == 0 and m == 0 and k % 8 == 0:
                                    kg = k // 8
                                    t.wait_ge(s_h2[kg].h, s_h2[kg].n)
                                i2 = t.matmul(P.pbk[4 + m], wv(w2f, k, mt, DT),
                                              ct(h2s[k // 8], k % 8),
                                              start=(k == 0), stop=False)
                                if p == 1 and m == 3 and k % 8 == 7:
                                    P.smm.inc(i2, 1)
                                    p2s[(e, c, k // 8)] = P.smm.n
                            i2 = t.matmul(P.pbk[4 + m], b2t[:, mt * 128:(mt + 1) * 128],
                                          mrw[e % 2][:, c * CH:(c + 1) * CH],
                                          start=False, stop=True)
                            P.smm.inc(i2, 1)
                            pm[(e, c, p, m)] = P.smm.n
                        if p == 1 and c == NCH - 1:
                            rec["mmlast"] = P.smm.n
                        return i2, 1

                    def ve_y(v, e=e, c=c, p=p):
                        if (e, c - 1, p) in sacc:
                            v.wait_ge(s_acc.h, sacc[(e, c - 1, p)])
                        elif (e - 1, NCH - 1, p) in sacc:
                            v.wait_ge(s_acc.h, sacc[(e - 1, NCH - 1, p)])
                        for m in range(4):
                            v.wait_ge(P.smm.h, pm[(e, c, p, m)])
                            i2 = v.tensor_mul(ct(yac[p], m), P.pbk[4 + m], mrep[c % 2])
                            P.sev.inc(i2, 1)
                            vye[(e, c, p, m)] = P.sev.n
                        return i2, 1

                    def gp_acc(g, e=e, c=c, p=p):
                        g.wait_ge(P.sev.h, vye[(e, c, p, 3)])
                        g.wait_ge(s_acc.h, s_acc.n)
                        rows = hg[p * 512:(p + 1) * 512, c * CH:(c + 1) * CH]
                        kw = {} if e == 0 else {"accum_op": ALU.add}
                        i2 = g.dma_start(out=rows.rearrange("(k q) n -> q k n", q=128),
                                         in_=yac[p].rearrange("q (k n) -> q k n", n=CH), **kw)
                        s_acc.inc(i2, 16)
                        sacc[(e, c, p)] = s_acc.n
                        return i2, 16
                    P.seq(f"m3p_{e}_{c}_{p}", [("tensor", tn_p), ("vector", ve_y),
                                               ("gpsimd", gp_acc)])
                    if p == 0 and c + 1 < NCH:
                        mk_mrep(e, c + 1)

        def sy_end(sync):
            sync.wait_ge(s_acc.h, s_acc.n)
            return sync.nop(), 1
        P.seq("m3end", [("sync", sy_end)])
        fin(P, "moe3")


def emit_quant_steps3(P, steps, xin, xout, tmps, nblk, roll, guard_fn, tag):
    W = nblk * 16
    xa, sg, vm, cnt, r1, scl, rs = (tmps[k] for k in
                                    ("xa", "sg", "vm", "cnt", "r1", "scl", "rs"))

    def sA(v):
        v.tensor_scalar_mul(out=xa[:, 0:W], in0=xin, scalar1=-1.0)
        return P.sev.inc(v.tensor_max(xa[:, 0:W], xa[:, 0:W], xin), 1), 1

    def vA(v):
        if roll["sB"]:
            v.wait_ge(P.ssc.h, roll["sB"])
        xav = xa[:, 0:W].rearrange("p (b s) -> p b s", s=16)
        v.tensor_reduce(out=scl[:, 0:nblk], in_=xav, axis=AX.X, op=ALU.max,
                        apply_absolute_value=True)
        v.tensor_scalar(out=scl[:, 0:nblk], in0=scl[:, 0:nblk], scalar1=1.0 / 6.0,
                        scalar2=1e-12, op0=ALU.mult, op1=ALU.max)
        v.reciprocal(out=rs[:, 0:nblk], in_=scl[:, 0:nblk])
        rsb = rs[:, 0:nblk].broadcast_to([128, nblk, 16])
        vmv = vm[:, 0:W].rearrange("p (b s) -> p b s", s=16)
        v.tensor_tensor(out=vmv, in0=xav, in1=rsb, op=ALU.mult)
        v.tensor_scalar(out=cnt[:, 0:W], in0=vm[:, 0:W], scalar1=0.25, scalar2=None,
                        op0=ALU.is_gt)
        for t in QTHR:
            v.scalar_tensor_tensor(out=cnt[:, 0:W], in0=vm[:, 0:W], scalar=t,
                                   in1=cnt[:, 0:W], op0=ALU.is_gt, op1=ALU.add)
        i2 = v.scalar_tensor_tensor(out=cnt[:, 0:W], in0=vm[:, 0:W], scalar=5.0,
                                    in1=cnt[:, 0:W], op0=ALU.is_gt, op1=ALU.add)
        P.sev.inc(i2, 1)
        roll["vA"] = P.sev.n
        return i2, 1

    def sB(s):
        s.wait_ge(P.sev.h, roll["vA"])
        if roll["vB"]:
            s.wait_ge(P.sev.h, roll["vB"])
        s.activation(out=r1[:, 0:W], in_=cnt[:, 0:W], func=AF.Relu, bias=P.cc(-4.0))
        i2 = s.activation(out=vm[:, 0:W], in_=cnt[:, 0:W], func=AF.Relu, bias=P.cc(-6.0))
        P.ssc.inc(i2, 1)
        roll["sB"] = P.ssc.n
        return i2, 1

    def vB(v):
        v.wait_ge(P.ssc.h, roll["sB"])
        qtm_guard = guard_fn() if guard_fn else None
        if qtm_guard:
            v.wait_ge(P.smm.h, qtm_guard)
        v.tensor_scalar(out=sg[:, 0:W], in0=xin, scalar1=0.0, scalar2=None,
                        op0=ALU.is_gt)
        v.tensor_add(cnt[:, 0:W], cnt[:, 0:W], r1[:, 0:W])
        v.scalar_tensor_tensor(out=cnt[:, 0:W], in0=cnt[:, 0:W], scalar=0.5,
                               in1=vm[:, 0:W], op0=ALU.mult, op1=ALU.add)
        hsb = scl[:, 0:nblk].broadcast_to([128, nblk, 16])
        cntv = cnt[:, 0:W].rearrange("p (b s) -> p b s", s=16)
        v.tensor_tensor(out=cntv, in0=cntv, in1=hsb, op=ALU.mult)
        v.tensor_mul(r1[:, 0:W], cnt[:, 0:W], sg[:, 0:W])
        i2 = v.scalar_tensor_tensor(out=xout, in0=r1[:, 0:W], scalar=2.0,
                                    in1=cnt[:, 0:W], op0=ALU.mult, op1=ALU.subtract)
        P.sev.inc(i2, 1)
        roll["vB"] = P.sev.n
        return i2, 1

    steps += [(f"{tag}a", "vector", sA), (f"{tag}A", "vector", vA),
              (f"{tag}b", "scalar", sB), (f"{tag}B", "vector", vB)]


def ph_fp4_mlp2(P):
    hxq = P.d("xq_b", [D, L], BF16)
    hqg = P.d("qgT_b", [D, IFF], BF16)
    hqu = P.d("quT_b", [D, IFF], BF16)
    hout = P.d("hq_b", [IFF, L], BF16)
    MN = 1
    NW = (ITT + MN - 1) // MN
    QH = IFF // 2
    G1 = [(g, min(4, ITT - g)) for g in range(0, ITT, 4)]
    G2 = [(g, min(8, ITT - g)) for g in range(0, ITT, 8)]

    def wslice(h, m0, mn):
        return h[:, m0 * 128:(m0 + mn) * 128].rearrange(
            "(k p) (m q) -> p k m q", p=128, q=128)

    with contextlib.ExitStack() as es:
        mov = [P.sb(es, f"f2_x{i}", [128, DT * CH], BF16) for i in range(2)]
        wgw = [P.sb(es, f"f2_g{i}", [128, DT * MN * 128], BF16) for i in range(2)]
        wuw = [P.sb(es, f"f2_u{i}", [128, DT * MN * 128], BF16) for i in range(2)]
        gf = P.sb(es, "f2_gf", [128, 2 * CH], F32)
        hf = [P.sb(es, f"f2_hf{i}", [128, ITT * CH], F32) for i in range(2)]
        htm = P.sb(es, "f2_htm", [128, IFF], F32)
        qtm = P.sb(es, "f2_qtm", [128, IFF], BF16)
        outb = [P.sb(es, f"f2_ob{i}", [128, ITT * 128], BF16) for i in range(2)]
        tmps = mk_qtmp(P, es, "f2_t", QH)
        s_mv = [P.newsem(f"f2mv{i}") for i in range(2)]
        s_wv = [P.newsem(f"f2wv{i}") for i in range(2)]
        s_st = [P.newsem(f"f2st{i}") for i in range(2)]

        pb2 = [P.pbk[6].bitcast(BF16), P.pbt]
        rWu = {}; rSG = {}; rVU = {}; rT1 = {}; rT2 = {}
        roll = {"ve10": 0, "ve11": 0, "ve20": 0, "ve21": 0,
                "vA": 0, "sB": 0, "vB": 0}

        def wave_elems(c):
            elems = []

            def sy_x(sync, c=c):
                if c == 0:
                    P.w(sync)
                g2 = (c - 1) * NW - 1
                if g2 >= 0:
                    sync.wait_ge(P.smm.h, rWu[g2])
                i2 = sync.dma_start(out=sview(mov[c % 2]), in_=cview(hxq, c))
                return s_mv[c % 2].inc(i2, 16), 16
            elems.append((f"f2x{c}", [("sync", sy_x)]))

            for w in range(NW):
                gw = c * NW + w
                m0 = w * MN
                mn = min(MN, ITT - m0)

                def syw(sync, gw=gw, m0=m0, mn=mn):
                    if gw - 2 >= 0:
                        sync.wait_ge(P.smm.h, rWu[gw - 2])
                    sl = s_wv[gw % 2]
                    dg = wgw[gw % 2][:, 0:DT * mn * 128].rearrange(
                        "p (k m q) -> p k m q", m=mn, q=128)
                    du = wuw[gw % 2][:, 0:DT * mn * 128].rearrange(
                        "p (k m q) -> p k m q", m=mn, q=128)
                    sl.inc(sync.dma_start(out=dg, in_=wslice(hqg, m0, mn)), 16)
                    return sl.inc(sync.dma_start(out=du, in_=wslice(hqu, m0, mn)), 16), 16

                def tn_g(t, c=c, gw=gw):
                    if gw == 0:
                        P.w(t)
                    t.wait_ge(s_wv[gw % 2].h, s_wv[gw % 2].n)
                    t.wait_ge(s_mv[c % 2].h, s_mv[c % 2].n)
                    if gw - 2 in rSG:
                        t.wait_ge(P.ssc.h, rSG[gw - 2])
                    for k in range(DT):
                        i2 = t.matmul(P.pbk[gw % 2], wv(wgw[gw % 2], k, 0, 1),
                                      ct(mov[c % 2], k),
                                      start=(k == 0), stop=(k == DT - 1))
                    return P.smm.inc(i2, 1), 1

                def sc_g(s, gw=gw):
                    s.wait_ge(P.smm.h, P.smm.n)
                    if gw - 2 in rVU:
                        s.wait_ge(P.sev.h, rVU[gw - 2])
                    i2 = s.activation(out=ct(gf, gw % 2), in_=P.pbk[gw % 2], func=AF.Silu)
                    P.ssc.inc(i2, 1)
                    rSG[gw] = P.ssc.n
                    return i2, 1

                def tn_u(t, c=c, gw=gw):
                    if gw - 2 in rVU:
                        t.wait_ge(P.sev.h, rVU[gw - 2])
                    for k in range(DT):
                        i2 = t.matmul(P.pbk[2 + gw % 2], wv(wuw[gw % 2], k, 0, 1),
                                      ct(mov[c % 2], k),
                                      start=(k == 0), stop=(k == DT - 1))
                    P.smm.inc(i2, 1)
                    rWu[gw] = P.smm.n
                    return i2, 1

                def ve_u(v, c=c, gw=gw, m0=m0):
                    v.wait_ge(P.smm.h, P.smm.n)
                    v.wait_ge(P.ssc.h, rSG[gw])
                    if c > 1:
                        v.wait_ge(P.smm.h, rT1[(c - 2, 3, 4 * (m0 // 4))])
                    i2 = v.tensor_mul(ct(hf[c % 2], m0), ct(gf, gw % 2), P.pbk[2 + gw % 2])
                    P.sev.inc(i2, 1)
                    rVU[gw] = P.sev.n
                    return i2, 1

                elems.append((f"f2w{gw}", [("sync", syw), ("tensor", tn_g), ("scalar", sc_g),
                                           ("tensor", tn_u), ("vector", ve_u)]))
            return elems

        def quant_elems(c):
            elems = []
            for lt in range(4):
                for gi, (g0, gn) in enumerate(G1):
                    bk = 4 + gi % 2
                    def tn1(t, c=c, lt=lt, g0=g0, gn=gn, bk=bk, gi=gi):
                        lastw = min((g0 + gn - 1) // MN, NW - 1)
                        t.wait_ge(P.sev.h, rVU[c * NW + lastw])
                        if roll[f"ve1{gi % 2}"]:
                            t.wait_ge(P.sev.h, roll[f"ve1{gi % 2}"])
                        for j in range(gn):
                            k = g0 + j
                            i2 = t.transpose(P.pbk[bk][:, j * 128:(j + 1) * 128],
                                             hf[c % 2][:, k * CH + lt * 128: k * CH + lt * 128 + 128],
                                             P.ident_f)
                        P.smm.inc(i2, 1)
                        rT1[(c, lt, g0)] = P.smm.n
                        return i2, 1

                    def ve1(v, g0=g0, gn=gn, bk=bk, gi=gi):
                        v.wait_ge(P.smm.h, P.smm.n)
                        i2 = v.tensor_copy(out=htm[:, g0 * 128:(g0 + gn) * 128],
                                           in_=P.pbk[bk][:, 0:gn * 128])
                        P.sev.inc(i2, 1)
                        roll[f"ve1{gi % 2}"] = P.sev.n
                        return i2, 1
                    elems.append((f"q1_{c}_{lt}_{g0}",
                                  [("tensor", tn1), ("vector", ve1)]))

                def qg(c=c, lt=lt):
                    return rT2.get((c, lt - 1)) if lt > 0 else rT2.get((c - 1, 3))
                halves = []
                for h in range(2):
                    qsteps = []
                    emit_quant_steps3(P, qsteps, htm[:, h * QH:(h + 1) * QH],
                                      qtm[:, h * QH:(h + 1) * QH], tmps, QH // 16,
                                      roll, qg, f"q_{c}_{lt}_{h}")
                    halves.append(qsteps)
                h0, h1 = halves
                order = [h0[0], h0[1], h0[2], h1[0], h0[3], h1[1], h1[2], h1[3]]
                elems += [(nm, [(en, f)]) for nm, en, f in order]

                for gi, (g0, gn) in enumerate(G2):
                    def tn2(t, c=c, lt=lt, g0=g0, gn=gn, gi=gi):
                        t.wait_ge(P.sev.h, roll["vB"])
                        if roll[f"ve2{gi % 2}"]:
                            t.wait_ge(P.sev.h, roll[f"ve2{gi % 2}"])
                        pb = pb2[gi % 2]
                        for j in range(gn):
                            k = g0 + j
                            i2 = t.transpose(pb[:, j * 128:(j + 1) * 128],
                                             qtm[:, k * 128:(k + 1) * 128], P.ident_b)
                        P.smm.inc(i2, 1)
                        rT2[(c, lt)] = P.smm.n
                        return i2, 1

                    def ve2(v, c=c, lt=lt, g0=g0, gn=gn, gi=gi):
                        v.wait_ge(P.smm.h, P.smm.n)
                        sl = s_st[lt % 2]
                        v.wait_ge(sl.h, sl.n)
                        pb = pb2[gi % 2]
                        for j in range(gn):
                            k = g0 + j
                            i2 = v.tensor_copy(out=outb[lt % 2][:, k * 128:(k + 1) * 128],
                                               in_=pb[:, j * 128:(j + 1) * 128])
                        P.sev.inc(i2, 1)
                        roll[f"ve2{gi % 2}"] = P.sev.n
                        return i2, 1
                    elems.append((f"q2_{c}_{lt}_{g0}",
                                  [("tensor", tn2), ("vector", ve2)]))

                def st(sync, c=c, lt=lt):
                    sync.wait_ge(P.sev.h, max(roll["ve20"], roll["ve21"]))
                    dst = hout[:, c * CH + lt * 128: c * CH + (lt + 1) * 128]
                    i2 = sync.dma_start(out=dst.rearrange("(k p) n -> p k n", p=128),
                                        in_=outb[lt % 2].rearrange("p (k n) -> p k n", n=128))
                    return s_st[lt % 2].inc(i2, 16), 16
                elems.append((f"qst_{c}_{lt}", [("sync", st)]))
            return elems

        def emit_merged(A, B):
            na, nb = len(A), len(B)
            ia = ib = 0
            while ia < na or ib < nb:
                if ib >= nb or (ia < na and ia * nb <= ib * na):
                    nm, st = A[ia]; ia += 1
                else:
                    nm, st = B[ib]; ib += 1
                P.seq(nm, st)

        MERGE = True
        for c in range(NCH):
            if MERGE:
                emit_merged(wave_elems(c), quant_elems(c - 1) if c > 0 else [])
            else:
                for nm, st in wave_elems(c):
                    P.seq(nm, st)
                for nm, st in quant_elems(c):
                    P.seq(nm, st)
        if MERGE:
            for nm, st in quant_elems(NCH - 1):
                P.seq(nm, st)

        def sy_end(sync):
            sync.wait_ge(s_st[0].h, s_st[0].n)
            sync.wait_ge(s_st[1].h, s_st[1].n)
            return sync.nop(), 1
        P.seq("f2end", [("sync", sy_end)])
        fin(P, "fm")


def host_fir_diag(inputs):
    ws = []
    for K in (3, 7, 15, 31):
        w = np.asarray(inputs["L1_attn_fir_w" + str(K)], np.float32)
        ws.append(w.reshape(-1, w.shape[-1]))
    out = np.zeros((DT, 56, 128, 128), np.float32)
    idx = np.arange(128)
    for dt in range(DT):
        t0 = 0
        for w in ws:
            K = w.shape[-1]
            for k in range(K):
                out[dt, t0 + k, idx, idx] = w[dt * 128:(dt + 1) * 128, k]
            t0 += K
    return out


def ph_l1_fir2(P):
    PAD = 32
    hs = P.d("vs_f", [D, L])
    hfd = P.d("fird", [DT, 56, 128, 128], F32)
    KS = (3, 7, 15, 31)
    houts = {K: P.d(f"fir{K}_f", [D, L]) for K in KS}
    with contextlib.ExitStack() as es:
        xt = [P.sb(es, f"fr_x{i}", [128, PAD + L], BF16) for i in range(2)]
        wdg = [P.sb(es, f"fr_w{i}", [128, 56 * 128], BF16) for i in range(2)]
        outb = [P.sb(es, f"fr_o{i}", [128, 4 * CH], F32) for i in range(2)]
        s_xt = [P.newsem(f"frx{i}") for i in range(2)]
        s_wd = [P.newsem(f"frw{i}") for i in range(2)]
        s_st = [P.newsem(f"frs{i}") for i in range(2)]
        rMM = {}; rEV = {}; rST = {}

        def gp0(g):
            P.w(g)
            g.memset(xt[0][:, 0:PAD], 0.0)
            return P.sgp.inc(g.memset(xt[1][:, 0:PAD], 0.0), 1), 1
        P.seq("fr_pad", [("gpsimd", gp0)])

        for dt in range(DT):
            def gp_w(g, dt=dt):
                if dt == 0:
                    P.w(g)
                if dt - 2 in rMM:
                    g.wait_ge(P.smm.h, rMM[dt - 2])
                i2 = g.dma_start(out=wdg[dt % 2].rearrange("p (t q) -> p t q", q=128),
                                 in_=hfd[dt].rearrange("t p q -> p t q"))
                s_wd[dt % 2].inc(i2, 16)
                i2 = g.dma_start(out=xt[dt % 2][:, PAD:PAD + L],
                                 in_=hs[dt * 128:(dt + 1) * 128, :])
                return s_xt[dt % 2].inc(i2, 16), 16
            P.seq(f"fr_ld{dt}", [("gpsimd", gp_w)])

            for c in range(NCH):
                def tn(t, dt=dt, c=c):
                    if c == 0:
                        t.wait_ge(s_wd[dt % 2].h, s_wd[dt % 2].n)
                        t.wait_ge(s_xt[dt % 2].h, s_xt[dt % 2].n)
                    key = (dt, c - 2) if c >= 2 else (dt - 1, NCH - 2 + c)
                    if key in rEV:
                        t.wait_ge(P.sev.h, rEV[key])
                    t0 = 0
                    for fi, K in enumerate(KS):
                        bank = (c % 2) * 4 + fi
                        for k in range(K):
                            base = PAD - (K - 1) + k + c * CH
                            i2 = t.matmul(P.pbk[bank],
                                          wdg[dt % 2][:, (t0 + k) * 128:(t0 + k + 1) * 128],
                                          xt[dt % 2][:, base:base + CH],
                                          start=(k == 0), stop=(k == K - 1))
                        t0 += K
                    P.smm.inc(i2, 1)
                    rMM[dt] = P.smm.n
                    return i2, 1

                def ve(v, dt=dt, c=c):
                    v.wait_ge(P.smm.h, P.smm.n)
                    key = (dt, c - 2) if c >= 2 else (dt - 1, NCH - 2 + c)
                    if key in rST:
                        v.wait_ge(s_st[c % 2].h, rST[key])
                    for fi in range(4):
                        i2 = v.tensor_copy(out=ct(outb[c % 2], fi),
                                           in_=P.pbk[(c % 2) * 4 + fi])
                    P.sev.inc(i2, 1)
                    rEV[(dt, c)] = P.sev.n
                    return i2, 1

                def sy2(sync, dt=dt, c=c):
                    sync.wait_ge(P.sev.h, rEV[(dt, c)])
                    for fi, K in enumerate(KS):
                        i2 = sync.dma_start(
                            out=houts[K][dt * 128:(dt + 1) * 128, c * CH:(c + 1) * CH],
                            in_=ct(outb[c % 2], fi))
                        s_st[c % 2].inc(i2, 16)
                    rST[(dt, c)] = s_st[c % 2].n
                    return i2, 16
                P.seq(f"fr_{dt}_{c}", [("tensor", tn), ("vector", ve), ("sync", sy2)])

        def sy_end(sync):
            sync.wait_ge(s_st[0].h, s_st[0].n)
            sync.wait_ge(s_st[1].h, s_st[1].n)
            return sync.nop(), 1
        P.seq("fr_end", [("sync", sy_end)])
        fin(P, "fir")
